# version 03
# baseline (speedup 1.0000x reference)
import os
"""Fused single-launch Trainium2 megakernel for mk_seq_postln_d1 (mamba mixer +
dense-MoE FFN, depth 1, d_model 1024, batch 4, seq 4096).

Sharding: core k -> batch k//2, mamba-channel half k%2, per-SC token half k%2
(each SC of 256 tokens: core m owns columns [m*128, m*128+128)).
Cross-core (pair) exchange via addr_space="Shared" DRAM + flag polling.
Layout: feature-major activations ([feature-chunk on partitions, tokens free]).
Scan: segmented tensor_tensor_scan (16 state lanes per channel row, reset cols).
"""
from contextlib import ExitStack

import numpy as np
import ml_dtypes

import concourse.bass as bass
import concourse.mybir as mybir
from concourse.bass_utils import run_bass_kernel_spmd

F32 = mybir.dt.float32
BF16 = mybir.dt.bfloat16
U32 = mybir.dt.uint32
AF = mybir.ActivationFunctionType
ALU = mybir.AluOpType
AX = mybir.AxisListType

B, L, D, E, N, DR = 4, 4096, 1024, 2048, 16, 64
EH = E // 2
NEXP = 15
NE = 16
SC = 256
NSC = L // SC
HSC = SC // 2
TOK = NSC * HSC
FS = 512
HF = FS // 2
NFS = TOK // FS
SEG = SC + 1
P = 128
MAGIC = 0x5A17C0DE
NCORES = 8

CONSTS = [1e-5, 1.0 / 1024.0, -0.5, -1.0, 1.0, 0.0, 2.0, 0.5]


def prep_core_inputs(core, I):
    b = core // 2
    m = core % 2
    sl = slice(m * EH, (m + 1) * EH)
    d = {}
    d["resid_fm"] = np.ascontiguousarray(np.asarray(I["L0_attn_resid"][b]).T).astype(np.float32)
    for nm, w in (("w_in", I["L0_attn_in_w"]), ("w_skip", I["L0_attn_skip_w"])):
        wt = np.asarray(w)[sl].T.astype(np.float32)
        d[nm] = np.ascontiguousarray(wt.reshape(8, P, 8, P).transpose(2, 0, 1, 3))
    ot = np.asarray(I["L0_attn_out_w"]).T.astype(np.float32)
    oc = np.concatenate([ot[sl], ot[(1 - m) * EH:(2 - m) * EH]], axis=0)
    d["w_out"] = np.ascontiguousarray(oc.reshape(16, P, 8, P).transpose(2, 0, 1, 3))
    wibc = np.zeros((8, P, P), np.float32)
    w = np.asarray(I["L0_attn_wd1"])[:, sl].T.astype(np.float32).reshape(8, P, DR)
    wibc[:, :, 0:64] = w
    wb = np.asarray(I["L0_attn_wB"])[:, sl].T.astype(np.float32).reshape(8, P, N)
    wibc[:, :, 64:80] = wb
    wc = np.asarray(I["L0_attn_wC"])[:, sl].T.astype(np.float32).reshape(8, P, N)
    wibc[:, :, 96:112] = wc
    d["w_ibc"] = np.ascontiguousarray(wibc)
    d["w_wd2"] = np.ascontiguousarray(np.asarray(I["L0_attn_wd2"])[sl].T.astype(np.float32))
    rw = np.zeros((8, P, P), np.float32)
    rw[:, :, 0:NEXP] = np.asarray(I["L0_ffn_router_W"]).astype(np.float32).reshape(8, P, NEXP)
    d["w_rt"] = np.ascontiguousarray(rw)
    e1 = np.concatenate([np.asarray(I["L0_ffn_eW1"]), np.asarray(I["L0_ffn_sW1"])[None]], 0).astype(np.float32)
    e2 = np.concatenate([np.asarray(I["L0_ffn_eW2"]), np.asarray(I["L0_ffn_sW2"])[None]], 0).astype(np.float32)
    d["w_f1"] = np.ascontiguousarray(e1.reshape(NE, 8, P, 8, P).transpose(0, 3, 1, 2, 4))
    d["w_f2"] = np.ascontiguousarray(e2.reshape(NE, 8, P, 8, P).transpose(0, 3, 1, 2, 4))
    d["b2cat"] = np.ascontiguousarray(
        np.concatenate([np.asarray(I["L0_ffn_eb2"]), np.asarray(I["L0_ffn_sb2"])[None]], 0).astype(np.float32))
    cols = []

    def pp(v):
        return np.asarray(v).astype(np.float32)[sl].reshape(8, P).T

    cols.append(pp(I["L0_attn_conv_b"]))
    cols.append(pp(I["L0_attn_W_D"]))
    cols.append(pp(I["L0_attn_wd2_b"]))
    cw = np.asarray(I["L0_attn_conv_w"]).astype(np.float32)[sl]
    cols.append(cw.reshape(8, P, 4).transpose(1, 0, 2).reshape(P, 32))
    cols.append(np.asarray(I["L0_attn_norm_w"]).astype(np.float32).reshape(8, P).T)
    al = np.asarray(I["L0_attn_A_log"]).astype(np.float32)[sl]
    cols.append(al.reshape(8, P, N).transpose(1, 0, 2).reshape(P, 128))
    rb = np.zeros((P, 1), np.float32)
    rb[:NEXP, 0] = np.asarray(I["L0_ffn_router_b"]).astype(np.float32)
    cols.append(rb)
    eb1 = np.concatenate([np.asarray(I["L0_ffn_eb1"]), np.asarray(I["L0_ffn_sb1"])[None]], 0).astype(np.float32)
    cols.append(eb1.reshape(NE, 8, P).transpose(2, 1, 0).reshape(P, 128))
    cols.append(np.ones((P, 1), np.float32))
    d["vecs"] = np.ascontiguousarray(np.concatenate(cols, axis=1))
    d["consts"] = np.broadcast_to(np.array(CONSTS, np.float32), (P, len(CONSTS))).copy()
    d["id128b"] = np.eye(P, dtype=np.float32)
    d["id16"] = np.eye(16, dtype=np.float32)
    return d


def build_nc():
    nc = bass.Bass("TRN2", target_bir_lowering=False)

    di = lambda n, sh, dt=F32: nc.dram_tensor(n, sh, dt, kind="ExternalInput")
    t_res = di("resid_fm", [D, L])
    t_win = di("w_in", [8, 8, P, P])
    t_wsk = di("w_skip", [8, 8, P, P])
    t_wout = di("w_out", [8, 16, P, P])
    t_wibc = di("w_ibc", [8, P, P])
    t_wd2 = di("w_wd2", [DR, EH])
    t_wrt = di("w_rt", [8, P, P])
    t_wf1 = di("w_f1", [NE, 8, 8, P, P])
    t_wf2 = di("w_f2", [NE, 8, 8, P, P])
    t_b2 = di("b2cat", [16, D])
    t_vecs = di("vecs", [P, 322])
    t_consts = di("consts", [P, 8])
    t_idb = di("id128b", [P, P])
    t_id16 = di("id16", [16, 16])

    t_out = nc.dram_tensor("o_x2", [D, TOK], F32, kind="ExternalOutput")
    KDBG = bool(os.environ.get("KDEBUG"))
    if KDBG:
        dbg = {}
        for nm, sh in (("dbg_xn", [P, 8 * SC]), ("dbg_x", [P, 8 * SC]), ("dbg_dl", [P, 8 * SC]),
                       ("dbg_ibc", [112, SC]), ("dbg_a", [P, N * SEG]), ("dbg_u", [P, N * SEG]),
                       ("dbg_h", [P, N * SEG]), ("dbg_y2", [P, 8 * SC]), ("dbg_x1", [P, 8 * SC]),
                       ("dbg_cw", [NEXP, 2 * P]), ("dbg_sg", [P, 8 * SC]), ("dbg_zd", [P, 8 * SC]),
                       ("dbg_y2p", [P, 8 * SC]), ("dbg_x1pre", [P, 8 * SC]), ("dbg_wo", [P, 16 * P]),
                       ("dbg_hs", [P, 8 * HF])):
            dbg[nm] = nc.dram_tensor(nm, sh, F32, kind="ExternalOutput")
        for fi in range(4):
            for nm, sh in ((f"dbg_cwo{fi}", [16, FS]), (f"dbg_fx1{fi}", [P, 4 * FS]),
                           (f"dbg_racc{fi}", [P, 4 * HF]), (f"dbg_pdn{fi}", [P, 6 * 2 * HF])):
                dbg[nm] = nc.dram_tensor(nm, sh, F32, kind="ExternalOutput")
            for h in range(2):
                for nm, sh in ((f"dbg_x2a{fi}{h}", [P, 8 * HF]), (f"dbg_x2n{fi}{h}", [P, 8 * HF])):
                    dbg[nm] = nc.dram_tensor(nm, sh, F32, kind="ExternalOutput")
        dbg["dbg_x1fd"] = nc.dram_tensor("dbg_x1fd", [P, 8 * TOK], F32, kind="ExternalOutput")

    ds = lambda n, sh, dt: nc.dram_tensor(n, sh, dt)
    w_in_b = ds("w_in_b", [8, 8, P, P], BF16)
    w_sk_b = ds("w_sk_b", [8, 8, P, P], BF16)
    w_out_b = ds("w_out_b", [8, 16, P, P], BF16)
    w_f1_b = ds("w_f1_b", [NE, 8, 8, P, P], BF16)
    w_f2_b = ds("w_f2_b", [NE, 8, 8, P, P], BF16)
    x1b_d = ds("x1b_d", [8, P, TOK], BF16)
    rows_d = ds("rows_d", [4, 4096], F32)
    bc_d = ds("bc_d", [2, 16, L], BF16)
    cw_d = ds("cw_d", [NEXP, TOK], BF16)
    ln2_d = ds("ln2_d", [2, TOK], F32)

    sh_flags = nc.dram_tensor("sh_flags", [2, 64], U32, addr_space="Shared")
    sh_ibc = nc.dram_tensor("sh_ibc", [2, NSC, 112, SC], F32, addr_space="Shared")
    sh_y2 = nc.dram_tensor("sh_y2", [2, NSC, P, 8 * SC], BF16, addr_space="Shared")
    FL_HELLO, FL_IBC, FL_Y2 = 0, 1, 20

    ctx = ExitStack()
    sb = lambda n, sh, dt=F32: ctx.enter_context(nc.sbuf_tensor(n, sh, dt))

    s_vecs = sb("s_vecs", [P, 322])
    s_consts = sb("s_consts", [P, 8])
    for v, i in zip(CONSTS, range(len(CONSTS))):
        nc.const_aps.register(F32, v, s_consts[:, i:i + 1])
    s_idb = sb("s_idb", [P, P], BF16)
    s_id16 = sb("s_id16", [16, 16])
    s_A = sb("s_A", [P, 8, N])
    s_wibc = sb("s_wibc", [P, 8, P], BF16)
    s_wd2 = sb("s_wd2", [DR, EH], BF16)
    s_wrt = sb("s_wrt", [P, 8, P], BF16)
    s_b2 = sb("s_b2", [16, D], BF16)

    s_resid = sb("s_resid", [P, 1, 8, SC])
    s_sq = sb("s_sq", [P, 2, SC], BF16)
    s_onesb = sb("s_onesb", [P, 1], BF16)
    s_o128 = sb("s_o128", [P, P], BF16)
    s_lnt = sb("s_lnt", [P, SC])
    s_lnt2 = sb("s_lnt2", [P, SC])
    s_rows = sb("s_rows", [1, 64])
    s_rep = sb("s_rep", [P, 2, SC])
    s_xn = sb("s_xn", [P, 8, SC], BF16)
    s_xpre = sb("s_xpre", [P, 8, SC + 3], BF16)
    s_xtail = sb("s_xtail", [P, 8, 3], BF16)
    s_cacc = sb("s_cacc", [P, 2, SC])
    s_x = sb("s_x", [P, 1, 8, SC], BF16)
    s_sg = sb("s_sg", [P, 1, 8, SC], BF16)
    s_zd = sb("s_zd", [P, 1, 8, SC], BF16)
    _ci_bf = None
    s_dl = sb("s_dl", [P, 8, SC], BF16)
    s_w = sb("s_w", [P, 8, SC], BF16)
    s_ibc = sb("s_ibc", [112, SC])
    s_ibc2 = sb("s_ibc2", [112, SC])
    s_innb = sb("s_innb", [64, SC], BF16)
    s_bcbB = sb("s_bcbB", [16, SC], BF16)
    s_bcbC = sb("s_bcbC", [16, SC], BF16)
    s_Brep = sb("s_Brep", [P, 2, N, HSC], BF16)
    s_Crep = sb("s_Crep", [P, 2, N, HSC], BF16)
    s_a = sb("s_a", [P, 2, N, SEG])
    s_uh = sb("s_uh", [P, 3, N, SEG], BF16)
    s_hc = sb("s_hc", [P, 8, N])
    s_y = sb("s_y", [P, SC])
    s_g1 = sb("s_g1", [P, SC])
    s_y2 = sb("s_y2", [P, 1, 8, SC], BF16)
    s_y2p = sb("s_y2p", [P, 8, SC], BF16)
    s_x1p = sb("s_x1p", [P, 8, SC])
    s_x1b = sb("s_x1b", [P, 8, SC], BF16)
    s_lg = sb("s_lg", [NEXP, SC])
    s_lgt = sb("s_lgt", [P, 2, NEXP])
    s_sm = sb("s_sm", [P, 2, NEXP])
    s_wk = sb("s_wk", [P, 2, NEXP])
    s_mx = sb("s_mx", [P, 2, 4])
    s_msk = sb("s_msk", [P, 2, NEXP], U32)
    s_negs = sb("s_negs", [P, NEXP])
    s_cwt = sb("s_cwt", [P, 2, NEXP], BF16)
    s_cwf = sb("s_cwf", [NEXP, 2, P], BF16)
    s_rin = sb("s_rin", [P, 1, 8, P], BF16)
    s_rsk = sb("s_rsk", [P, 1, 8, P], BF16)
    s_rot = sb("s_rot", [P, 1, 16, P], BF16)
    s_rw1 = sb("s_rw1", [P, 1, 8, P], BF16)
    s_rw2 = sb("s_rw2", [P, 1, 8, P], BF16)
    s_x1s = sb("s_x1s", [P, 8, FS], BF16)
    s_hs = sb("s_hs", [P, 2, 8, HF], BF16)
    s_racc = sb("s_racc", [P, 2, 2, HF])
    s_cwr = sb("s_cwr", [P, FS], BF16)
    s_cwo = sb("s_cwo", [16, FS], BF16)
    s_x2p = sb("s_x2p", [P, 8, HF])

    s_r2 = sb("s_r2", [P, 2, HF])
    s_ci = sb("s_ci", [P, 2, 256])
    s_co = sb("s_co", [P, 2, 256], BF16)
    _cibf = s_ci.ap().rearrange("p a b -> p (a b)").bitcast(BF16)

    s_flag = sb("s_flag", [1, 4], U32)
    s_sp1t = sb("s_sp1t", [P, 8, SC], BF16)
    s_one = sb("s_one", [1, 68], U32)

    p_dn = ctx.enter_context(nc.psum_tensor("p_dn", [P, 6, 2, HF], F32))
    p_rot = ctx.enter_context(nc.psum_tensor("p_rot", [P, 2, FS], F32))

    class Sem:
        def __init__(self, name):
            self.h = ctx.enter_context(nc.semaphore(name))
            self.c = 0

    sm_w = Sem("sm_w")
    sm_ld = Sem("sm_ld")
    sm_ex = Sem("sm_ex")
    sm_st = Sem("sm_st")
    sm_po = Sem("sm_po")
    sm_pe = Sem("sm_pe")
    sm_act = Sem("sm_act")
    sm_dve = Sem("sm_dve")
    sm_pool = Sem("sm_pool")
    sm_bar = Sem("sm_bar")

    cmds = {e: [] for e in ("sync", "tensor", "vector", "scalar", "gpsimd")}

    def emit(eng, fn):
        cmds[eng].append(fn)

    def wt(eng, sem, val):
        emit(eng, lambda e, _s=sem.h, _v=val: e.wait_ge(_s, _v))

    def op(eng, make, sem=None, n=1):
        if sem is not None:
            sem.c += n

        def f(e, _mk=make, _sem=sem, _n=n):
            ins = _mk(e)
            if _sem is not None:
                ins.then_inc(_sem.h, _n)
        emit(eng, f)
        return sem.c if sem is not None else None

    def dma(eng, out_ap, in_ap, sem):
        return op(eng, lambda e, _o=out_ap, _i=in_ap: e.dma_start(_o, _i), sem, 16)

    ENGS = ("tensor", "vector", "scalar", "gpsimd", "sync")

    def barrier():
        tgt = sm_bar.c + len(ENGS)
        st_now = sm_st.c
        for e in ENGS:
            def f(eng, _t=tgt, _st=st_now):
                eng.wait_ge(sm_st.h, _st)
                eng.drain().then_inc(sm_bar.h, 1)
                eng.wait_ge(sm_bar.h, _t)
            emit(e, f)
        sm_bar.c = tgt

    R = {}

    def sync_setup(eng):
        for nm in ("me", "pn", "pc", "fl", "fl2", "ne"):
            R[nm] = eng.alloc_register("r_" + nm)
        pid = eng.partition_id()
        eng.reg_mov(R["me"], pid)
        eng.reg_mod(R["me"], R["me"], 2)
        eng.reg_mov(R["pn"], 1)
        eng.reg_sub(R["pn"], R["pn"], R["me"])
        eng.reg_mov(R["pc"], 0)
    emit("sync", sync_setup)

    def me_sw(e, fn0, fn1):
        with e.If_eq(R["me"], 0):
            fn0(e)
        with e.Else():
            fn1(e)

    def flag_write(eng, slot):
        me_sw(eng,
              lambda e: e.dma_start(sh_flags.ap()[0:1, slot:slot + 1], s_one[0:1, 0:1]).then_inc(sm_ex.h, 16),
              lambda e: e.dma_start(sh_flags.ap()[1:2, slot:slot + 1], s_one[0:1, 0:1]).then_inc(sm_ex.h, 16))

    def flag_poll(eng, slot):
        eng.reg_mov(R["ne"], 1)
        with eng.While(R["ne"]):
            eng.reg_add(R["pc"], R["pc"], 16)
            eng.dma_start(s_flag[0:1, 0:2],
                          bass.AP(sh_flags, slot, [[1, 1], [64, 2], [1, 1]]),
                          allow_slow_non_contiguous=True).then_inc(sm_po.h, 16)
            eng.wait_ge(sm_po.h, R["pc"])
            eng.reg_load(R["fl"], s_flag[0:1, 0:1])
            eng.reg_load(R["fl2"], s_flag[0:1, 1:2])
            eng.reg_mul(R["fl"], R["fl"], R["me"])
            eng.reg_mul(R["fl2"], R["fl2"], R["pn"])
            eng.reg_add(R["fl"], R["fl"], R["fl2"])
            eng.reg_sub(R["ne"], R["fl"], MAGIC)

    def ddump(name, src_ap, sem, mark):
        if not KDBG:
            return
        wt("sync", sem, mark)
        nr = src_ap.partition_size()
        nf = src_ap.free_size()
        t = dbg[name]
        if src_ap.dtype == BF16:
            dst = t.ap().bitcast(BF16)[0:nr, 0:nf]
        else:
            dst = t.ap()[0:nr, 0:nf]
        dma("sync", dst, src_ap, sm_st)

    rot_i = [0]
    rot_guard = [None, None]

    def rot_next():
        b = rot_i[0] % 2
        rot_i[0] += 1
        if rot_guard[b] is not None:
            gsem, gval = rot_guard[b]
            wt("tensor", gsem, gval)
        return b

    def rot_release(b, sem, val):
        rot_guard[b] = (sem, val)

    def prologue():
        dma("sync", s_vecs[:], t_vecs[:, :], sm_ld)
        dma("sync", s_consts[:], t_consts[:, :], sm_ld)
        dma("sync", s_id16[:], t_id16[:, :], sm_ld)
        small_jobs = [(t_idb[:, :], s_idb[:, :], P)]
        for a in range(8):
            small_jobs.append((t_wibc.ap()[a], s_wibc[:, a, :], P))
            small_jobs.append((t_wrt.ap()[a], s_wrt[:, a, :], P))
        for c0 in range(0, EH, 256):
            small_jobs.append((t_wd2[:, c0:c0 + 256], s_wd2[:, c0:c0 + 256], 64))
            small_jobs.append((t_b2[:, c0:c0 + 256], s_b2[:, c0:c0 + 256], 16))
        for (src, dst, npart) in small_jobs:
            cw = src.shape[-1]
            m = dma("sync", s_ci[0:npart, 0, 0:cw], src, sm_ld)
            wt("scalar", sm_ld, m)
            m = op("scalar", lambda e, _d=dst, _cw=cw, _np=npart:
                   e.activation(_d, s_ci[0:_np, 0, 0:_cw], AF.Copy), sm_act)
            wt("sync", sm_act, m)
        m = op("scalar", lambda e: e.activation(
            s_A.ap().rearrange("p a b -> p (a b)"), s_vecs[:, 64:192], AF.Exp), sm_act)
        wt("vector", sm_act, m)

        def fv(e):
            e.tensor_scalar_mul(s_A.ap().rearrange("p a b -> p (a b)"),
                                s_A.ap().rearrange("p a b -> p (a b)"), -1.0)
            e.memset(s_onesb[:], 1.0)
            e.memset(s_o128[:, :], 1.0)
            e.memset(s_one[0:1, 0:1], MAGIC)
            e.memset(s_one[0:1, 1:68], 0)
            e.memset(s_negs[:], -1.0)
            e.memset(s_xtail.ap().rearrange("p a b -> p (a b)"), 0.0)
            e.memset(s_hc.ap().rearrange("p a b -> p (a b)"), 0.0)
            return e.memset(s_rows[:], 0.0)
        m = op("vector", fv, sm_dve)
        wt("sync", sm_dve, m)

        ex0 = sm_ex.c

        if not os.environ.get("NOHELLO"):
            def f(e, _b=ex0):
                me_sw(e,
                      lambda ee: ee.dma_start(sh_flags.ap()[0:1, 0:64], s_one[0:1, 4:68]).then_inc(sm_ex.h, 16),
                      lambda ee: ee.dma_start(sh_flags.ap()[1:2, 0:64], s_one[0:1, 4:68]).then_inc(sm_ex.h, 16))
                e.wait_ge(sm_ex.h, _b + 16)
                flag_write(e, FL_HELLO)
                e.wait_ge(sm_ex.h, _b + 32)
                flag_poll(e, FL_HELLO)
            emit("sync", f)
            sm_ex.c += 32

        CH = 256
        store_marks = [0, 0]
        _jobs = [
            (t_win, w_in_b, 8 * 8 * P * P),
            (t_wsk, w_sk_b, 8 * 8 * P * P),
            (t_wout, w_out_b, 8 * 16 * P * P),
            (t_wf1, w_f1_b, NE * 8 * 8 * P * P),
            (t_wf2, w_f2_b, NE * 8 * 8 * P * P),
        ]
        if os.environ.get("NOCAST"):
            _jobs = _jobs[:2]
        for (src, dst, nel) in _jobs:
            nch = nel // (P * CH)
            for c in range(nch):
                par = c % 2
                if store_marks[par]:
                    wt("sync", sm_w, store_marks[par])
                m = dma("sync", s_ci[:, par, :],
                        bass.AP(src, c * P * CH, [[CH, P], [1, CH]]), sm_w)
                wt("scalar", sm_w, m)
                cm = op("scalar", lambda e, _p=par: e.activation(s_co[:, _p, :], s_ci[:, _p, :], AF.Copy), sm_act)
                wt("sync", sm_act, cm)
                store_marks[par] = dma("sync", bass.AP(dst, c * P * CH, [[CH, P], [1, CH]]),
                                       s_co[:, par, :], sm_w)
        wt("sync", sm_w, sm_w.c)
        barrier()

    def phaseA(s, acut=99):
        slot3 = 0
        par = 0
        rm = dma("sync", s_resid[:, slot3, :, :],
                 t_res.ap().rearrange("(dc p) t -> p dc t", p=P)[:, :, s * SC:(s + 1) * SC], sm_ld)
        wt("vector", sm_ld, rm)

        bk = rot_next()
        st_marks = {}
        for dc in range(8):
            if dc >= 2:
                wt("vector", sm_pe, st_marks[dc - 2])
            mq = op("vector", lambda e, _sl=slot3, _dc=dc:
                    e.tensor_tensor(s_sq[:, _dc % 2, :], s_resid[:, _sl, _dc, :],
                                    s_resid[:, _sl, _dc, :], ALU.mult), sm_dve)
            wt("tensor", sm_dve, mq)
            st_marks[dc] = op("tensor", lambda e, _dc=dc, _bk=bk:
               e.matmul(p_rot[:, _bk, 0:SC], s_o128[:, :], s_sq[:, _dc % 2, :],
                        start=(_dc == 0), stop=(_dc == 7)), sm_pe)
        wt("scalar", sm_pe, sm_pe.c)

        def fa(e, _bk=bk):
            return e.activation(s_lnt[:, :], p_rot[:, _bk, 0:SC], AF.Ln,
                                scale=1.0 / 1024.0, bias=1e-5)
        m = op("scalar", fa, sm_act)
        rot_release(bk, sm_act, m)
        wt("scalar", sm_act, m)
        m = op("scalar", lambda e: e.activation(s_rep[:, 0, :], s_lnt[:, :], AF.Exp, scale=-0.5), sm_act)
        if acut <= 1:
            return
        wt("vector", sm_act, m)
        for dc in range(8):
            op("vector", lambda e, _sl=slot3, _dc=dc:
               e.scalar_tensor_tensor(s_xn[:, _dc, :], s_resid[:, _sl, _dc, :],
                                      s_vecs[:, 56 + _dc:57 + _dc], s_rep[:, 0, :],
                                      ALU.mult, ALU.mult), sm_dve)
        xn_done = sm_dve.c
        if acut <= 2:
            return

        evac_marks = {}
        for et in range(8):
            if et >= 1:
                wt("sync", sm_act, evac_marks[et - 1])
            dma("sync", s_rin[:, 0, :, :],
                w_in_b.ap()[et].rearrange("dc p m -> p dc m"), sm_ld)
            m = dma("sync", s_rsk[:, 0, :, :],
                    w_sk_b.ap()[et].rearrange("dc p m -> p dc m"), sm_ld)
            bki = rot_next()
            bks = rot_next()
            wt("tensor", sm_ld, m)
            if et == 0:
                wt("tensor", sm_dve, xn_done)

            _plvl = int(os.environ.get("PLVL", "9"))

            def ft(e, _et=et, _bi=bki, _bs=bks):
                last = None
                for dc in range(8):
                    last = e.matmul(p_rot[:, _bi, 0:SC], s_rin[:, 0, dc, :], s_xn[:, dc, :],
                                    start=(dc == 0), stop=(dc == 7))
                return last
            m1 = op("tensor", ft, sm_pe)

            def ft2(e, _et=et, _bs=bks):
                last = None
                for dc in range(8):
                    last = e.matmul(p_rot[:, _bs, 0:SC], s_rsk[:, 0, dc, :], s_xn[:, dc, :],
                                    start=(dc == 0), stop=(dc == 7))
                return last
            m2 = op("tensor", ft2, sm_pe)
            wt("scalar", sm_pe, m2)
            mi = op("scalar", lambda e, _et=et, _bi=bki:
                    e.activation(s_xpre[:, _et, 3:3 + SC], p_rot[:, _bi, 0:SC], AF.Copy), sm_act)
            rot_release(bki, sm_act, mi)
            em = op("scalar", lambda e, _et=et, _bs=bks, _par=par:
                    e.activation(s_sg[:, _par, _et, :], p_rot[:, _bs, 0:SC], AF.Silu), sm_act)
            rot_release(bks, sm_act, em)
            evac_marks[et] = em
        if acut <= 3:
            return
        wt("vector", sm_act, sm_act.c)
        m = op("vector", lambda e: e.tensor_copy(s_xpre[:, :, 0:3], s_xtail[:]), sm_dve)
        wt("vector", sm_dve, m)

        silu_marks = {}
        for et in range(8):
            cp = et % 2
            if et >= 1:
                wt("vector", sm_act, silu_marks[et - 1])
            base = sm_dve.c
            op("vector", lambda e, _et=et, _cp=cp:
               e.tensor_scalar(s_cacc[:, _cp, :], s_xpre[:, _et, 0:SC],
                               s_vecs[:, 24 + _et * 4:25 + _et * 4],
                               s_vecs[:, 0 + _et:1 + _et], ALU.mult, ALU.add), sm_dve)
            for k in (1, 2, 3):
                wt("vector", sm_dve, base + k)
                dstp = cp if k % 2 == 0 else 1 - cp
                srcp = 1 - dstp
                op("vector", lambda e, _et=et, _k=k, _d=dstp, _s=srcp:
                   e.scalar_tensor_tensor(s_cacc[:, _d, :], s_xpre[:, _et, _k:_k + SC],
                                          s_vecs[:, 24 + _et * 4 + _k:25 + _et * 4 + _k],
                                          s_cacc[:, _s, :], ALU.mult, ALU.add), sm_dve)
            cv = sm_dve.c
            wt("scalar", sm_dve, cv)
            silu_marks[et] = op("scalar", lambda e, _et=et, _par=par:
               e.activation(s_x[:, _par, _et, :], s_cacc[:, 1 - (_et % 2), :], AF.Silu), sm_act)
        x_done = sm_act.c
        wt("vector", sm_act, x_done)
        op("vector", lambda e: e.tensor_copy(s_xtail[:], s_xpre[:, :, SC:SC + 3]), sm_dve)

        if acut <= 4:
            return
        wt("tensor", sm_act, x_done)
        bk2 = rot_next()

        def ft(e, _bk=bk2, _par=par):
            last = None
            for et in range(8):
                last = e.matmul(p_rot[:, _bk, 0:SC], s_wibc[:, et, :], s_x[:, _par, et, :],
                                start=(et == 0), stop=(et == 7))
            return last
        m = op("tensor", ft, sm_pe)
        wt("scalar", sm_pe, m)
        for (mm_, off) in ((DR, 0), (N, 64), (N, 96)):
            mev = op("scalar", lambda e, _mm=mm_, _off=off, _bk=bk2:
               e.activation(s_ibc[_off:_off + _mm, :], p_rot[_off:_off + _mm, _bk, 0:SC], AF.Copy), sm_act)
        rot_release(bk2, sm_act, mev)
        ibc_evac = sm_act.c

        if acut <= 5:
            return
        wt("sync", sm_act, ibc_evac)
        ex0 = sm_ex.c

        def f(e, _s=s, _b=ex0):
            me_sw(e,
                  lambda ee: ee.dma_start(sh_ibc.ap()[0, _s], s_ibc[:, :]).then_inc(sm_ex.h, 16),
                  lambda ee: ee.dma_start(sh_ibc.ap()[1, _s], s_ibc[:, :]).then_inc(sm_ex.h, 16))
            e.wait_ge(sm_ex.h, _b + 16)
            flag_write(e, FL_IBC + _s)
            flag_poll(e, FL_IBC + _s)
            me_sw(e,
                  lambda ee: ee.dma_start(s_ibc2[:, :], sh_ibc.ap()[1, _s]).then_inc(sm_ex.h, 16),
                  lambda ee: ee.dma_start(s_ibc2[:, :], sh_ibc.ap()[0, _s]).then_inc(sm_ex.h, 16))
        emit("sync", f)
        sm_ex.c += 48
        wt("vector", sm_ex, sm_ex.c)
        m = op("vector", lambda e: e.tensor_add(s_ibc[:, :], s_ibc[:, :], s_ibc2[:, :]), sm_dve)
        wt("scalar", sm_dve, m)
        op("scalar", lambda e: e.activation(s_bcbB[:, :], s_ibc[64:80, :], AF.Copy), sm_act)
        op("scalar", lambda e: e.activation(s_bcbC[:, :], s_ibc[96:112, :], AF.Copy), sm_act)
        m = op("scalar", lambda e: e.activation(s_innb[:, :], s_ibc[0:64, :], AF.Copy), sm_act)
        bc_cast = m

        if acut <= 6:
            return
        wt("sync", sm_act, bc_cast)
        dma("sync", bc_d.ap()[0, :, s * SC:(s + 1) * SC], s_bcbB[:, :], sm_ld)
        m = dma("sync", bc_d.ap()[1, :, s * SC:(s + 1) * SC], s_bcbC[:, :], sm_ld)
        wt("sync", sm_ld, m)
        for h in range(2):
            dma("sync", s_Brep[:, h, :, :],
                bc_d.ap()[0, :, s * SC + h * HSC:s * SC + (h + 1) * HSC]
                .unsqueeze(0).broadcast_to([P, N, HSC]), sm_ld)
            dma("sync", s_Crep[:, h, :, :],
                bc_d.ap()[1, :, s * SC + h * HSC:s * SC + (h + 1) * HSC]
                .unsqueeze(0).broadcast_to([P, N, HSC]), sm_ld)
        bc_ready = sm_ld.c

        if acut <= 7:
            return
        wt("tensor", sm_act, bc_cast)
        for et in range(8):
            bb = rot_next()
            m = op("tensor", lambda e, _et=et, _bb=bb:
                   e.matmul(p_rot[:, _bb, 0:SC], s_wd2[:, _et * P:(_et + 1) * P],
                            s_innb[:, :], start=True, stop=True), sm_pe)
            wt("scalar", sm_pe, m)
            em = op("scalar", lambda e, _et=et, _par=par, _bb=bb:
                    e.activation(s_zd[:, _par, _et, :], p_rot[:, _bb, 0:SC], AF.Copy), sm_act)
            rot_release(bb, sm_act, em)
        zd_done = sm_act.c

        if acut <= 8:
            return
        wt("scalar", sm_act, zd_done)
        for et in range(8):
            op("scalar", lambda e, _et=et, _par=par:
               e.activation(s_sp1t[:, _et, :], s_zd[:, _par, _et, :], AF.Abs,
                            bias=s_vecs[:, 16 + _et:17 + _et], scale=1.0), sm_act)
        m = sm_act.c
        wt("scalar", sm_act, m)
        m = op("scalar", lambda e: e.activation(
            s_sp1t.ap().rearrange("p a b -> p (a b)"),
            s_sp1t.ap().rearrange("p a b -> p (a b)"), AF.Exp, scale=-1.0), sm_act)
        wt("scalar", sm_act, m)
        m = op("scalar", lambda e: e.activation(
            s_sp1t.ap().rearrange("p a b -> p (a b)"),
            s_sp1t.ap().rearrange("p a b -> p (a b)"), AF.Ln, bias=1.0), sm_act)
        sp_done = m
        wt("vector", sm_act, sp_done)
        for et in range(8):
            op("vector", lambda e, _et=et, _par=par:
               e.tensor_scalar(s_dl[:, _et, :], s_zd[:, _par, _et, :],
                               s_vecs[:, 16 + _et:17 + _et], 0.0, ALU.add, ALU.max), sm_dve)
        m = sm_dve.c
        wt("vector", sm_dve, m)
        m = op("vector", lambda e: e.tensor_add(
            s_dl.ap().rearrange("p a b -> p (a b)"),
            s_dl.ap().rearrange("p a b -> p (a b)"),
            s_sp1t.ap().rearrange("p a b -> p (a b)")), sm_dve)
        wt("vector", sm_dve, m)
        op("vector", lambda e, _par=par: e.tensor_tensor(
            s_w.ap().rearrange("p a b -> p (a b)"),
            s_dl.ap().rearrange("p a b -> p (a b)"),
            s_x.ap()[:, _par, :, :].rearrange("p a b -> p (a b)"), ALU.mult), sm_dve)
        wt("vector", sm_ld, bc_ready)
        if KDBG and s == 0:
            ddump("dbg_xn", s_xn.ap().rearrange("p a t -> p (a t)"), sm_dve, sm_dve.c)
            ddump("dbg_x", s_x.ap()[:, 0, :, :].rearrange("p a t -> p (a t)"), sm_dve, sm_dve.c)
            ddump("dbg_dl", s_dl.ap().rearrange("p a t -> p (a t)"), sm_dve, sm_dve.c)
            ddump("dbg_ibc", s_ibc.ap(), sm_dve, sm_dve.c)
            ddump("dbg_zd", s_zd.ap()[:, 0, :, :].rearrange("p a t -> p (a t)"), sm_dve, sm_dve.c)
            ddump("dbg_sg", s_sg.ap()[:, 0, :, :].rearrange("p a t -> p (a t)"), sm_act, sm_act.c)
        barrier()

    def phaseS(s):
        par = 0
        tree_marks = {}
        scan_marks = {}
        for et in range(8):
            i = s * 8 + et
            ap_ = et % 2
            u_sl = (2 * i) % 3
            h_sl = (2 * i + 1) % 3
            if et >= 2:
                wt("scalar", sm_dve, scan_marks[et - 2])

            def fa(e, _et=et, _ap=ap_):
                e.memzero(s_a[:, _ap, :, 0:1])
                last = None
                for n in range(N):
                    last = e.activation(s_a[:, _ap, n, 1:SEG], s_dl[:, _et, :], AF.Exp,
                                        scale=s_A[:, _et, n:n + 1])
                return last
            a_done = op("scalar", fa, sm_act)

            if et >= 2:
                wt("gpsimd", sm_dve, tree_marks[et - 2])

            def fp(e, _et=et, _u=u_sl, _par=par):
                e.tensor_copy(s_uh[:, _u, :, 0:1], s_hc.ap()[:, _et, :].unsqueeze(2))
                last = None
                for h in range(2):
                    last = e.tensor_tensor(
                        s_uh[:, _u, :, 1 + h * HSC:1 + (h + 1) * HSC],
                        s_w.ap()[:, _et, h * HSC:(h + 1) * HSC].unsqueeze(1).broadcast_to([P, N, HSC]),
                        s_Brep[:, h, :, :], ALU.mult)
                return last
            u_done = op("gpsimd", fp, sm_pool)

            wt("vector", sm_act, a_done)
            wt("vector", sm_pool, u_done)
            m = op("vector", lambda e, _ap=ap_, _u=u_sl, _h=h_sl:
                   e.tensor_tensor_scan(
                       s_uh.ap()[:, _h, :, :].rearrange("p n t -> p (n t)"),
                       s_a.ap()[:, _ap, :, :].rearrange("p n t -> p (n t)"),
                       s_uh.ap()[:, _u, :, :].rearrange("p n t -> p (n t)"),
                       0.0, ALU.mult, ALU.add), sm_dve)
            wt("vector", sm_dve, m)
            m = op("vector", lambda e, _et=et, _h=h_sl:
                   e.tensor_copy(s_hc.ap()[:, _et, :].unsqueeze(2),
                                 s_uh[:, _h, :, SEG - 1:SEG]), sm_dve)
            scan_marks[et] = m

            if KDBG and s == 0 and et == 0:
                ddump("dbg_a", s_a.ap()[:, ap_, :, :].rearrange("p n t -> p (n t)"), sm_act, a_done)
                ddump("dbg_u", s_uh.ap()[:, u_sl, :, :].rearrange("p n t -> p (n t)"), sm_pool, u_done)
                ddump("dbg_h", s_uh.ap()[:, h_sl, :, :].rearrange("p n t -> p (n t)"), sm_dve, m)
            wt("gpsimd", sm_dve, m)

            def fp2(e, _h=h_sl):
                last = None
                for h in range(2):
                    last = e.tensor_tensor(
                        s_uh[:, _h, :, 1 + h * HSC:1 + (h + 1) * HSC],
                        s_uh[:, _h, :, 1 + h * HSC:1 + (h + 1) * HSC],
                        s_Crep[:, h, :, :], ALU.mult)
                return last
            m_done = op("gpsimd", fp2, sm_pool)

            wt("vector", sm_pool, m_done)
            base = sm_dve.c

            def fv1(e, _h=h_sl):
                hb = s_uh[:, _h]
                return e.tensor_add(hb[:, 0:8, :], hb[:, 0:8, :], hb[:, 8:16, :])
            op("vector", fv1, sm_dve)
            wt("vector", sm_dve, base + 1)
            op("vector", lambda e, _h=h_sl: e.tensor_add(
                s_uh[:, _h][:, 0:4, :], s_uh[:, _h][:, 0:4, :], s_uh[:, _h][:, 4:8, :]), sm_dve)
            wt("vector", sm_dve, base + 2)
            op("vector", lambda e, _h=h_sl: e.tensor_add(
                s_uh[:, _h][:, 0:2, :], s_uh[:, _h][:, 0:2, :], s_uh[:, _h][:, 2:4, :]), sm_dve)
            wt("vector", sm_dve, base + 3)
            op("vector", lambda e, _h=h_sl: e.tensor_tensor(
                s_y[:, :], s_uh[:, _h][:, 0, 1:SEG], s_uh[:, _h][:, 1, 1:SEG], ALU.add), sm_dve)
            wt("vector", sm_dve, base + 4)
            op("vector", lambda e, _et=et, _par=par: e.scalar_tensor_tensor(
                s_g1[:], s_x[:, _par, _et, :], s_vecs[:, 8 + _et:9 + _et],
                s_y[:], ALU.mult, ALU.add), sm_dve)
            wt("vector", sm_dve, base + 5)
            m = op("vector", lambda e, _et=et, _par=par: e.tensor_tensor(
                s_y2[:, _par, _et, :], s_g1[:], s_sg[:, _par, _et, :], ALU.mult), sm_dve)
            tree_marks[et] = m
        barrier()

    def phaseO(s):
        par = 0
        ex0 = sm_ex.c

        def f(e, _s=s, _par=par, _b=ex0):
            src = s_y2.ap()[:, _par, :, :].rearrange("p a t -> p (a t)")
            me_sw(e,
                  lambda ee: ee.dma_start(sh_y2.ap()[0, _s], src).then_inc(sm_ex.h, 16),
                  lambda ee: ee.dma_start(sh_y2.ap()[1, _s], src).then_inc(sm_ex.h, 16))
            e.wait_ge(sm_ex.h, _b + 16)
            flag_write(e, FL_Y2 + _s)
            flag_poll(e, FL_Y2 + _s)
            dst = s_y2p.ap().rearrange("p a t -> p (a t)")
            me_sw(e,
                  lambda ee: ee.dma_start(dst, sh_y2.ap()[1, _s]).then_inc(sm_ex.h, 16),
                  lambda ee: ee.dma_start(dst, sh_y2.ap()[0, _s]).then_inc(sm_ex.h, 16))
        emit("sync", f)
        sm_ex.c += 48
        y2x = sm_ex.c
        if KDBG and s == 0:
            ddump("dbg_y2p", s_y2p.ap().rearrange("p a t -> p (a t)"), sm_ex, y2x)
            dwo = dbg["dbg_wo"].ap().bitcast(BF16)
            for ec in range(16):
                dma("sync", dwo[:, ec * P:(ec + 1) * P], w_out_b.ap()[0, ec], sm_st)

        wt("tensor", sm_ex, y2x)
        evac_marks = {}
        for dt in range(8):
            if dt >= 1:
                wt("sync", sm_dve, evac_marks[dt - 1])
            m = dma("sync", s_rot[:, 0, :, :],
                    w_out_b.ap()[dt].rearrange("ec p m -> p ec m"), sm_ld)
            bk = rot_next()
            wt("tensor", sm_ld, m)

            def ft(e, _dt=dt, _bk=bk, _par=par):
                last = None
                for ec in range(16):
                    rhs = s_y2[:, _par, ec, :] if ec < 8 else s_y2p[:, ec - 8, :]
                    last = e.matmul(p_rot[:, _bk, 0:SC], s_rot[:, 0, ec, :], rhs,
                                    start=(ec == 0), stop=(ec == 15))
                return last
            m = op("tensor", ft, sm_pe)
            wt("vector", sm_pe, m)
            em = op("vector", lambda e, _dt=dt, _bk=bk, _sl=0:
                    e.scalar_tensor_tensor(s_x1p[:, _dt, :], s_resid[:, _sl, _dt, :], 2.0,
                                           p_rot[:, _bk, 0:SC], ALU.mult, ALU.add), sm_dve)
            evac_marks[dt] = em
            rot_release(bk, sm_dve, em)

        wt("scalar", sm_dve, evac_marks[7])
        mcp = op("scalar", lambda e: e.activation(
            s_xn.ap().rearrange("p a t -> p (a t)"),
            s_x1p.ap().rearrange("p a t -> p (a t)"), AF.Copy), sm_act)
        wt("tensor", sm_act, mcp)
        if KDBG and s == 0:
            ddump("dbg_x1pre", s_xn.ap().rearrange("p a t -> p (a t)"), sm_act, mcp)
        bk = rot_next()
        bk2 = rot_next()
        st_marks = {}
        for dc in range(8):
            if dc >= 2:
                wt("vector", sm_pe, st_marks[dc - 2])
            mq = op("vector", lambda e, _dc=dc: e.tensor_tensor(
                s_sq[:, _dc % 2, :], s_x1p[:, _dc, :], s_x1p[:, _dc, :], ALU.mult), sm_dve)
            wt("tensor", sm_dve, mq)

            def ft(e, _dc=dc, _b1=bk, _b2=bk2):
                e.matmul(p_rot[:, _b1, 0:SC], s_o128[:, :], s_xn[:, _dc, :],
                         start=(_dc == 0), stop=(_dc == 7))
                return e.matmul(p_rot[:, _b2, 0:SC], s_o128[:, :], s_sq[:, _dc % 2, :],
                                start=(_dc == 0), stop=(_dc == 7))
            st_marks[dc] = op("tensor", ft, sm_pe)
        st_done = sm_pe.c
        wt("vector", sm_pe, st_done)
        base = sm_dve.c
        op("vector", lambda e, _b=bk: e.tensor_scalar_mul(s_rep[:, 0, :], p_rot[:, _b, 0:SC], 1.0 / 1024.0), sm_dve)
        op("vector", lambda e, _b=bk2: e.tensor_scalar_mul(s_lnt[:, :], p_rot[:, _b, 0:SC], 1.0 / 1024.0), sm_dve)
        rot_release(bk, sm_dve, base + 1)
        rot_release(bk2, sm_dve, base + 2)
        wt("vector", sm_dve, base + 2)
        op("vector", lambda e: e.tensor_tensor(s_lnt2[:, :], s_rep[:, 0, :], s_rep[:, 0, :], ALU.mult), sm_dve)
        wt("vector", sm_dve, base + 3)
        m = op("vector", lambda e: e.tensor_sub(s_lnt[:, :], s_lnt[:, :], s_lnt2[:, :]), sm_dve)
        wt("scalar", sm_dve, m)
        m = op("scalar", lambda e: e.activation(s_lnt[:, :], s_lnt[:, :], AF.Ln, bias=1e-5), sm_act)
        wt("scalar", sm_act, m)
        m = op("scalar", lambda e: e.activation(s_lnt[:, :], s_lnt[:, :], AF.Exp, scale=-0.5), sm_act)
        wt("vector", sm_act, m)
        base2 = sm_dve.c
        op("vector", lambda e: e.tensor_tensor(s_rep[:, 1, :], s_rep[:, 0, :], s_lnt[:, :], ALU.mult), sm_dve)
        wt("vector", sm_dve, base2 + 1)
        m = op("vector", lambda e: e.tensor_copy(s_rep[:, 0, :], s_lnt[:, :]), sm_dve)
        wt("vector", sm_dve, base2 + 2)
        base = sm_dve.c
        op("vector", lambda e: e.tensor_tensor(
            s_x1p.ap(), s_x1p.ap(),
            s_rep.ap()[:, 0, :].unsqueeze(1).broadcast_to([P, 8, SC]),
            ALU.mult), sm_dve)
        wt("vector", sm_dve, base + 1)
        m = op("vector", lambda e: e.tensor_tensor(
            s_x1p.ap(), s_x1p.ap(),
            s_rep.ap()[:, 1, :].unsqueeze(1).broadcast_to([P, 8, SC]),
            ALU.subtract), sm_dve)
        wt("scalar", sm_dve, m)
        x1b_done = op("scalar", lambda e: e.activation(
            s_x1b.ap().rearrange("p a t -> p (a t)"),
            s_x1p.ap().rearrange("p a t -> p (a t)"), AF.Copy), sm_act)

        wt("sync", sm_act, x1b_done)

        def fst(e, _s=s):
            d1 = x1b_d.ap()[:, :, _s * HSC:(_s + 1) * HSC].rearrange("a p t -> p a t")
            me_sw(e,
                  lambda ee: ee.dma_start(d1, s_x1b[:, :, 0:HSC]).then_inc(sm_st.h, 16),
                  lambda ee: ee.dma_start(d1, s_x1b[:, :, HSC:SC]).then_inc(sm_st.h, 16))
        emit("sync", fst)
        sm_st.c += 16

        bk = rot_next()
        wt("tensor", sm_act, x1b_done)

        def ft(e, _bk=bk):
            last = None
            for dc in range(8):
                last = e.matmul(p_rot[:, _bk, 0:SC], s_wrt[:, dc, :], s_x1b[:, dc, :],
                                start=(dc == 0), stop=(dc == 7))
            return last
        m = op("tensor", ft, sm_pe)
        wt("scalar", sm_pe, m)
        m = op("scalar", lambda e, _bk=bk: e.activation(
            s_lg[:, :], p_rot[0:NEXP, _bk, 0:SC], AF.Copy), sm_act)
        rot_release(bk, sm_act, m)
        wt("vector", sm_act, m)
        m = op("vector", lambda e: e.tensor_scalar_add(
            s_lg[:, :], s_lg[:, :], s_vecs[0:NEXP, 192:193]), sm_dve)
        bk = rot_next()
        wt("tensor", sm_dve, m)

        def ft2(e, _bk=bk):
            last = None
            for blk in range(2):
                last = e.transpose(p_rot[0:P, _bk, blk * NEXP:(blk + 1) * NEXP],
                                   s_lg[:, blk * P:(blk + 1) * P], s_id16[0:NEXP, 0:NEXP])
            return last
        m = op("tensor", ft2, sm_pe)
        wt("scalar", sm_pe, m)
        m = op("scalar", lambda e, _bk=bk: e.activation(
            s_lgt.ap().rearrange("p a b -> p (a b)"),
            p_rot[:, _bk, 0:2 * NEXP], AF.Copy), sm_act)
        rot_release(bk, sm_act, m)
        lgt_done = m

        for blk in range(2):
            wt("vector", sm_act, lgt_done)
            m = op("vector", lambda e, _b=blk: e.tensor_reduce(
                s_mx[:, _b, 0:1], s_lgt[:, _b, :], AX.X, ALU.max, negate=True), sm_dve)
            wt("scalar", sm_dve, m)
            m = op("scalar", lambda e, _b=blk: e.activation(
                s_sm[:, _b, :], s_lgt[:, _b, :], AF.Exp, bias=s_mx[:, _b, 0:1]), sm_act)
            wt("vector", sm_act, m)
            m = op("vector", lambda e, _b=blk: e.tensor_reduce(
                s_mx[:, _b, 1:2], s_sm[:, _b, :], AX.X, ALU.add), sm_dve)
            wt("scalar", sm_dve, m)
            m = op("scalar", lambda e, _b=blk: e.activation(
                s_mx[:, _b, 2:3], s_mx[:, _b, 1:2], AF.Ln), sm_act)
            wt("scalar", sm_act, m)
            m = op("scalar", lambda e, _b=blk: e.activation(
                s_mx[:, _b, 2:3], s_mx[:, _b, 2:3], AF.Exp, scale=-1.0), sm_act)
            wt("vector", sm_act, m)
            base = sm_dve.c
            op("vector", lambda e, _b=blk: e.tensor_scalar_mul(
                s_sm[:, _b, :], s_sm[:, _b, :], s_mx[:, _b, 2:3]), sm_dve)
            wt("vector", sm_dve, base + 1)
            op("vector", lambda e, _b=blk: e.tensor_copy(s_wk[:, _b, :], s_sm[:, _b, :]), sm_dve)
            wt("vector", sm_dve, base + 2)
            for it in range(3):
                op("vector", lambda e, _b=blk: e.tensor_reduce(
                    s_mx[:, _b, 3:4], s_wk[:, _b, :], AX.X, ALU.max), sm_dve)
                wt("vector", sm_dve, sm_dve.c)
                if it < 2:
                    op("vector", lambda e, _b=blk: e.tensor_single_scalar(
                        s_msk[:, _b, :], s_wk[:, _b, :], s_mx[:, _b, 3:4], ALU.is_ge), sm_dve)
                    wt("vector", sm_dve, sm_dve.c)
                    op("vector", lambda e, _b=blk: e.copy_predicated(
                        s_wk[:, _b, :], s_msk[:, _b, :], s_negs[:, :]), sm_dve)
                    wt("vector", sm_dve, sm_dve.c)
            op("vector", lambda e, _b=blk: e.tensor_single_scalar(
                s_msk[:, _b, :], s_sm[:, _b, :], s_mx[:, _b, 3:4], ALU.is_ge), sm_dve)
            wt("vector", sm_dve, sm_dve.c)
            op("vector", lambda e, _b=blk: e.memset(s_wk[:, _b, :], 0.0), sm_dve)
            wt("vector", sm_dve, sm_dve.c)
            op("vector", lambda e, _b=blk: e.copy_predicated(
                s_wk[:, _b, :], s_msk[:, _b, :], s_sm[:, _b, :]), sm_dve)
            wt("vector", sm_dve, sm_dve.c)
            op("vector", lambda e, _b=blk: e.tensor_copy(s_cwt[:, _b, :], s_wk[:, _b, :]), sm_dve)
        cw_done = sm_dve.c

        bk = rot_next()
        wt("tensor", sm_dve, cw_done)
        pbf = p_rot.ap().bitcast(BF16)

        def ft3(e, _bk=bk, _v=pbf):
            last = None
            for blk in range(2):
                last = e.transpose(_v[0:NEXP, _bk, blk * P:(blk + 1) * P], s_cwt[:, blk, :], s_idb[:, :])
            return last
        m = op("tensor", ft3, sm_pe)
        wt("scalar", sm_pe, m)

        def fa(e, _bk=bk, _v=pbf):
            last = None
            for blk in range(2):
                last = e.activation(s_cwf[:, blk, :], _v[0:NEXP, _bk, blk * P:(blk + 1) * P], AF.Copy)
            return last
        m = op("scalar", fa, sm_act)
        rot_release(bk, sm_act, m)
        wt("sync", sm_act, m)

        if KDBG and s == 0:
            ddump("dbg_y2", s_y2.ap()[:, 0, :, :].rearrange("p a t -> p (a t)"), sm_dve, sm_dve.c)
            ddump("dbg_x1", s_x1p.ap().rearrange("p a t -> p (a t)"), sm_dve, sm_dve.c)
            ddump("dbg_cw", s_cwf.ap().rearrange("a b c -> a (b c)"), sm_act, sm_act.c)

        def fcw2(e, _s=s):
            d = cw_d.ap()[:, _s * HSC:(_s + 1) * HSC]
            me_sw(e,
                  lambda ee: ee.dma_start(d, s_cwf[:, 0, :]).then_inc(sm_st.h, 16),
                  lambda ee: ee.dma_start(d, s_cwf[:, 1, :]).then_inc(sm_st.h, 16))
        emit("sync", fcw2)
        sm_st.c += 16
        barrier()

    def phaseF(fi):
        x1s_ld = dma("sync", s_x1s.ap(),
                     x1b_d.ap()[:, :, fi * FS:(fi + 1) * FS].rearrange("a p t -> p a t"), sm_ld)
        mz = op("vector", lambda e: e.memset(s_cwo[0:16, :], 1.0), sm_dve)
        wt("sync", sm_dve, mz)
        m = dma("sync", s_cwo[0:NEXP, :], cw_d.ap()[:, fi * FS:(fi + 1) * FS], sm_ld)
        wt("vector", sm_ld, m)
        cwo_done = op("vector", lambda e: e.memset(
            s_racc.ap().rearrange("p a b t -> p (a b t)"), 0.0), sm_dve)

        wt("tensor", sm_dve, cwo_done)
        wt("tensor", sm_ld, m)
        wt("tensor", sm_ld, x1s_ld)
        if KDBG:
            ddump(f"dbg_cwo{fi}", s_cwo.ap()[0:16, :], sm_ld, m)
            ddump(f"dbg_fx1{fi}", s_x1s.ap().rearrange("p a t -> p (a t)"), sm_ld, x1s_ld)

        def ft(e):
            last = None
            for dt in range(6):
                for h in range(2):
                    last = e.matmul(p_dn[:, dt, h, :], s_b2[:, dt * P:(dt + 1) * P],
                                    s_cwo[:, h * HF:(h + 1) * HF], start=(h == 0), stop=False)
            return last
        op("tensor", ft, sm_pe)
        for dt in (6, 7):
            bk = rot_next()
            m = op("tensor", lambda e, _dt=dt, _bk=bk:
                   [e.matmul(p_rot[:, _bk, h * HF:(h + 1) * HF], s_b2[:, _dt * P:(_dt + 1) * P],
                             s_cwo[:, h * HF:(h + 1) * HF], start=(h == 0), stop=(h == 1)) for h in range(2)][-1], sm_pe)
            wt("scalar", sm_pe, m)
            m = op("scalar", lambda e, _dt=dt, _bk=bk:
                   [e.activation(s_racc[:, _dt - 6, h, :], p_rot[:, _bk, h * HF:(h + 1) * HF], AF.Copy)
                    for h in range(2)][-1], sm_act)
            rot_release(bk, sm_act, m)

        w1_marks = {}
        w2_marks = {}
        for ex in range(NE):
            cwr_ld = None
            if ex < NEXP:
                cwr_ld = dma("sync", s_cwr[:, :],
                             bass.AP(cw_d, ex * TOK + fi * FS, [[0, P], [1, 1], [1, FS]]), sm_ld)
            for ht in range(8):
                gidx = ex * 8 + ht
                if gidx >= 1:
                    wt("sync", sm_pe, w1_marks[gidx - 1])
                m = dma("sync", s_rw1[:, 0, :, :],
                        w_f1_b.ap()[ex, ht].rearrange("dc p m -> p dc m"), sm_ld)
                bk = rot_next()
                wt("tensor", sm_ld, m)

                def ft(e, _ht=ht, _bk=bk):
                    last = None
                    for dc in range(8):
                        last = e.matmul(p_rot[:, _bk, :], s_rw1[:, 0, dc, :], s_x1s[:, dc, :],
                                        start=(dc == 0), stop=(dc == 7))
                    return last
                mmk = op("tensor", ft, sm_pe)
                w1_marks[gidx] = mmk
                wt("scalar", sm_pe, mmk)
                m = op("scalar", lambda e, _e=ex, _ht=ht, _bk=bk:
                       [e.activation(s_hs[:, h, _ht, :], p_rot[:, _bk, h * HF:(h + 1) * HF],
                                     AF.Relu, bias=s_vecs[:, 193 + _ht * 16 + _e:194 + _ht * 16 + _e])
                        for h in range(2)][-1], sm_act)
                rot_release(bk, sm_act, m)
                if ex < NEXP:
                    wt("gpsimd", sm_act, m)
                    if ht == 0:
                        wt("gpsimd", sm_ld, cwr_ld)
                    op("gpsimd", lambda e, _ht=ht:
                       [e.tensor_tensor(s_hs[:, h, _ht, :], s_hs[:, h, _ht, :],
                                        s_cwr.ap().rearrange("p (h t) -> p h t", h=2)[:, h, :], ALU.mult)
                        for h in range(2)][-1], sm_pool)
            hs_pool = sm_pool.c
            hs_act = sm_act.c
            if KDBG and fi == 0 and ex == 0:
                ddump("dbg_hs", s_hs.ap().rearrange("p a b t -> p (a b t)"), sm_pool, hs_pool)
            if ex < NEXP:
                wt("tensor", sm_pool, hs_pool)
            else:
                wt("tensor", sm_act, hs_act)
            for dt in range(8):
                gdx = ex * 8 + dt
                if gdx >= 1:
                    wt("sync", sm_pe, w2_marks[gdx - 1])
                m = dma("sync", s_rw2[:, 0, :, :],
                        w_f2_b.ap()[ex, dt].rearrange("ht p m -> p ht m"), sm_ld)
                wt("tensor", sm_ld, m)
                if dt < 6:
                    def ft(e, _e=ex, _dt=dt):
                        last = None
                        for ht in range(8):
                            for h in range(2):
                                last = e.matmul(p_dn[:, _dt, h, :], s_rw2[:, 0, ht, :],
                                                s_hs[:, h, ht, :], start=False,
                                                stop=(_e == NE - 1 and ht == 7))
                        return last
                    w2_marks[gdx] = op("tensor", ft, sm_pe)
                else:
                    bk = rot_next()

                    def ft(e, _dt=dt, _bk=bk):
                        last = None
                        for ht in range(8):
                            for h in range(2):
                                last = e.matmul(p_rot[:, _bk, h * HF:(h + 1) * HF],
                                                s_rw2[:, 0, ht, :], s_hs[:, h, ht, :],
                                                start=(ht == 0 and h == 0),
                                                stop=(ht == 7 and h == 1))
                        return last
                    m = op("tensor", ft, sm_pe)
                    w2_marks[gdx] = m
                    wt("vector", sm_pe, m)
                    base = sm_dve.c
                    for h in range(2):
                        op("vector", lambda e, _dt=dt, _bk=bk, _h=h:
                           e.scalar_tensor_tensor(s_racc[:, _dt - 6, _h, :],
                                                  p_rot[:, _bk, _h * HF:(_h + 1) * HF], 1.0,
                                                  s_racc[:, _dt - 6, _h, :], ALU.mult, ALU.add), sm_dve)
                        wt("vector", sm_dve, base + 1 + h)
                    rot_release(bk, sm_dve, base + 2)
        dn_done = sm_pe.c
        if KDBG:
            ddump(f"dbg_racc{fi}", s_racc.ap().rearrange("p a b t -> p (a b t)"), sm_dve, sm_dve.c)
            for dt in range(6):
                wt("scalar", sm_pe, dn_done)
                if dt >= 1:
                    wt("scalar", sm_st, sm_st.c)
                mdx = op("scalar", lambda e, _dt=dt: e.activation(
                    s_hs.ap()[:, 0, 0:2, :].rearrange("p a t -> p (a t)"),
                    p_dn.ap()[:, _dt, :, :].rearrange("p b t -> p (b t)"),
                    AF.Copy), sm_act)
                wt("sync", sm_act, mdx)
                dma("sync", dbg[f"dbg_pdn{fi}"].ap().bitcast(BF16)[:, dt * 2 * HF:(dt + 1) * 2 * HF],
                    s_hs.ap()[:, 0, 0:2, :].rearrange("p a t -> p (a t)"), sm_st)
            wt("scalar", sm_st, sm_st.c)

        for h in range(2):
            x2p_marks = []
            for dt in range(8):
                if dt == 0:
                    wt("vector", sm_pe, dn_done)
                if dt < 6:
                    m = op("vector", lambda e, _dt=dt, _h=h:
                           e.scalar_tensor_tensor(s_x2p[:, _dt, :], p_dn[:, _dt, _h, :], 1.0,
                                                  s_x1s[:, _dt, _h * HF:(_h + 1) * HF],
                                                  ALU.mult, ALU.add), sm_dve)
                else:
                    m = op("vector", lambda e, _dt=dt, _h=h:
                           e.tensor_add(s_x2p[:, _dt, :], s_racc[:, _dt - 6, _h, :],
                                        s_x1s[:, _dt, _h * HF:(_h + 1) * HF]), sm_dve)
                x2p_marks.append(m)
            if KDBG:
                ddump(f"dbg_x2a{fi}{h}", s_x2p.ap().rearrange("p a t -> p (a t)"), sm_dve, sm_dve.c)
            wt("scalar", sm_dve, x2p_marks[7])
            mcp = op("scalar", lambda e: e.activation(
                s_x1s.ap()[:, :, 0:HF].rearrange("p a t -> p a t"),
                s_x2p.ap(), AF.Copy), sm_act)
            wt("tensor", sm_act, mcp)
            bk = rot_next()
            bk2 = rot_next()
            st_marks = {}
            for dc in range(8):
                if dc >= 2:
                    wt("vector", sm_pe, st_marks[dc - 2])
                mq = op("vector", lambda e, _dc=dc: e.tensor_tensor(
                    s_sq[:, _dc % 2, 0:HF], s_x2p[:, _dc, :], s_x2p[:, _dc, :], ALU.mult), sm_dve)
                wt("tensor", sm_dve, mq)

                def ft(e, _dc=dc, _b1=bk, _b2=bk2):
                    e.matmul(p_rot[:, _b1, 0:HF], s_o128[:, :], s_x1s[:, _dc, 0:HF],
                             start=(_dc == 0), stop=(_dc == 7))
                    return e.matmul(p_rot[:, _b2, 0:HF], s_o128[:, :], s_sq[:, _dc % 2, 0:HF],
                                    start=(_dc == 0), stop=(_dc == 7))
                st_marks[dc] = op("tensor", ft, sm_pe)
            st2 = sm_pe.c
            wt("vector", sm_pe, st2)
            base = sm_dve.c
            op("vector", lambda e, _b=bk: e.tensor_scalar_mul(s_r2[:, 0, :], p_rot[:, _b, 0:HF], 1.0 / 1024.0), sm_dve)
            op("vector", lambda e, _b=bk2: e.tensor_scalar_mul(s_lnt[:, 0:HF], p_rot[:, _b, 0:HF], 1.0 / 1024.0), sm_dve)
            rot_release(bk, sm_dve, base + 1)
            rot_release(bk2, sm_dve, base + 2)
            wt("vector", sm_dve, base + 2)
            op("vector", lambda e: e.tensor_tensor(s_lnt2[:, 0:HF], s_r2[:, 0, :], s_r2[:, 0, :], ALU.mult), sm_dve)
            wt("vector", sm_dve, base + 3)
            m = op("vector", lambda e: e.tensor_sub(s_lnt[:, 0:HF], s_lnt[:, 0:HF], s_lnt2[:, 0:HF]), sm_dve)
            wt("scalar", sm_dve, m)
            m = op("scalar", lambda e: e.activation(s_lnt[:, 0:HF], s_lnt[:, 0:HF], AF.Ln, bias=1e-5), sm_act)
            wt("scalar", sm_act, m)
            m = op("scalar", lambda e: e.activation(s_lnt[:, 0:HF], s_lnt[:, 0:HF], AF.Exp, scale=-0.5), sm_act)
            wt("vector", sm_act, m)
            base2 = sm_dve.c
            op("vector", lambda e: e.tensor_tensor(s_r2[:, 1, :], s_r2[:, 0, :], s_lnt[:, 0:HF], ALU.mult), sm_dve)
            wt("vector", sm_dve, base2 + 1)
            m = op("vector", lambda e: e.tensor_copy(s_r2[:, 0, :], s_lnt[:, 0:HF]), sm_dve)
            wt("vector", sm_dve, base2 + 2)
            base = sm_dve.c
            op("vector", lambda e: e.tensor_tensor(
                s_x2p.ap(), s_x2p.ap(),
                s_r2.ap()[:, 0, :].unsqueeze(1).broadcast_to([P, 8, HF]),
                ALU.mult), sm_dve)
            wt("vector", sm_dve, base + 1)
            m = op("vector", lambda e: e.tensor_tensor(
                s_x2p.ap(), s_x2p.ap(),
                s_r2.ap()[:, 1, :].unsqueeze(1).broadcast_to([P, 8, HF]),
                ALU.subtract), sm_dve)
            wt("sync", sm_dve, m)
            if KDBG:
                ddump(f"dbg_x2n{fi}{h}", s_x2p.ap().rearrange("p a t -> p (a t)"), sm_dve, m)
            col = fi * FS + h * HF
            dma("sync", t_out.ap().rearrange("(a p) t -> p a t", p=P)[:, :, col:col + HF],
                s_x2p[:, :, :], sm_st)
            barrier()

    pass
    _lim = os.environ.get("KLIM", "")
    if _lim == "A0":
        prologue()
        phaseA(0)
    elif _lim == "A0NP":
        phaseA(0)
    elif _lim.startswith("N"):
        n = int(_lim[1:])
        prologue()
        for s in range(n):
            phaseA(s)
            phaseS(s)
            phaseO(s)
    elif _lim.startswith("A:"):
        prologue()
        phaseA(0, acut=int(_lim[2:]))
    elif _lim in ("A", "AS", "ASO"):
        prologue()
        phaseA(0)
        if len(_lim) >= 2:
            phaseS(0)
        if len(_lim) >= 3:
            phaseO(0)
    else:
        prologue()
        for s in range(NSC):
            phaseA(s)
            phaseS(s)
            phaseO(s)
        for fi in range(NFS):
            phaseF(fi)

    final_st = sm_st.c
    for e in ("sync", "gpsimd"):
        emit(e, lambda eng, _v=final_st: eng.wait_ge(sm_st.h, _v))

    with ctx:
        with nc.Block() as block:
            @block.sync
            def _(eng):
                for fn in cmds["sync"]:
                    fn(eng)

            @block.tensor
            def _(eng):
                for fn in cmds["tensor"]:
                    fn(eng)

            @block.vector
            def _(eng):
                for fn in cmds["vector"]:
                    fn(eng)

            @block.scalar
            def _(eng):
                for fn in cmds["scalar"]:
                    fn(eng)

            @block.gpsimd
            def _(eng):
                for fn in cmds["gpsimd"]:
                    fn(eng)

    return nc


_NC_CACHE = {}


def kernel(**inputs):
    if "nc" not in _NC_CACHE:
        _NC_CACHE["nc"] = build_nc()
    nc = _NC_CACHE["nc"]
    in_maps = [prep_core_inputs(k, inputs) for k in range(NCORES)]
    res = run_bass_kernel_spmd(nc, in_maps, core_ids=list(range(NCORES)), trace=bool(os.environ.get("KTRACE")))
    _NC_CACHE["res"] = res
    out = np.zeros((B, L, D), np.float32)
    for k in range(NCORES):
        b, m = k // 2, k % 2
        o = np.asarray(res.results[k]["o_x2"])
        for s in range(NSC):
            t0 = s * SC + m * HSC
            out[b, t0:t0 + HSC, :] = o[:, s * HSC:(s + 1) * HSC].T
    return out


# version 04
# speedup vs baseline: 1.1018x; 1.1018x over previous
import os
"""Fused single-launch Trainium2 megakernel for mk_seq_postln_d1 (mamba mixer +
dense-MoE FFN, depth 1, d_model 1024, batch 4, seq 4096).

Sharding: core k -> batch k//2, mamba-channel half k%2, per-SC token half k%2
(each SC of 256 tokens: core m owns columns [m*128, m*128+128)).
Cross-core (pair) exchange via addr_space="Shared" DRAM + flag polling.
Layout: feature-major activations ([feature-chunk on partitions, tokens free]).
Scan: segmented tensor_tensor_scan (16 state lanes per channel row, reset cols).
"""
from contextlib import ExitStack

import numpy as np
import ml_dtypes

import concourse.bass as bass
import concourse.mybir as mybir
from concourse.bass_utils import run_bass_kernel_spmd

F32 = mybir.dt.float32
BF16 = mybir.dt.bfloat16
U32 = mybir.dt.uint32
AF = mybir.ActivationFunctionType
ALU = mybir.AluOpType
AX = mybir.AxisListType

B, L, D, E, N, DR = 4, 4096, 1024, 2048, 16, 64
EH = E // 2
NEXP = 15
NE = 16
SC = 256
NSC = L // SC
HSC = SC // 2
TOK = NSC * HSC
FS = 512
HF = FS // 2
NFS = TOK // FS
SEG = SC + 1
P = 128
MAGIC = 0x5A17C0DE
NCORES = 8

CONSTS = [1e-5, 1.0 / 1024.0, -0.5, -1.0, 1.0, 0.0, 2.0, 0.5]


def prep_core_inputs(core, I):
    b = core // 2
    m = core % 2
    sl = slice(m * EH, (m + 1) * EH)
    d = {}
    d["resid_fm"] = np.ascontiguousarray(np.asarray(I["L0_attn_resid"][b]).T).astype(np.float32)
    for nm, w in (("w_in", I["L0_attn_in_w"]), ("w_skip", I["L0_attn_skip_w"])):
        wt = np.asarray(w)[sl].T.astype(np.float32)
        d[nm] = np.ascontiguousarray(wt.reshape(8, P, 8, P).transpose(2, 0, 1, 3))
    ot = np.asarray(I["L0_attn_out_w"]).T.astype(np.float32)
    oc = np.concatenate([ot[sl], ot[(1 - m) * EH:(2 - m) * EH]], axis=0)
    d["w_out"] = np.ascontiguousarray(oc.reshape(16, P, 8, P).transpose(2, 0, 1, 3))
    wibc = np.zeros((8, P, P), np.float32)
    w = np.asarray(I["L0_attn_wd1"])[:, sl].T.astype(np.float32).reshape(8, P, DR)
    wibc[:, :, 0:64] = w
    wb = np.asarray(I["L0_attn_wB"])[:, sl].T.astype(np.float32).reshape(8, P, N)
    wibc[:, :, 64:80] = wb
    wc = np.asarray(I["L0_attn_wC"])[:, sl].T.astype(np.float32).reshape(8, P, N)
    wibc[:, :, 96:112] = wc
    d["w_ibc"] = np.ascontiguousarray(wibc)
    d["w_wd2"] = np.ascontiguousarray(np.asarray(I["L0_attn_wd2"])[sl].T.astype(np.float32))
    rw = np.zeros((8, P, P), np.float32)
    rw[:, :, 0:NEXP] = np.asarray(I["L0_ffn_router_W"]).astype(np.float32).reshape(8, P, NEXP)
    d["w_rt"] = np.ascontiguousarray(rw)
    e1 = np.concatenate([np.asarray(I["L0_ffn_eW1"]), np.asarray(I["L0_ffn_sW1"])[None]], 0).astype(np.float32)
    e2 = np.concatenate([np.asarray(I["L0_ffn_eW2"]), np.asarray(I["L0_ffn_sW2"])[None]], 0).astype(np.float32)
    d["w_f1"] = np.ascontiguousarray(e1.reshape(NE, 8, P, 8, P).transpose(0, 3, 1, 2, 4))
    d["w_f2"] = np.ascontiguousarray(e2.reshape(NE, 8, P, 8, P).transpose(0, 3, 1, 2, 4))
    d["b2cat"] = np.ascontiguousarray(
        np.concatenate([np.asarray(I["L0_ffn_eb2"]), np.asarray(I["L0_ffn_sb2"])[None]], 0).astype(np.float32))
    cols = []

    def pp(v):
        return np.asarray(v).astype(np.float32)[sl].reshape(8, P).T

    cols.append(pp(I["L0_attn_conv_b"]))
    cols.append(pp(I["L0_attn_W_D"]))
    cols.append(pp(I["L0_attn_wd2_b"]))
    cw = np.asarray(I["L0_attn_conv_w"]).astype(np.float32)[sl]
    cols.append(cw.reshape(8, P, 4).transpose(1, 0, 2).reshape(P, 32))
    cols.append(np.asarray(I["L0_attn_norm_w"]).astype(np.float32).reshape(8, P).T)
    al = np.asarray(I["L0_attn_A_log"]).astype(np.float32)[sl]
    cols.append(al.reshape(8, P, N).transpose(1, 0, 2).reshape(P, 128))
    rb = np.zeros((P, 1), np.float32)
    rb[:NEXP, 0] = np.asarray(I["L0_ffn_router_b"]).astype(np.float32)
    cols.append(rb)
    eb1 = np.concatenate([np.asarray(I["L0_ffn_eb1"]), np.asarray(I["L0_ffn_sb1"])[None]], 0).astype(np.float32)
    cols.append(eb1.reshape(NE, 8, P).transpose(2, 1, 0).reshape(P, 128))
    cols.append(np.ones((P, 1), np.float32))
    d["vecs"] = np.ascontiguousarray(np.concatenate(cols, axis=1))
    d["consts"] = np.broadcast_to(np.array(CONSTS, np.float32), (P, len(CONSTS))).copy()
    d["id128b"] = np.eye(P, dtype=np.float32)
    d["id16"] = np.eye(16, dtype=np.float32)
    return d


def build_nc():
    nc = bass.Bass("TRN2", target_bir_lowering=False)

    di = lambda n, sh, dt=F32: nc.dram_tensor(n, sh, dt, kind="ExternalInput")
    t_res = di("resid_fm", [D, L])
    t_win = di("w_in", [8, 8, P, P])
    t_wsk = di("w_skip", [8, 8, P, P])
    t_wout = di("w_out", [8, 16, P, P])
    t_wibc = di("w_ibc", [8, P, P])
    t_wd2 = di("w_wd2", [DR, EH])
    t_wrt = di("w_rt", [8, P, P])
    t_wf1 = di("w_f1", [NE, 8, 8, P, P])
    t_wf2 = di("w_f2", [NE, 8, 8, P, P])
    t_b2 = di("b2cat", [16, D])
    t_vecs = di("vecs", [P, 322])
    t_consts = di("consts", [P, 8])
    t_idb = di("id128b", [P, P])
    t_id16 = di("id16", [16, 16])

    t_out = nc.dram_tensor("o_x2", [D, TOK], F32, kind="ExternalOutput")
    KDBG = bool(os.environ.get("KDEBUG"))
    if KDBG:
        dbg = {}
        for nm, sh in (("dbg_xn", [P, 8 * SC]), ("dbg_x", [P, 8 * SC]), ("dbg_dl", [P, 8 * SC]),
                       ("dbg_ibc", [112, SC]), ("dbg_a", [P, N * SEG]), ("dbg_u", [P, N * SEG]),
                       ("dbg_h", [P, N * SEG]), ("dbg_y2", [P, 8 * SC]), ("dbg_x1", [P, 8 * SC]),
                       ("dbg_cw", [NEXP, 2 * P]), ("dbg_sg", [P, 8 * SC]), ("dbg_zd", [P, 8 * SC]),
                       ("dbg_y2p", [P, 8 * SC]), ("dbg_x1pre", [P, 8 * SC]), ("dbg_wo", [P, 16 * P]),
                       ("dbg_hs", [P, 8 * HF])):
            dbg[nm] = nc.dram_tensor(nm, sh, F32, kind="ExternalOutput")
        for fi in range(4):
            for nm, sh in ((f"dbg_cwo{fi}", [16, FS]), (f"dbg_fx1{fi}", [P, 4 * FS]),
                           (f"dbg_racc{fi}", [P, 4 * HF]), (f"dbg_pdn{fi}", [P, 6 * 2 * HF])):
                dbg[nm] = nc.dram_tensor(nm, sh, F32, kind="ExternalOutput")
            for h in range(2):
                for nm, sh in ((f"dbg_x2a{fi}{h}", [P, 8 * HF]), (f"dbg_x2n{fi}{h}", [P, 8 * HF])):
                    dbg[nm] = nc.dram_tensor(nm, sh, F32, kind="ExternalOutput")
        dbg["dbg_x1fd"] = nc.dram_tensor("dbg_x1fd", [P, 8 * TOK], F32, kind="ExternalOutput")

    ds = lambda n, sh, dt: nc.dram_tensor(n, sh, dt)
    w_in_b = ds("w_in_b", [8, 8, P, P], BF16)
    w_sk_b = ds("w_sk_b", [8, 8, P, P], BF16)
    w_out_b = ds("w_out_b", [8, 16, P, P], BF16)
    w_f1_b = ds("w_f1_b", [NE, 8, 8, P, P], BF16)
    w_f2_b = ds("w_f2_b", [NE, 8, 8, P, P], BF16)
    x1b_d = ds("x1b_d", [8, P, TOK], BF16)
    rows_d = ds("rows_d", [4, 4096], F32)
    bc_d = ds("bc_d", [2, 16, L], BF16)
    cw_d = ds("cw_d", [NEXP, TOK], BF16)
    ln2_d = ds("ln2_d", [2, TOK], F32)

    sh_flags = nc.dram_tensor("sh_flags", [2, 64], U32, addr_space="Shared")
    sh_ibc = nc.dram_tensor("sh_ibc", [2, NSC, 112, SC], F32, addr_space="Shared")
    sh_y2 = nc.dram_tensor("sh_y2", [2, NSC, P, 8 * SC], BF16, addr_space="Shared")
    FL_HELLO, FL_IBC, FL_Y2 = 0, 1, 20

    ctx = ExitStack()
    sb = lambda n, sh, dt=F32: ctx.enter_context(nc.sbuf_tensor(n, sh, dt))

    s_vecs = sb("s_vecs", [P, 322])
    s_consts = sb("s_consts", [P, 8])
    for v, i in zip(CONSTS, range(len(CONSTS))):
        nc.const_aps.register(F32, v, s_consts[:, i:i + 1])
    s_idb = sb("s_idb", [P, P], BF16)
    s_id16 = sb("s_id16", [16, 16])
    s_A = sb("s_A", [P, 8, N])
    s_wibc = sb("s_wibc", [P, 8, P], BF16)
    s_wd2 = sb("s_wd2", [DR, EH], BF16)
    s_wrt = sb("s_wrt", [P, 8, P], BF16)
    s_b2 = sb("s_b2", [16, D], BF16)

    s_resid = sb("s_resid", [P, 1, 8, SC])
    s_sq = sb("s_sq", [P, 2, SC], BF16)
    s_onesb = sb("s_onesb", [P, 1], BF16)
    s_o128 = sb("s_o128", [P, P], BF16)
    s_lnt = sb("s_lnt", [P, SC])
    s_lnt2 = sb("s_lnt2", [P, SC])
    s_rows = sb("s_rows", [1, 64])
    s_rep = sb("s_rep", [P, 2, SC])
    s_xn = sb("s_xn", [P, 8, SC], BF16)
    s_xpre = sb("s_xpre", [P, 8, SC + 3], BF16)
    s_xtail = sb("s_xtail", [P, 8, 3], BF16)
    s_cacc = sb("s_cacc", [P, 2, SC])
    s_x = sb("s_x", [P, 1, 8, SC], BF16)
    s_sg = sb("s_sg", [P, 1, 8, SC], BF16)
    s_zd = sb("s_zd", [P, 1, 8, SC], BF16)
    _ci_bf = None
    s_dl = sb("s_dl", [P, 8, SC], BF16)
    s_w = sb("s_w", [P, 8, SC], BF16)
    s_ibc = sb("s_ibc", [112, SC])
    s_ibc2 = sb("s_ibc2", [112, SC])
    s_innb = sb("s_innb", [64, SC], BF16)
    s_bcbB = sb("s_bcbB", [16, SC], BF16)
    s_bcbC = sb("s_bcbC", [16, SC], BF16)
    s_Brep = sb("s_Brep", [P, 2, N, HSC], BF16)
    s_Crep = sb("s_Crep", [P, 2, N, HSC], BF16)
    s_a = sb("s_a", [P, 2, N, SEG])
    s_uh = sb("s_uh", [P, 3, N, SEG], BF16)
    s_hc = sb("s_hc", [P, 8, N])
    s_y = sb("s_y", [P, SC])
    s_g1 = sb("s_g1", [P, SC])
    s_y2 = sb("s_y2", [P, 1, 8, SC], BF16)
    s_y2p = sb("s_y2p", [P, 8, SC], BF16)
    s_x1p = sb("s_x1p", [P, 8, SC])
    s_x1b = sb("s_x1b", [P, 8, SC], BF16)
    s_lg = sb("s_lg", [NEXP, SC])
    s_lgt = sb("s_lgt", [P, 2, NEXP])
    s_sm = sb("s_sm", [P, 2, NEXP])
    s_wk = sb("s_wk", [P, 2, NEXP])
    s_mx = sb("s_mx", [P, 2, 4])
    s_msk = sb("s_msk", [P, 2, NEXP], U32)
    s_negs = sb("s_negs", [P, NEXP])
    s_cwt = sb("s_cwt", [P, 2, NEXP], BF16)
    s_cwf = sb("s_cwf", [NEXP, 2, P], BF16)
    s_rin = sb("s_rin", [P, 1, 8, P], BF16)
    s_rsk = sb("s_rsk", [P, 1, 8, P], BF16)
    s_rot = sb("s_rot", [P, 1, 16, P], BF16)
    s_rw1 = sb("s_rw1", [P, 2, 8, P], BF16)
    s_rw2 = sb("s_rw2", [P, 1, 8, P], BF16)
    s_x1s = sb("s_x1s", [P, 8, FS], BF16)
    s_hs = sb("s_hs", [P, 2, 8, HF], BF16)
    s_racc = sb("s_racc", [P, 2, 2, HF])
    s_cwr = sb("s_cwr", [P, FS], BF16)
    s_cwo = sb("s_cwo", [16, FS], BF16)
    s_x2p = sb("s_x2p", [P, 8, HF])

    s_r2 = sb("s_r2", [P, 2, HF])
    s_ci = sb("s_ci", [P, 2, 256])
    s_co = sb("s_co", [P, 2, 256], BF16)
    _cibf = s_ci.ap().rearrange("p a b -> p (a b)").bitcast(BF16)

    s_flag = sb("s_flag", [1, 4], U32)
    s_sp1t = sb("s_sp1t", [P, 8, SC], BF16)
    s_one = sb("s_one", [1, 68], U32)

    p_dn = ctx.enter_context(nc.psum_tensor("p_dn", [P, 6, 2, HF], F32))
    p_rot = ctx.enter_context(nc.psum_tensor("p_rot", [P, 2, FS], F32))

    class Sem:
        def __init__(self, name):
            self.h = ctx.enter_context(nc.semaphore(name))
            self.c = 0

    sm_w = Sem("sm_w")
    sm_ld = Sem("sm_ld")
    sm_ex = Sem("sm_ex")
    sm_st = Sem("sm_st")
    sm_po = Sem("sm_po")
    sm_pe = Sem("sm_pe")
    sm_act = Sem("sm_act")
    sm_dve = Sem("sm_dve")
    sm_pool = Sem("sm_pool")
    sm_bar = Sem("sm_bar")

    cmds = {e: [] for e in ("sync", "tensor", "vector", "scalar", "gpsimd")}

    def emit(eng, fn):
        cmds[eng].append(fn)

    def wt(eng, sem, val):
        emit(eng, lambda e, _s=sem.h, _v=val: e.wait_ge(_s, _v))

    def op(eng, make, sem=None, n=1):
        if sem is not None:
            sem.c += n

        def f(e, _mk=make, _sem=sem, _n=n):
            ins = _mk(e)
            if _sem is not None:
                ins.then_inc(_sem.h, _n)
        emit(eng, f)
        return sem.c if sem is not None else None

    def dma(eng, out_ap, in_ap, sem):
        return op(eng, lambda e, _o=out_ap, _i=in_ap: e.dma_start(_o, _i), sem, 16)

    ENGS = ("tensor", "vector", "scalar", "gpsimd", "sync")

    def barrier():
        tgt = sm_bar.c + len(ENGS)
        st_now = sm_st.c
        for e in ENGS:
            def f(eng, _t=tgt, _st=st_now):
                eng.wait_ge(sm_st.h, _st)
                eng.drain().then_inc(sm_bar.h, 1)
                eng.wait_ge(sm_bar.h, _t)
            emit(e, f)
        sm_bar.c = tgt

    R = {}

    def sync_setup(eng):
        for nm in ("me", "pn", "pc", "fl", "fl2", "ne"):
            R[nm] = eng.alloc_register("r_" + nm)
        pid = eng.partition_id()
        eng.reg_mov(R["me"], pid)
        eng.reg_mod(R["me"], R["me"], 2)
        eng.reg_mov(R["pn"], 1)
        eng.reg_sub(R["pn"], R["pn"], R["me"])
        eng.reg_mov(R["pc"], 0)
    emit("sync", sync_setup)

    def me_sw(e, fn0, fn1):
        with e.If_eq(R["me"], 0):
            fn0(e)
        with e.Else():
            fn1(e)

    def flag_write(eng, slot):
        me_sw(eng,
              lambda e: e.dma_start(sh_flags.ap()[0:1, slot:slot + 1], s_one[0:1, 0:1]).then_inc(sm_ex.h, 16),
              lambda e: e.dma_start(sh_flags.ap()[1:2, slot:slot + 1], s_one[0:1, 0:1]).then_inc(sm_ex.h, 16))

    def flag_poll(eng, slot):
        eng.reg_mov(R["ne"], 1)
        with eng.While(R["ne"]):
            eng.reg_add(R["pc"], R["pc"], 16)
            eng.dma_start(s_flag[0:1, 0:2],
                          bass.AP(sh_flags, slot, [[1, 1], [64, 2], [1, 1]]),
                          allow_slow_non_contiguous=True).then_inc(sm_po.h, 16)
            eng.wait_ge(sm_po.h, R["pc"])
            eng.reg_load(R["fl"], s_flag[0:1, 0:1])
            eng.reg_load(R["fl2"], s_flag[0:1, 1:2])
            eng.reg_mul(R["fl"], R["fl"], R["me"])
            eng.reg_mul(R["fl2"], R["fl2"], R["pn"])
            eng.reg_add(R["fl"], R["fl"], R["fl2"])
            eng.reg_sub(R["ne"], R["fl"], MAGIC)

    def ddump(name, src_ap, sem, mark):
        if not KDBG:
            return
        wt("sync", sem, mark)
        nr = src_ap.partition_size()
        nf = src_ap.free_size()
        t = dbg[name]
        if src_ap.dtype == BF16:
            dst = t.ap().bitcast(BF16)[0:nr, 0:nf]
        else:
            dst = t.ap()[0:nr, 0:nf]
        dma("sync", dst, src_ap, sm_st)

    rot_i = [0]
    rot_guard = [None, None]

    def rot_next():
        b = rot_i[0] % 2
        rot_i[0] += 1
        if rot_guard[b] is not None:
            gsem, gval = rot_guard[b]
            wt("tensor", gsem, gval)
        return b

    def rot_release(b, sem, val):
        rot_guard[b] = (sem, val)

    def prologue():
        dma("sync", s_vecs[:], t_vecs[:, :], sm_ld)
        dma("sync", s_consts[:], t_consts[:, :], sm_ld)
        dma("sync", s_id16[:], t_id16[:, :], sm_ld)
        small_jobs = [(t_idb[:, :], s_idb[:, :], P)]
        for a in range(8):
            small_jobs.append((t_wibc.ap()[a], s_wibc[:, a, :], P))
            small_jobs.append((t_wrt.ap()[a], s_wrt[:, a, :], P))
        for c0 in range(0, EH, 256):
            small_jobs.append((t_wd2[:, c0:c0 + 256], s_wd2[:, c0:c0 + 256], 64))
            small_jobs.append((t_b2[:, c0:c0 + 256], s_b2[:, c0:c0 + 256], 16))
        for (src, dst, npart) in small_jobs:
            cw = src.shape[-1]
            m = dma("sync", s_ci[0:npart, 0, 0:cw], src, sm_ld)
            wt("scalar", sm_ld, m)
            m = op("scalar", lambda e, _d=dst, _cw=cw, _np=npart:
                   e.activation(_d, s_ci[0:_np, 0, 0:_cw], AF.Copy), sm_act)
            wt("sync", sm_act, m)
        m = op("scalar", lambda e: e.activation(
            s_A.ap().rearrange("p a b -> p (a b)"), s_vecs[:, 64:192], AF.Exp), sm_act)
        wt("vector", sm_act, m)

        def fv(e):
            e.tensor_scalar_mul(s_A.ap().rearrange("p a b -> p (a b)"),
                                s_A.ap().rearrange("p a b -> p (a b)"), -1.0)
            e.memset(s_onesb[:], 1.0)
            e.memset(s_o128[:, :], 1.0)
            e.memset(s_one[0:1, 0:1], MAGIC)
            e.memset(s_one[0:1, 1:68], 0)
            e.memset(s_negs[:], -1.0)
            e.memset(s_xtail.ap().rearrange("p a b -> p (a b)"), 0.0)
            e.memset(s_hc.ap().rearrange("p a b -> p (a b)"), 0.0)
            return e.memset(s_rows[:], 0.0)
        m = op("vector", fv, sm_dve)
        wt("sync", sm_dve, m)

        ex0 = sm_ex.c

        if not os.environ.get("NOHELLO"):
            def f(e, _b=ex0):
                me_sw(e,
                      lambda ee: ee.dma_start(sh_flags.ap()[0:1, 0:64], s_one[0:1, 4:68]).then_inc(sm_ex.h, 16),
                      lambda ee: ee.dma_start(sh_flags.ap()[1:2, 0:64], s_one[0:1, 4:68]).then_inc(sm_ex.h, 16))
                e.wait_ge(sm_ex.h, _b + 16)
                flag_write(e, FL_HELLO)
                e.wait_ge(sm_ex.h, _b + 32)
                flag_poll(e, FL_HELLO)
            emit("sync", f)
            sm_ex.c += 32

        CH = 256
        store_marks = [0, 0]
        _jobs = [
            (t_win, w_in_b, 8 * 8 * P * P),
            (t_wsk, w_sk_b, 8 * 8 * P * P),
            (t_wout, w_out_b, 8 * 16 * P * P),
            (t_wf1, w_f1_b, NE * 8 * 8 * P * P),
            (t_wf2, w_f2_b, NE * 8 * 8 * P * P),
        ]
        if os.environ.get("NOCAST"):
            _jobs = _jobs[:2]
        for (src, dst, nel) in _jobs:
            nch = nel // (P * CH)
            for c in range(nch):
                par = c % 2
                if store_marks[par]:
                    wt("sync", sm_w, store_marks[par])
                m = dma("sync", s_ci[:, par, :],
                        bass.AP(src, c * P * CH, [[CH, P], [1, CH]]), sm_w)
                wt("scalar", sm_w, m)
                cm = op("scalar", lambda e, _p=par: e.activation(s_co[:, _p, :], s_ci[:, _p, :], AF.Copy), sm_act)
                wt("sync", sm_act, cm)
                store_marks[par] = dma("sync", bass.AP(dst, c * P * CH, [[CH, P], [1, CH]]),
                                       s_co[:, par, :], sm_w)
        wt("sync", sm_w, sm_w.c)
        barrier()

    def phaseA(s, acut=99):
        slot3 = 0
        par = 0
        rm = dma("sync", s_resid[:, slot3, :, :],
                 t_res.ap().rearrange("(dc p) t -> p dc t", p=P)[:, :, s * SC:(s + 1) * SC], sm_ld)
        wt("vector", sm_ld, rm)

        bk = rot_next()
        st_marks = {}
        for dc in range(8):
            if dc >= 2:
                wt("vector", sm_pe, st_marks[dc - 2])
            mq = op("vector", lambda e, _sl=slot3, _dc=dc:
                    e.tensor_tensor(s_sq[:, _dc % 2, :], s_resid[:, _sl, _dc, :],
                                    s_resid[:, _sl, _dc, :], ALU.mult), sm_dve)
            wt("tensor", sm_dve, mq)
            st_marks[dc] = op("tensor", lambda e, _dc=dc, _bk=bk:
               e.matmul(p_rot[:, _bk, 0:SC], s_o128[:, :], s_sq[:, _dc % 2, :],
                        start=(_dc == 0), stop=(_dc == 7)), sm_pe)
        wt("scalar", sm_pe, sm_pe.c)

        def fa(e, _bk=bk):
            return e.activation(s_lnt[:, :], p_rot[:, _bk, 0:SC], AF.Ln,
                                scale=1.0 / 1024.0, bias=1e-5)
        m = op("scalar", fa, sm_act)
        rot_release(bk, sm_act, m)
        wt("scalar", sm_act, m)
        m = op("scalar", lambda e: e.activation(s_rep[:, 0, :], s_lnt[:, :], AF.Exp, scale=-0.5), sm_act)
        if acut <= 1:
            return
        wt("vector", sm_act, m)
        for dc in range(8):
            op("vector", lambda e, _sl=slot3, _dc=dc:
               e.scalar_tensor_tensor(s_xn[:, _dc, :], s_resid[:, _sl, _dc, :],
                                      s_vecs[:, 56 + _dc:57 + _dc], s_rep[:, 0, :],
                                      ALU.mult, ALU.mult), sm_dve)
        xn_done = sm_dve.c
        if acut <= 2:
            return

        evac_marks = {}
        for et in range(8):
            if et >= 1:
                wt("sync", sm_act, evac_marks[et - 1])
            dma("sync", s_rin[:, 0, :, :],
                w_in_b.ap()[et].rearrange("dc p m -> p dc m"), sm_ld)
            m = dma("sync", s_rsk[:, 0, :, :],
                    w_sk_b.ap()[et].rearrange("dc p m -> p dc m"), sm_ld)
            bki = rot_next()
            bks = rot_next()
            wt("tensor", sm_ld, m)
            if et == 0:
                wt("tensor", sm_dve, xn_done)

            _plvl = int(os.environ.get("PLVL", "9"))

            def ft(e, _et=et, _bi=bki, _bs=bks):
                last = None
                for dc in range(8):
                    last = e.matmul(p_rot[:, _bi, 0:SC], s_rin[:, 0, dc, :], s_xn[:, dc, :],
                                    start=(dc == 0), stop=(dc == 7))
                return last
            m1 = op("tensor", ft, sm_pe)

            def ft2(e, _et=et, _bs=bks):
                last = None
                for dc in range(8):
                    last = e.matmul(p_rot[:, _bs, 0:SC], s_rsk[:, 0, dc, :], s_xn[:, dc, :],
                                    start=(dc == 0), stop=(dc == 7))
                return last
            m2 = op("tensor", ft2, sm_pe)
            wt("scalar", sm_pe, m2)
            mi = op("scalar", lambda e, _et=et, _bi=bki:
                    e.activation(s_xpre[:, _et, 3:3 + SC], p_rot[:, _bi, 0:SC], AF.Copy), sm_act)
            rot_release(bki, sm_act, mi)
            em = op("scalar", lambda e, _et=et, _bs=bks, _par=par:
                    e.activation(s_sg[:, _par, _et, :], p_rot[:, _bs, 0:SC], AF.Silu), sm_act)
            rot_release(bks, sm_act, em)
            evac_marks[et] = em
        if acut <= 3:
            return
        wt("vector", sm_act, sm_act.c)
        m = op("vector", lambda e: e.tensor_copy(s_xpre[:, :, 0:3], s_xtail[:]), sm_dve)
        wt("vector", sm_dve, m)

        silu_marks = {}
        for et in range(8):
            cp = et % 2
            if et >= 1:
                wt("vector", sm_act, silu_marks[et - 1])
            base = sm_dve.c
            op("vector", lambda e, _et=et, _cp=cp:
               e.tensor_scalar(s_cacc[:, _cp, :], s_xpre[:, _et, 0:SC],
                               s_vecs[:, 24 + _et * 4:25 + _et * 4],
                               s_vecs[:, 0 + _et:1 + _et], ALU.mult, ALU.add), sm_dve)
            for k in (1, 2, 3):
                wt("vector", sm_dve, base + k)
                dstp = cp if k % 2 == 0 else 1 - cp
                srcp = 1 - dstp
                op("vector", lambda e, _et=et, _k=k, _d=dstp, _s=srcp:
                   e.scalar_tensor_tensor(s_cacc[:, _d, :], s_xpre[:, _et, _k:_k + SC],
                                          s_vecs[:, 24 + _et * 4 + _k:25 + _et * 4 + _k],
                                          s_cacc[:, _s, :], ALU.mult, ALU.add), sm_dve)
            cv = sm_dve.c
            wt("scalar", sm_dve, cv)
            silu_marks[et] = op("scalar", lambda e, _et=et, _par=par:
               e.activation(s_x[:, _par, _et, :], s_cacc[:, 1 - (_et % 2), :], AF.Silu), sm_act)
        x_done = sm_act.c
        wt("vector", sm_act, x_done)
        op("vector", lambda e: e.tensor_copy(s_xtail[:], s_xpre[:, :, SC:SC + 3]), sm_dve)

        if acut <= 4:
            return
        wt("tensor", sm_act, x_done)
        bk2 = rot_next()

        def ft(e, _bk=bk2, _par=par):
            last = None
            for et in range(8):
                last = e.matmul(p_rot[:, _bk, 0:SC], s_wibc[:, et, :], s_x[:, _par, et, :],
                                start=(et == 0), stop=(et == 7))
            return last
        m = op("tensor", ft, sm_pe)
        wt("scalar", sm_pe, m)
        for (mm_, off) in ((DR, 0), (N, 64), (N, 96)):
            mev = op("scalar", lambda e, _mm=mm_, _off=off, _bk=bk2:
               e.activation(s_ibc[_off:_off + _mm, :], p_rot[_off:_off + _mm, _bk, 0:SC], AF.Copy), sm_act)
        rot_release(bk2, sm_act, mev)
        ibc_evac = sm_act.c

        if acut <= 5:
            return
        wt("sync", sm_act, ibc_evac)
        ex0 = sm_ex.c

        def f(e, _s=s, _b=ex0):
            me_sw(e,
                  lambda ee: ee.dma_start(sh_ibc.ap()[0, _s], s_ibc[:, :]).then_inc(sm_ex.h, 16),
                  lambda ee: ee.dma_start(sh_ibc.ap()[1, _s], s_ibc[:, :]).then_inc(sm_ex.h, 16))
            e.wait_ge(sm_ex.h, _b + 16)
            flag_write(e, FL_IBC + _s)
            flag_poll(e, FL_IBC + _s)
            me_sw(e,
                  lambda ee: ee.dma_start(s_ibc2[:, :], sh_ibc.ap()[1, _s]).then_inc(sm_ex.h, 16),
                  lambda ee: ee.dma_start(s_ibc2[:, :], sh_ibc.ap()[0, _s]).then_inc(sm_ex.h, 16))
        emit("sync", f)
        sm_ex.c += 48
        wt("vector", sm_ex, sm_ex.c)
        m = op("vector", lambda e: e.tensor_add(s_ibc[:, :], s_ibc[:, :], s_ibc2[:, :]), sm_dve)
        wt("scalar", sm_dve, m)
        op("scalar", lambda e: e.activation(s_bcbB[:, :], s_ibc[64:80, :], AF.Copy), sm_act)
        op("scalar", lambda e: e.activation(s_bcbC[:, :], s_ibc[96:112, :], AF.Copy), sm_act)
        m = op("scalar", lambda e: e.activation(s_innb[:, :], s_ibc[0:64, :], AF.Copy), sm_act)
        bc_cast = m

        if acut <= 6:
            return
        wt("sync", sm_act, bc_cast)
        dma("sync", bc_d.ap()[0, :, s * SC:(s + 1) * SC], s_bcbB[:, :], sm_ld)
        m = dma("sync", bc_d.ap()[1, :, s * SC:(s + 1) * SC], s_bcbC[:, :], sm_ld)
        wt("sync", sm_ld, m)
        for h in range(2):
            dma("sync", s_Brep[:, h, :, :],
                bc_d.ap()[0, :, s * SC + h * HSC:s * SC + (h + 1) * HSC]
                .unsqueeze(0).broadcast_to([P, N, HSC]), sm_ld)
            dma("sync", s_Crep[:, h, :, :],
                bc_d.ap()[1, :, s * SC + h * HSC:s * SC + (h + 1) * HSC]
                .unsqueeze(0).broadcast_to([P, N, HSC]), sm_ld)
        bc_ready = sm_ld.c

        if acut <= 7:
            return
        wt("tensor", sm_act, bc_cast)
        for et in range(8):
            bb = rot_next()
            m = op("tensor", lambda e, _et=et, _bb=bb:
                   e.matmul(p_rot[:, _bb, 0:SC], s_wd2[:, _et * P:(_et + 1) * P],
                            s_innb[:, :], start=True, stop=True), sm_pe)
            wt("scalar", sm_pe, m)
            em = op("scalar", lambda e, _et=et, _par=par, _bb=bb:
                    e.activation(s_zd[:, _par, _et, :], p_rot[:, _bb, 0:SC], AF.Copy), sm_act)
            rot_release(bb, sm_act, em)
        zd_done = sm_act.c

        if acut <= 8:
            return
        wt("scalar", sm_act, zd_done)
        for et in range(8):
            op("scalar", lambda e, _et=et, _par=par:
               e.activation(s_sp1t[:, _et, :], s_zd[:, _par, _et, :], AF.Abs,
                            bias=s_vecs[:, 16 + _et:17 + _et], scale=1.0), sm_act)
        m = sm_act.c
        wt("scalar", sm_act, m)
        m = op("scalar", lambda e: e.activation(
            s_sp1t.ap().rearrange("p a b -> p (a b)"),
            s_sp1t.ap().rearrange("p a b -> p (a b)"), AF.Exp, scale=-1.0), sm_act)
        wt("scalar", sm_act, m)
        m = op("scalar", lambda e: e.activation(
            s_sp1t.ap().rearrange("p a b -> p (a b)"),
            s_sp1t.ap().rearrange("p a b -> p (a b)"), AF.Ln, bias=1.0), sm_act)
        sp_done = m
        wt("vector", sm_act, sp_done)
        for et in range(8):
            op("vector", lambda e, _et=et, _par=par:
               e.tensor_scalar(s_dl[:, _et, :], s_zd[:, _par, _et, :],
                               s_vecs[:, 16 + _et:17 + _et], 0.0, ALU.add, ALU.max), sm_dve)
        m = sm_dve.c
        wt("vector", sm_dve, m)
        m = op("vector", lambda e: e.tensor_add(
            s_dl.ap().rearrange("p a b -> p (a b)"),
            s_dl.ap().rearrange("p a b -> p (a b)"),
            s_sp1t.ap().rearrange("p a b -> p (a b)")), sm_dve)
        wt("vector", sm_dve, m)
        op("vector", lambda e, _par=par: e.tensor_tensor(
            s_w.ap().rearrange("p a b -> p (a b)"),
            s_dl.ap().rearrange("p a b -> p (a b)"),
            s_x.ap()[:, _par, :, :].rearrange("p a b -> p (a b)"), ALU.mult), sm_dve)
        wt("vector", sm_ld, bc_ready)
        if KDBG and s == 0:
            ddump("dbg_xn", s_xn.ap().rearrange("p a t -> p (a t)"), sm_dve, sm_dve.c)
            ddump("dbg_x", s_x.ap()[:, 0, :, :].rearrange("p a t -> p (a t)"), sm_dve, sm_dve.c)
            ddump("dbg_dl", s_dl.ap().rearrange("p a t -> p (a t)"), sm_dve, sm_dve.c)
            ddump("dbg_ibc", s_ibc.ap(), sm_dve, sm_dve.c)
            ddump("dbg_zd", s_zd.ap()[:, 0, :, :].rearrange("p a t -> p (a t)"), sm_dve, sm_dve.c)
            ddump("dbg_sg", s_sg.ap()[:, 0, :, :].rearrange("p a t -> p (a t)"), sm_act, sm_act.c)
        barrier()

    def phaseS(s):
        par = 0
        tree_marks = {}
        scan_marks = {}
        for et in range(8):
            i = s * 8 + et
            ap_ = et % 2
            u_sl = (2 * i) % 3
            h_sl = (2 * i + 1) % 3
            if et >= 2:
                wt("scalar", sm_dve, scan_marks[et - 2])

            def fa(e, _et=et, _ap=ap_):
                e.memzero(s_a[:, _ap, :, 0:1])
                last = None
                for n in range(N):
                    last = e.activation(s_a[:, _ap, n, 1:SEG], s_dl[:, _et, :], AF.Exp,
                                        scale=s_A[:, _et, n:n + 1])
                return last
            a_done = op("scalar", fa, sm_act)

            if et >= 2:
                wt("gpsimd", sm_dve, tree_marks[et - 2])

            def fp(e, _et=et, _u=u_sl, _par=par):
                e.tensor_copy(s_uh[:, _u, :, 0:1], s_hc.ap()[:, _et, :].unsqueeze(2))
                last = None
                for h in range(2):
                    last = e.tensor_tensor(
                        s_uh[:, _u, :, 1 + h * HSC:1 + (h + 1) * HSC],
                        s_w.ap()[:, _et, h * HSC:(h + 1) * HSC].unsqueeze(1).broadcast_to([P, N, HSC]),
                        s_Brep[:, h, :, :], ALU.mult)
                return last
            u_done = op("gpsimd", fp, sm_pool)

            wt("vector", sm_act, a_done)
            wt("vector", sm_pool, u_done)
            m = op("vector", lambda e, _ap=ap_, _u=u_sl, _h=h_sl:
                   e.tensor_tensor_scan(
                       s_uh.ap()[:, _h, :, :].rearrange("p n t -> p (n t)"),
                       s_a.ap()[:, _ap, :, :].rearrange("p n t -> p (n t)"),
                       s_uh.ap()[:, _u, :, :].rearrange("p n t -> p (n t)"),
                       0.0, ALU.mult, ALU.add), sm_dve)
            wt("vector", sm_dve, m)
            m = op("vector", lambda e, _et=et, _h=h_sl:
                   e.tensor_copy(s_hc.ap()[:, _et, :].unsqueeze(2),
                                 s_uh[:, _h, :, SEG - 1:SEG]), sm_dve)
            scan_marks[et] = m

            if KDBG and s == 0 and et == 0:
                ddump("dbg_a", s_a.ap()[:, ap_, :, :].rearrange("p n t -> p (n t)"), sm_act, a_done)
                ddump("dbg_u", s_uh.ap()[:, u_sl, :, :].rearrange("p n t -> p (n t)"), sm_pool, u_done)
                ddump("dbg_h", s_uh.ap()[:, h_sl, :, :].rearrange("p n t -> p (n t)"), sm_dve, m)
            wt("gpsimd", sm_dve, m)

            def fp2(e, _h=h_sl):
                last = None
                for h in range(2):
                    last = e.tensor_tensor(
                        s_uh[:, _h, :, 1 + h * HSC:1 + (h + 1) * HSC],
                        s_uh[:, _h, :, 1 + h * HSC:1 + (h + 1) * HSC],
                        s_Crep[:, h, :, :], ALU.mult)
                return last
            m_done = op("gpsimd", fp2, sm_pool)

            wt("vector", sm_pool, m_done)
            base = sm_dve.c

            def fv1(e, _h=h_sl):
                hb = s_uh[:, _h]
                return e.tensor_add(hb[:, 0:8, :], hb[:, 0:8, :], hb[:, 8:16, :])
            op("vector", fv1, sm_dve)
            wt("vector", sm_dve, base + 1)
            op("vector", lambda e, _h=h_sl: e.tensor_add(
                s_uh[:, _h][:, 0:4, :], s_uh[:, _h][:, 0:4, :], s_uh[:, _h][:, 4:8, :]), sm_dve)
            wt("vector", sm_dve, base + 2)
            op("vector", lambda e, _h=h_sl: e.tensor_add(
                s_uh[:, _h][:, 0:2, :], s_uh[:, _h][:, 0:2, :], s_uh[:, _h][:, 2:4, :]), sm_dve)
            wt("vector", sm_dve, base + 3)
            op("vector", lambda e, _h=h_sl: e.tensor_tensor(
                s_y[:, :], s_uh[:, _h][:, 0, 1:SEG], s_uh[:, _h][:, 1, 1:SEG], ALU.add), sm_dve)
            wt("vector", sm_dve, base + 4)
            op("vector", lambda e, _et=et, _par=par: e.scalar_tensor_tensor(
                s_g1[:], s_x[:, _par, _et, :], s_vecs[:, 8 + _et:9 + _et],
                s_y[:], ALU.mult, ALU.add), sm_dve)
            wt("vector", sm_dve, base + 5)
            m = op("vector", lambda e, _et=et, _par=par: e.tensor_tensor(
                s_y2[:, _par, _et, :], s_g1[:], s_sg[:, _par, _et, :], ALU.mult), sm_dve)
            tree_marks[et] = m
        barrier()

    def phaseO(s):
        par = 0
        ex0 = sm_ex.c

        def f(e, _s=s, _par=par, _b=ex0):
            src = s_y2.ap()[:, _par, :, :].rearrange("p a t -> p (a t)")
            me_sw(e,
                  lambda ee: ee.dma_start(sh_y2.ap()[0, _s], src).then_inc(sm_ex.h, 16),
                  lambda ee: ee.dma_start(sh_y2.ap()[1, _s], src).then_inc(sm_ex.h, 16))
            e.wait_ge(sm_ex.h, _b + 16)
            flag_write(e, FL_Y2 + _s)
            flag_poll(e, FL_Y2 + _s)
            dst = s_y2p.ap().rearrange("p a t -> p (a t)")
            me_sw(e,
                  lambda ee: ee.dma_start(dst, sh_y2.ap()[1, _s]).then_inc(sm_ex.h, 16),
                  lambda ee: ee.dma_start(dst, sh_y2.ap()[0, _s]).then_inc(sm_ex.h, 16))
        emit("sync", f)
        sm_ex.c += 48
        y2x = sm_ex.c
        if KDBG and s == 0:
            ddump("dbg_y2p", s_y2p.ap().rearrange("p a t -> p (a t)"), sm_ex, y2x)
            dwo = dbg["dbg_wo"].ap().bitcast(BF16)
            for ec in range(16):
                dma("sync", dwo[:, ec * P:(ec + 1) * P], w_out_b.ap()[0, ec], sm_st)

        wt("tensor", sm_ex, y2x)
        evac_marks = {}
        for dt in range(8):
            if dt >= 1:
                wt("sync", sm_dve, evac_marks[dt - 1])
            m = dma("sync", s_rot[:, 0, :, :],
                    w_out_b.ap()[dt].rearrange("ec p m -> p ec m"), sm_ld)
            bk = rot_next()
            wt("tensor", sm_ld, m)

            def ft(e, _dt=dt, _bk=bk, _par=par):
                last = None
                for ec in range(16):
                    rhs = s_y2[:, _par, ec, :] if ec < 8 else s_y2p[:, ec - 8, :]
                    last = e.matmul(p_rot[:, _bk, 0:SC], s_rot[:, 0, ec, :], rhs,
                                    start=(ec == 0), stop=(ec == 15))
                return last
            m = op("tensor", ft, sm_pe)
            wt("vector", sm_pe, m)
            em = op("vector", lambda e, _dt=dt, _bk=bk, _sl=0:
                    e.scalar_tensor_tensor(s_x1p[:, _dt, :], s_resid[:, _sl, _dt, :], 2.0,
                                           p_rot[:, _bk, 0:SC], ALU.mult, ALU.add), sm_dve)
            evac_marks[dt] = em
            rot_release(bk, sm_dve, em)

        wt("scalar", sm_dve, evac_marks[7])
        mcp = op("scalar", lambda e: e.activation(
            s_xn.ap().rearrange("p a t -> p (a t)"),
            s_x1p.ap().rearrange("p a t -> p (a t)"), AF.Copy), sm_act)
        wt("tensor", sm_act, mcp)
        if KDBG and s == 0:
            ddump("dbg_x1pre", s_xn.ap().rearrange("p a t -> p (a t)"), sm_act, mcp)
        bk = rot_next()
        bk2 = rot_next()
        st_marks = {}
        for dc in range(8):
            if dc >= 2:
                wt("vector", sm_pe, st_marks[dc - 2])
            mq = op("vector", lambda e, _dc=dc: e.tensor_tensor(
                s_sq[:, _dc % 2, :], s_x1p[:, _dc, :], s_x1p[:, _dc, :], ALU.mult), sm_dve)
            wt("tensor", sm_dve, mq)

            def ft(e, _dc=dc, _b1=bk, _b2=bk2):
                e.matmul(p_rot[:, _b1, 0:SC], s_o128[:, :], s_xn[:, _dc, :],
                         start=(_dc == 0), stop=(_dc == 7))
                return e.matmul(p_rot[:, _b2, 0:SC], s_o128[:, :], s_sq[:, _dc % 2, :],
                                start=(_dc == 0), stop=(_dc == 7))
            st_marks[dc] = op("tensor", ft, sm_pe)
        st_done = sm_pe.c
        wt("vector", sm_pe, st_done)
        base = sm_dve.c
        op("vector", lambda e, _b=bk: e.tensor_scalar_mul(s_rep[:, 0, :], p_rot[:, _b, 0:SC], 1.0 / 1024.0), sm_dve)
        op("vector", lambda e, _b=bk2: e.tensor_scalar_mul(s_lnt[:, :], p_rot[:, _b, 0:SC], 1.0 / 1024.0), sm_dve)
        rot_release(bk, sm_dve, base + 1)
        rot_release(bk2, sm_dve, base + 2)
        wt("vector", sm_dve, base + 2)
        op("vector", lambda e: e.tensor_tensor(s_lnt2[:, :], s_rep[:, 0, :], s_rep[:, 0, :], ALU.mult), sm_dve)
        wt("vector", sm_dve, base + 3)
        m = op("vector", lambda e: e.tensor_sub(s_lnt[:, :], s_lnt[:, :], s_lnt2[:, :]), sm_dve)
        wt("scalar", sm_dve, m)
        m = op("scalar", lambda e: e.activation(s_lnt[:, :], s_lnt[:, :], AF.Ln, bias=1e-5), sm_act)
        wt("scalar", sm_act, m)
        m = op("scalar", lambda e: e.activation(s_lnt[:, :], s_lnt[:, :], AF.Exp, scale=-0.5), sm_act)
        wt("vector", sm_act, m)
        base2 = sm_dve.c
        op("vector", lambda e: e.tensor_tensor(s_rep[:, 1, :], s_rep[:, 0, :], s_lnt[:, :], ALU.mult), sm_dve)
        wt("vector", sm_dve, base2 + 1)
        m = op("vector", lambda e: e.tensor_copy(s_rep[:, 0, :], s_lnt[:, :]), sm_dve)
        wt("vector", sm_dve, base2 + 2)
        base = sm_dve.c
        op("vector", lambda e: e.tensor_tensor(
            s_x1p.ap(), s_x1p.ap(),
            s_rep.ap()[:, 0, :].unsqueeze(1).broadcast_to([P, 8, SC]),
            ALU.mult), sm_dve)
        wt("vector", sm_dve, base + 1)
        m = op("vector", lambda e: e.tensor_tensor(
            s_x1p.ap(), s_x1p.ap(),
            s_rep.ap()[:, 1, :].unsqueeze(1).broadcast_to([P, 8, SC]),
            ALU.subtract), sm_dve)
        wt("scalar", sm_dve, m)
        x1b_done = op("scalar", lambda e: e.activation(
            s_x1b.ap().rearrange("p a t -> p (a t)"),
            s_x1p.ap().rearrange("p a t -> p (a t)"), AF.Copy), sm_act)

        wt("sync", sm_act, x1b_done)

        def fst(e, _s=s):
            d1 = x1b_d.ap()[:, :, _s * HSC:(_s + 1) * HSC].rearrange("a p t -> p a t")
            me_sw(e,
                  lambda ee: ee.dma_start(d1, s_x1b[:, :, 0:HSC]).then_inc(sm_st.h, 16),
                  lambda ee: ee.dma_start(d1, s_x1b[:, :, HSC:SC]).then_inc(sm_st.h, 16))
        emit("sync", fst)
        sm_st.c += 16

        bk = rot_next()
        wt("tensor", sm_act, x1b_done)

        def ft(e, _bk=bk):
            last = None
            for dc in range(8):
                last = e.matmul(p_rot[:, _bk, 0:SC], s_wrt[:, dc, :], s_x1b[:, dc, :],
                                start=(dc == 0), stop=(dc == 7))
            return last
        m = op("tensor", ft, sm_pe)
        wt("scalar", sm_pe, m)
        m = op("scalar", lambda e, _bk=bk: e.activation(
            s_lg[:, :], p_rot[0:NEXP, _bk, 0:SC], AF.Copy), sm_act)
        rot_release(bk, sm_act, m)
        wt("vector", sm_act, m)
        m = op("vector", lambda e: e.tensor_scalar_add(
            s_lg[:, :], s_lg[:, :], s_vecs[0:NEXP, 192:193]), sm_dve)
        bk = rot_next()
        wt("tensor", sm_dve, m)

        def ft2(e, _bk=bk):
            last = None
            for blk in range(2):
                last = e.transpose(p_rot[0:P, _bk, blk * NEXP:(blk + 1) * NEXP],
                                   s_lg[:, blk * P:(blk + 1) * P], s_id16[0:NEXP, 0:NEXP])
            return last
        m = op("tensor", ft2, sm_pe)
        wt("scalar", sm_pe, m)
        m = op("scalar", lambda e, _bk=bk: e.activation(
            s_lgt.ap().rearrange("p a b -> p (a b)"),
            p_rot[:, _bk, 0:2 * NEXP], AF.Copy), sm_act)
        rot_release(bk, sm_act, m)
        lgt_done = m

        for blk in range(2):
            wt("vector", sm_act, lgt_done)
            m = op("vector", lambda e, _b=blk: e.tensor_reduce(
                s_mx[:, _b, 0:1], s_lgt[:, _b, :], AX.X, ALU.max, negate=True), sm_dve)
            wt("scalar", sm_dve, m)
            m = op("scalar", lambda e, _b=blk: e.activation(
                s_sm[:, _b, :], s_lgt[:, _b, :], AF.Exp, bias=s_mx[:, _b, 0:1]), sm_act)
            wt("vector", sm_act, m)
            m = op("vector", lambda e, _b=blk: e.tensor_reduce(
                s_mx[:, _b, 1:2], s_sm[:, _b, :], AX.X, ALU.add), sm_dve)
            wt("scalar", sm_dve, m)
            m = op("scalar", lambda e, _b=blk: e.activation(
                s_mx[:, _b, 2:3], s_mx[:, _b, 1:2], AF.Ln), sm_act)
            wt("scalar", sm_act, m)
            m = op("scalar", lambda e, _b=blk: e.activation(
                s_mx[:, _b, 2:3], s_mx[:, _b, 2:3], AF.Exp, scale=-1.0), sm_act)
            wt("vector", sm_act, m)
            base = sm_dve.c
            op("vector", lambda e, _b=blk: e.tensor_scalar_mul(
                s_sm[:, _b, :], s_sm[:, _b, :], s_mx[:, _b, 2:3]), sm_dve)
            wt("vector", sm_dve, base + 1)
            op("vector", lambda e, _b=blk: e.tensor_copy(s_wk[:, _b, :], s_sm[:, _b, :]), sm_dve)
            wt("vector", sm_dve, base + 2)
            for it in range(3):
                op("vector", lambda e, _b=blk: e.tensor_reduce(
                    s_mx[:, _b, 3:4], s_wk[:, _b, :], AX.X, ALU.max), sm_dve)
                wt("vector", sm_dve, sm_dve.c)
                if it < 2:
                    op("vector", lambda e, _b=blk: e.tensor_single_scalar(
                        s_msk[:, _b, :], s_wk[:, _b, :], s_mx[:, _b, 3:4], ALU.is_ge), sm_dve)
                    wt("vector", sm_dve, sm_dve.c)
                    op("vector", lambda e, _b=blk: e.copy_predicated(
                        s_wk[:, _b, :], s_msk[:, _b, :], s_negs[:, :]), sm_dve)
                    wt("vector", sm_dve, sm_dve.c)
            op("vector", lambda e, _b=blk: e.tensor_single_scalar(
                s_msk[:, _b, :], s_sm[:, _b, :], s_mx[:, _b, 3:4], ALU.is_ge), sm_dve)
            wt("vector", sm_dve, sm_dve.c)
            op("vector", lambda e, _b=blk: e.memset(s_wk[:, _b, :], 0.0), sm_dve)
            wt("vector", sm_dve, sm_dve.c)
            op("vector", lambda e, _b=blk: e.copy_predicated(
                s_wk[:, _b, :], s_msk[:, _b, :], s_sm[:, _b, :]), sm_dve)
            wt("vector", sm_dve, sm_dve.c)
            op("vector", lambda e, _b=blk: e.tensor_copy(s_cwt[:, _b, :], s_wk[:, _b, :]), sm_dve)
        cw_done = sm_dve.c

        bk = rot_next()
        wt("tensor", sm_dve, cw_done)
        pbf = p_rot.ap().bitcast(BF16)

        def ft3(e, _bk=bk, _v=pbf):
            last = None
            for blk in range(2):
                last = e.transpose(_v[0:NEXP, _bk, blk * P:(blk + 1) * P], s_cwt[:, blk, :], s_idb[:, :])
            return last
        m = op("tensor", ft3, sm_pe)
        wt("scalar", sm_pe, m)

        def fa(e, _bk=bk, _v=pbf):
            last = None
            for blk in range(2):
                last = e.activation(s_cwf[:, blk, :], _v[0:NEXP, _bk, blk * P:(blk + 1) * P], AF.Copy)
            return last
        m = op("scalar", fa, sm_act)
        rot_release(bk, sm_act, m)
        wt("sync", sm_act, m)

        if KDBG and s == 0:
            ddump("dbg_y2", s_y2.ap()[:, 0, :, :].rearrange("p a t -> p (a t)"), sm_dve, sm_dve.c)
            ddump("dbg_x1", s_x1p.ap().rearrange("p a t -> p (a t)"), sm_dve, sm_dve.c)
            ddump("dbg_cw", s_cwf.ap().rearrange("a b c -> a (b c)"), sm_act, sm_act.c)

        def fcw2(e, _s=s):
            d = cw_d.ap()[:, _s * HSC:(_s + 1) * HSC]
            me_sw(e,
                  lambda ee: ee.dma_start(d, s_cwf[:, 0, :]).then_inc(sm_st.h, 16),
                  lambda ee: ee.dma_start(d, s_cwf[:, 1, :]).then_inc(sm_st.h, 16))
        emit("sync", fcw2)
        sm_st.c += 16
        barrier()

    def phaseF(fi):
        x1s_ld = dma("sync", s_x1s.ap(),
                     x1b_d.ap()[:, :, fi * FS:(fi + 1) * FS].rearrange("a p t -> p a t"), sm_ld)
        mz = op("vector", lambda e: e.memset(s_cwo[0:16, :], 1.0), sm_dve)
        wt("sync", sm_dve, mz)
        m = dma("sync", s_cwo[0:NEXP, :], cw_d.ap()[:, fi * FS:(fi + 1) * FS], sm_ld)
        wt("vector", sm_ld, m)
        cwo_done = op("vector", lambda e: e.memset(
            s_racc.ap().rearrange("p a b t -> p (a b t)"), 0.0), sm_dve)

        wt("tensor", sm_dve, cwo_done)
        wt("tensor", sm_ld, m)
        wt("tensor", sm_ld, x1s_ld)
        if KDBG:
            ddump(f"dbg_cwo{fi}", s_cwo.ap()[0:16, :], sm_ld, m)
            ddump(f"dbg_fx1{fi}", s_x1s.ap().rearrange("p a t -> p (a t)"), sm_ld, x1s_ld)

        def ft(e):
            last = None
            for dt in range(6):
                for h in range(2):
                    last = e.matmul(p_dn[:, dt, h, :], s_b2[:, dt * P:(dt + 1) * P],
                                    s_cwo[:, h * HF:(h + 1) * HF], start=(h == 0), stop=False)
            return last
        op("tensor", ft, sm_pe)
        for dt in (6, 7):
            bk = rot_next()
            m = op("tensor", lambda e, _dt=dt, _bk=bk:
                   [e.matmul(p_rot[:, _bk, h * HF:(h + 1) * HF], s_b2[:, _dt * P:(_dt + 1) * P],
                             s_cwo[:, h * HF:(h + 1) * HF], start=(h == 0), stop=(h == 1)) for h in range(2)][-1], sm_pe)
            wt("scalar", sm_pe, m)
            m = op("scalar", lambda e, _dt=dt, _bk=bk:
                   [e.activation(s_racc[:, _dt - 6, h, :], p_rot[:, _bk, h * HF:(h + 1) * HF], AF.Copy)
                    for h in range(2)][-1], sm_act)
            rot_release(bk, sm_act, m)

        w1_marks = {}
        w2_marks = {}
        for ex in range(NE):
            cwr_ld = None
            if ex < NEXP:
                cwr_ld = dma("sync", s_cwr[:, :],
                             bass.AP(cw_d, ex * TOK + fi * FS, [[0, P], [1, 1], [1, FS]]), sm_ld)
            for ht in range(8):
                gidx = ex * 8 + ht
                if gidx >= 2:
                    wt("sync", sm_pe, w1_marks[gidx - 2])
                m = dma("sync", s_rw1[:, gidx % 2, :, :],
                        w_f1_b.ap()[ex, ht].rearrange("dc p m -> p dc m"), sm_ld)
                bk = rot_next()
                wt("tensor", sm_ld, m)

                def ft(e, _ht=ht, _bk=bk, _sl=gidx % 2):
                    last = None
                    for dc in range(8):
                        last = e.matmul(p_rot[:, _bk, :], s_rw1[:, _sl, dc, :], s_x1s[:, dc, :],
                                        start=(dc == 0), stop=(dc == 7))
                    return last
                mmk = op("tensor", ft, sm_pe)
                w1_marks[gidx] = mmk
                wt("scalar", sm_pe, mmk)
                m = op("scalar", lambda e, _e=ex, _ht=ht, _bk=bk:
                       [e.activation(s_hs[:, h, _ht, :], p_rot[:, _bk, h * HF:(h + 1) * HF],
                                     AF.Relu, bias=s_vecs[:, 193 + _ht * 16 + _e:194 + _ht * 16 + _e])
                        for h in range(2)][-1], sm_act)
                rot_release(bk, sm_act, m)
                if ex < NEXP:
                    wt("gpsimd", sm_act, m)
                    if ht == 0:
                        wt("gpsimd", sm_ld, cwr_ld)
                    op("gpsimd", lambda e, _ht=ht:
                       [e.tensor_tensor(s_hs[:, h, _ht, :], s_hs[:, h, _ht, :],
                                        s_cwr.ap().rearrange("p (h t) -> p h t", h=2)[:, h, :], ALU.mult)
                        for h in range(2)][-1], sm_pool)
            hs_pool = sm_pool.c
            hs_act = sm_act.c
            if KDBG and fi == 0 and ex == 0:
                ddump("dbg_hs", s_hs.ap().rearrange("p a b t -> p (a b t)"), sm_pool, hs_pool)
            if ex < NEXP:
                wt("tensor", sm_pool, hs_pool)
            else:
                wt("tensor", sm_act, hs_act)
            for dt in range(8):
                gdx = ex * 8 + dt
                if gdx >= 1:
                    wt("sync", sm_pe, w2_marks[gdx - 1])
                m = dma("sync", s_rw2[:, 0, :, :],
                        w_f2_b.ap()[ex, dt].rearrange("ht p m -> p ht m"), sm_ld)
                wt("tensor", sm_ld, m)
                if dt < 6:
                    def ft(e, _e=ex, _dt=dt):
                        last = None
                        for ht in range(8):
                            for h in range(2):
                                last = e.matmul(p_dn[:, _dt, h, :], s_rw2[:, 0, ht, :],
                                                s_hs[:, h, ht, :], start=False,
                                                stop=(_e == NE - 1 and ht == 7))
                        return last
                    w2_marks[gdx] = op("tensor", ft, sm_pe)
                else:
                    bk = rot_next()

                    def ft(e, _dt=dt, _bk=bk):
                        last = None
                        for ht in range(8):
                            for h in range(2):
                                last = e.matmul(p_rot[:, _bk, h * HF:(h + 1) * HF],
                                                s_rw2[:, 0, ht, :], s_hs[:, h, ht, :],
                                                start=(ht == 0 and h == 0),
                                                stop=(ht == 7 and h == 1))
                        return last
                    m = op("tensor", ft, sm_pe)
                    w2_marks[gdx] = m
                    wt("vector", sm_pe, m)
                    base = sm_dve.c
                    for h in range(2):
                        op("vector", lambda e, _dt=dt, _bk=bk, _h=h:
                           e.scalar_tensor_tensor(s_racc[:, _dt - 6, _h, :],
                                                  p_rot[:, _bk, _h * HF:(_h + 1) * HF], 1.0,
                                                  s_racc[:, _dt - 6, _h, :], ALU.mult, ALU.add), sm_dve)
                        wt("vector", sm_dve, base + 1 + h)
                    rot_release(bk, sm_dve, base + 2)
        dn_done = sm_pe.c
        if KDBG:
            ddump(f"dbg_racc{fi}", s_racc.ap().rearrange("p a b t -> p (a b t)"), sm_dve, sm_dve.c)
            for dt in range(6):
                wt("scalar", sm_pe, dn_done)
                if dt >= 1:
                    wt("scalar", sm_st, sm_st.c)
                mdx = op("scalar", lambda e, _dt=dt: e.activation(
                    s_hs.ap()[:, 0, 0:2, :].rearrange("p a t -> p (a t)"),
                    p_dn.ap()[:, _dt, :, :].rearrange("p b t -> p (b t)"),
                    AF.Copy), sm_act)
                wt("sync", sm_act, mdx)
                dma("sync", dbg[f"dbg_pdn{fi}"].ap().bitcast(BF16)[:, dt * 2 * HF:(dt + 1) * 2 * HF],
                    s_hs.ap()[:, 0, 0:2, :].rearrange("p a t -> p (a t)"), sm_st)
            wt("scalar", sm_st, sm_st.c)

        for h in range(2):
            x2p_marks = []
            for dt in range(8):
                if dt == 0:
                    wt("vector", sm_pe, dn_done)
                if dt < 6:
                    m = op("vector", lambda e, _dt=dt, _h=h:
                           e.scalar_tensor_tensor(s_x2p[:, _dt, :], p_dn[:, _dt, _h, :], 1.0,
                                                  s_x1s[:, _dt, _h * HF:(_h + 1) * HF],
                                                  ALU.mult, ALU.add), sm_dve)
                else:
                    m = op("vector", lambda e, _dt=dt, _h=h:
                           e.tensor_add(s_x2p[:, _dt, :], s_racc[:, _dt - 6, _h, :],
                                        s_x1s[:, _dt, _h * HF:(_h + 1) * HF]), sm_dve)
                x2p_marks.append(m)
            if KDBG:
                ddump(f"dbg_x2a{fi}{h}", s_x2p.ap().rearrange("p a t -> p (a t)"), sm_dve, sm_dve.c)
            wt("scalar", sm_dve, x2p_marks[7])
            mcp = op("scalar", lambda e: e.activation(
                s_x1s.ap()[:, :, 0:HF].rearrange("p a t -> p a t"),
                s_x2p.ap(), AF.Copy), sm_act)
            wt("tensor", sm_act, mcp)
            bk = rot_next()
            bk2 = rot_next()
            st_marks = {}
            for dc in range(8):
                if dc >= 2:
                    wt("vector", sm_pe, st_marks[dc - 2])
                mq = op("vector", lambda e, _dc=dc: e.tensor_tensor(
                    s_sq[:, _dc % 2, 0:HF], s_x2p[:, _dc, :], s_x2p[:, _dc, :], ALU.mult), sm_dve)
                wt("tensor", sm_dve, mq)

                def ft(e, _dc=dc, _b1=bk, _b2=bk2):
                    e.matmul(p_rot[:, _b1, 0:HF], s_o128[:, :], s_x1s[:, _dc, 0:HF],
                             start=(_dc == 0), stop=(_dc == 7))
                    return e.matmul(p_rot[:, _b2, 0:HF], s_o128[:, :], s_sq[:, _dc % 2, 0:HF],
                                    start=(_dc == 0), stop=(_dc == 7))
                st_marks[dc] = op("tensor", ft, sm_pe)
            st2 = sm_pe.c
            wt("vector", sm_pe, st2)
            base = sm_dve.c
            op("vector", lambda e, _b=bk: e.tensor_scalar_mul(s_r2[:, 0, :], p_rot[:, _b, 0:HF], 1.0 / 1024.0), sm_dve)
            op("vector", lambda e, _b=bk2: e.tensor_scalar_mul(s_lnt[:, 0:HF], p_rot[:, _b, 0:HF], 1.0 / 1024.0), sm_dve)
            rot_release(bk, sm_dve, base + 1)
            rot_release(bk2, sm_dve, base + 2)
            wt("vector", sm_dve, base + 2)
            op("vector", lambda e: e.tensor_tensor(s_lnt2[:, 0:HF], s_r2[:, 0, :], s_r2[:, 0, :], ALU.mult), sm_dve)
            wt("vector", sm_dve, base + 3)
            m = op("vector", lambda e: e.tensor_sub(s_lnt[:, 0:HF], s_lnt[:, 0:HF], s_lnt2[:, 0:HF]), sm_dve)
            wt("scalar", sm_dve, m)
            m = op("scalar", lambda e: e.activation(s_lnt[:, 0:HF], s_lnt[:, 0:HF], AF.Ln, bias=1e-5), sm_act)
            wt("scalar", sm_act, m)
            m = op("scalar", lambda e: e.activation(s_lnt[:, 0:HF], s_lnt[:, 0:HF], AF.Exp, scale=-0.5), sm_act)
            wt("vector", sm_act, m)
            base2 = sm_dve.c
            op("vector", lambda e: e.tensor_tensor(s_r2[:, 1, :], s_r2[:, 0, :], s_lnt[:, 0:HF], ALU.mult), sm_dve)
            wt("vector", sm_dve, base2 + 1)
            m = op("vector", lambda e: e.tensor_copy(s_r2[:, 0, :], s_lnt[:, 0:HF]), sm_dve)
            wt("vector", sm_dve, base2 + 2)
            base = sm_dve.c
            op("vector", lambda e: e.tensor_tensor(
                s_x2p.ap(), s_x2p.ap(),
                s_r2.ap()[:, 0, :].unsqueeze(1).broadcast_to([P, 8, HF]),
                ALU.mult), sm_dve)
            wt("vector", sm_dve, base + 1)
            m = op("vector", lambda e: e.tensor_tensor(
                s_x2p.ap(), s_x2p.ap(),
                s_r2.ap()[:, 1, :].unsqueeze(1).broadcast_to([P, 8, HF]),
                ALU.subtract), sm_dve)
            wt("sync", sm_dve, m)
            if KDBG:
                ddump(f"dbg_x2n{fi}{h}", s_x2p.ap().rearrange("p a t -> p (a t)"), sm_dve, m)
            col = fi * FS + h * HF
            dma("sync", t_out.ap().rearrange("(a p) t -> p a t", p=P)[:, :, col:col + HF],
                s_x2p[:, :, :], sm_st)
            barrier()

    pass
    _lim = os.environ.get("KLIM", "")
    if _lim == "A0":
        prologue()
        phaseA(0)
    elif _lim == "A0NP":
        phaseA(0)
    elif _lim.startswith("N"):
        n = int(_lim[1:])
        prologue()
        for s in range(n):
            phaseA(s)
            phaseS(s)
            phaseO(s)
    elif _lim.startswith("A:"):
        prologue()
        phaseA(0, acut=int(_lim[2:]))
    elif _lim in ("A", "AS", "ASO"):
        prologue()
        phaseA(0)
        if len(_lim) >= 2:
            phaseS(0)
        if len(_lim) >= 3:
            phaseO(0)
    else:
        prologue()
        for s in range(NSC):
            phaseA(s)
            phaseS(s)
            phaseO(s)
        for fi in range(NFS):
            phaseF(fi)

    final_st = sm_st.c
    for e in ("sync", "gpsimd"):
        emit(e, lambda eng, _v=final_st: eng.wait_ge(sm_st.h, _v))

    with ctx:
        with nc.Block() as block:
            @block.sync
            def _(eng):
                for fn in cmds["sync"]:
                    fn(eng)

            @block.tensor
            def _(eng):
                for fn in cmds["tensor"]:
                    fn(eng)

            @block.vector
            def _(eng):
                for fn in cmds["vector"]:
                    fn(eng)

            @block.scalar
            def _(eng):
                for fn in cmds["scalar"]:
                    fn(eng)

            @block.gpsimd
            def _(eng):
                for fn in cmds["gpsimd"]:
                    fn(eng)

    return nc


_NC_CACHE = {}


def kernel(**inputs):
    if "nc" not in _NC_CACHE:
        _NC_CACHE["nc"] = build_nc()
    nc = _NC_CACHE["nc"]
    in_maps = [prep_core_inputs(k, inputs) for k in range(NCORES)]
    res = run_bass_kernel_spmd(nc, in_maps, core_ids=list(range(NCORES)), trace=bool(os.environ.get("KTRACE")))
    _NC_CACHE["res"] = res
    out = np.zeros((B, L, D), np.float32)
    for k in range(NCORES):
        b, m = k // 2, k % 2
        o = np.asarray(res.results[k]["o_x2"])
        for s in range(NSC):
            t0 = s * SC + m * HSC
            out[b, t0:t0 + HSC, :] = o[:, s * HSC:(s + 1) * HSC].T
    return out


# version 06 (final)
# speedup vs baseline: 1.1603x; 1.0531x over previous
import os
"""Fused single-launch Trainium2 megakernel for mk_seq_postln_d1 (mamba mixer +
dense-MoE FFN, depth 1, d_model 1024, batch 4, seq 4096).

Sharding: core k -> batch k//2, mamba-channel half k%2, per-SC token half k%2
(each SC of 256 tokens: core m owns columns [m*128, m*128+128)).
Cross-core (pair) exchange via addr_space="Shared" DRAM + flag polling.
Layout: feature-major activations ([feature-chunk on partitions, tokens free]).
Scan: segmented tensor_tensor_scan (16 state lanes per channel row, reset cols).
"""
from contextlib import ExitStack

import numpy as np
import ml_dtypes

import concourse.bass as bass
import concourse.mybir as mybir
from concourse.bass_utils import run_bass_kernel_spmd

F32 = mybir.dt.float32
BF16 = mybir.dt.bfloat16
U32 = mybir.dt.uint32
AF = mybir.ActivationFunctionType
ALU = mybir.AluOpType
AX = mybir.AxisListType

B, L, D, E, N, DR = 4, 4096, 1024, 2048, 16, 64
EH = E // 2
NEXP = 15
NE = 16
SC = 256
NSC = L // SC
HSC = SC // 2
TOK = NSC * HSC
FS = 512
HF = FS // 2
NFS = TOK // FS
SEG = SC + 1
P = 128
MAGIC = 0x5A17C0DE
NCORES = 8

CONSTS = [1e-5, 1.0 / 1024.0, -0.5, -1.0, 1.0, 0.0, 2.0, 0.5]


def prep_core_inputs(core, I):
    b = core // 2
    m = core % 2
    sl = slice(m * EH, (m + 1) * EH)
    d = {}
    d["resid_fm"] = np.ascontiguousarray(np.asarray(I["L0_attn_resid"][b]).T).astype(np.float32)
    for nm, w in (("w_in", I["L0_attn_in_w"]), ("w_skip", I["L0_attn_skip_w"])):
        wt = np.asarray(w)[sl].T.astype(np.float32)
        d[nm] = np.ascontiguousarray(wt.reshape(8, P, 8, P).transpose(2, 0, 1, 3))
    ot = np.asarray(I["L0_attn_out_w"]).T.astype(np.float32)
    oc = np.concatenate([ot[sl], ot[(1 - m) * EH:(2 - m) * EH]], axis=0)
    d["w_out"] = np.ascontiguousarray(oc.reshape(16, P, 8, P).transpose(2, 0, 1, 3))
    wibc = np.zeros((8, P, P), np.float32)
    w = np.asarray(I["L0_attn_wd1"])[:, sl].T.astype(np.float32).reshape(8, P, DR)
    wibc[:, :, 0:64] = w
    wb = np.asarray(I["L0_attn_wB"])[:, sl].T.astype(np.float32).reshape(8, P, N)
    wibc[:, :, 64:80] = wb
    wc = np.asarray(I["L0_attn_wC"])[:, sl].T.astype(np.float32).reshape(8, P, N)
    wibc[:, :, 96:112] = wc
    d["w_ibc"] = np.ascontiguousarray(wibc)
    d["w_wd2"] = np.ascontiguousarray(np.asarray(I["L0_attn_wd2"])[sl].T.astype(np.float32))
    rw = np.zeros((8, P, P), np.float32)
    rw[:, :, 0:NEXP] = np.asarray(I["L0_ffn_router_W"]).astype(np.float32).reshape(8, P, NEXP)
    d["w_rt"] = np.ascontiguousarray(rw)
    e1 = np.concatenate([np.asarray(I["L0_ffn_eW1"]), np.asarray(I["L0_ffn_sW1"])[None]], 0).astype(np.float32)
    e2 = np.concatenate([np.asarray(I["L0_ffn_eW2"]), np.asarray(I["L0_ffn_sW2"])[None]], 0).astype(np.float32)
    d["w_f1"] = np.ascontiguousarray(e1.reshape(NE, 8, P, 8, P).transpose(0, 3, 1, 2, 4))
    d["w_f2"] = np.ascontiguousarray(e2.reshape(NE, 8, P, 8, P).transpose(0, 3, 1, 2, 4))
    d["b2cat"] = np.ascontiguousarray(
        np.concatenate([np.asarray(I["L0_ffn_eb2"]), np.asarray(I["L0_ffn_sb2"])[None]], 0).astype(np.float32))
    cols = []

    def pp(v):
        return np.asarray(v).astype(np.float32)[sl].reshape(8, P).T

    cols.append(pp(I["L0_attn_conv_b"]))
    cols.append(pp(I["L0_attn_W_D"]))
    cols.append(pp(I["L0_attn_wd2_b"]))
    cw = np.asarray(I["L0_attn_conv_w"]).astype(np.float32)[sl]
    cols.append(cw.reshape(8, P, 4).transpose(1, 0, 2).reshape(P, 32))
    cols.append(np.asarray(I["L0_attn_norm_w"]).astype(np.float32).reshape(8, P).T)
    al = np.asarray(I["L0_attn_A_log"]).astype(np.float32)[sl]
    cols.append(al.reshape(8, P, N).transpose(1, 0, 2).reshape(P, 128))
    rb = np.zeros((P, 1), np.float32)
    rb[:NEXP, 0] = np.asarray(I["L0_ffn_router_b"]).astype(np.float32)
    cols.append(rb)
    eb1 = np.concatenate([np.asarray(I["L0_ffn_eb1"]), np.asarray(I["L0_ffn_sb1"])[None]], 0).astype(np.float32)
    cols.append(eb1.reshape(NE, 8, P).transpose(2, 1, 0).reshape(P, 128))
    cols.append(np.ones((P, 1), np.float32))
    d["vecs"] = np.ascontiguousarray(np.concatenate(cols, axis=1))
    d["consts"] = np.broadcast_to(np.array(CONSTS, np.float32), (P, len(CONSTS))).copy()
    d["id128b"] = np.eye(P, dtype=np.float32)
    d["id16"] = np.eye(16, dtype=np.float32)
    return d


def build_nc():
    nc = bass.Bass("TRN2", target_bir_lowering=False)

    di = lambda n, sh, dt=F32: nc.dram_tensor(n, sh, dt, kind="ExternalInput")
    t_res = di("resid_fm", [D, L])
    t_win = di("w_in", [8, 8, P, P])
    t_wsk = di("w_skip", [8, 8, P, P])
    t_wout = di("w_out", [8, 16, P, P])
    t_wibc = di("w_ibc", [8, P, P])
    t_wd2 = di("w_wd2", [DR, EH])
    t_wrt = di("w_rt", [8, P, P])
    t_wf1 = di("w_f1", [NE, 8, 8, P, P])
    t_wf2 = di("w_f2", [NE, 8, 8, P, P])
    t_b2 = di("b2cat", [16, D])
    t_vecs = di("vecs", [P, 322])
    t_consts = di("consts", [P, 8])
    t_idb = di("id128b", [P, P])
    t_id16 = di("id16", [16, 16])

    t_out = nc.dram_tensor("o_x2", [D, TOK], F32, kind="ExternalOutput")
    KDBG = bool(os.environ.get("KDEBUG"))
    if KDBG:
        dbg = {}
        for nm, sh in (("dbg_xn", [P, 8 * SC]), ("dbg_x", [P, 8 * SC]), ("dbg_dl", [P, 8 * SC]),
                       ("dbg_ibc", [112, SC]), ("dbg_a", [P, N * SEG]), ("dbg_u", [P, N * SEG]),
                       ("dbg_h", [P, N * SEG]), ("dbg_y2", [P, 8 * SC]), ("dbg_x1", [P, 8 * SC]),
                       ("dbg_cw", [NEXP, 2 * P]), ("dbg_sg", [P, 8 * SC]), ("dbg_zd", [P, 8 * SC]),
                       ("dbg_y2p", [P, 8 * SC]), ("dbg_x1pre", [P, 8 * SC]), ("dbg_wo", [P, 16 * P]),
                       ("dbg_hs", [P, 8 * HF])):
            dbg[nm] = nc.dram_tensor(nm, sh, F32, kind="ExternalOutput")
        for fi in range(4):
            for nm, sh in ((f"dbg_cwo{fi}", [16, FS]), (f"dbg_fx1{fi}", [P, 4 * FS]),
                           (f"dbg_racc{fi}", [P, 4 * HF]), (f"dbg_pdn{fi}", [P, 6 * 2 * HF])):
                dbg[nm] = nc.dram_tensor(nm, sh, F32, kind="ExternalOutput")
            for h in range(2):
                for nm, sh in ((f"dbg_x2a{fi}{h}", [P, 8 * HF]), (f"dbg_x2n{fi}{h}", [P, 8 * HF])):
                    dbg[nm] = nc.dram_tensor(nm, sh, F32, kind="ExternalOutput")
        dbg["dbg_x1fd"] = nc.dram_tensor("dbg_x1fd", [P, 8 * TOK], F32, kind="ExternalOutput")

    ds = lambda n, sh, dt: nc.dram_tensor(n, sh, dt)
    w_in_b = ds("w_in_b", [8, 8, P, P], BF16)
    w_sk_b = ds("w_sk_b", [8, 8, P, P], BF16)
    w_out_b = ds("w_out_b", [8, 16, P, P], BF16)
    w_f1_b = ds("w_f1_b", [NE, 8, 8, P, P], BF16)
    w_f2_b = ds("w_f2_b", [NE, 8, 8, P, P], BF16)
    x1b_d = ds("x1b_d", [8, P, TOK], BF16)
    rows_d = ds("rows_d", [4, 4096], F32)
    bc_d = ds("bc_d", [2, 16, L], BF16)
    cw_d = ds("cw_d", [NEXP, TOK], BF16)
    ln2_d = ds("ln2_d", [2, TOK], F32)

    sh_flags = nc.dram_tensor("sh_flags", [2, 64], U32, addr_space="Shared")
    sh_ibc = nc.dram_tensor("sh_ibc", [2, NSC, 112, SC], F32, addr_space="Shared")
    sh_y2 = nc.dram_tensor("sh_y2", [2, NSC, P, 8 * SC], BF16, addr_space="Shared")
    FL_HELLO, FL_IBC, FL_Y2 = 0, 1, 20

    ctx = ExitStack()
    sb = lambda n, sh, dt=F32: ctx.enter_context(nc.sbuf_tensor(n, sh, dt))

    s_vecs = sb("s_vecs", [P, 322])
    s_consts = sb("s_consts", [P, 8])
    for v, i in zip(CONSTS, range(len(CONSTS))):
        nc.const_aps.register(F32, v, s_consts[:, i:i + 1])
    s_idb = sb("s_idb", [P, P], BF16)
    s_id16 = sb("s_id16", [16, 16])
    s_A = sb("s_A", [P, 8, N])
    s_wibc = sb("s_wibc", [P, 8, P], BF16)
    s_wd2 = sb("s_wd2", [DR, EH], BF16)
    s_wrt = sb("s_wrt", [P, 8, P], BF16)
    s_b2 = sb("s_b2", [16, D], BF16)

    s_resid = sb("s_resid", [P, 1, 8, SC])
    s_sq = sb("s_sq", [P, 2, SC], BF16)
    s_onesb = sb("s_onesb", [P, 1], BF16)
    s_o128 = sb("s_o128", [P, P], BF16)
    s_lnt = sb("s_lnt", [P, SC])
    s_rows = sb("s_rows", [1, 64])
    s_rep = sb("s_rep", [P, 2, SC])
    s_xn = sb("s_xn", [P, 8, SC], BF16)
    s_xpre = sb("s_xpre", [P, 8, SC + 3], BF16)
    s_xtail = sb("s_xtail", [P, 8, 3], BF16)
    s_cacc = sb("s_cacc", [P, 2, SC])
    s_x = sb("s_x", [P, 1, 8, SC], BF16)
    s_sg = sb("s_sg", [P, 1, 8, SC], BF16)
    s_zd = sb("s_zd", [P, 1, 8, SC], BF16)
    _ci_bf = None
    s_dl = sb("s_dl", [P, 8, SC], BF16)
    s_w = sb("s_w", [P, 8, SC], BF16)
    s_ibc = sb("s_ibc", [112, SC])
    s_ibc2 = sb("s_ibc2", [112, SC])
    s_innb = sb("s_innb", [64, SC], BF16)
    s_bcbB = sb("s_bcbB", [16, SC], BF16)
    s_bcbC = sb("s_bcbC", [16, SC], BF16)
    s_Brep = sb("s_Brep", [P, 2, N, HSC], BF16)
    s_Crep = sb("s_Crep", [P, 2, N, HSC], BF16)
    s_a = sb("s_a", [P, 2, N, SEG])
    s_uh = sb("s_uh", [P, 3, N, SEG], BF16)
    s_hc = sb("s_hc", [P, 8, N])
    s_y = sb("s_y", [P, SC])
    s_g1 = sb("s_g1", [P, SC])
    s_lnt2 = s_g1
    s_lg = None
    s_y2 = sb("s_y2", [P, 1, 8, SC], BF16)
    s_y2p = sb("s_y2p", [P, 8, SC], BF16)
    s_x1p = sb("s_x1p", [P, 8, SC])
    s_x1b = sb("s_x1b", [P, 8, SC], BF16)
    s_lgt = sb("s_lgt", [P, 2, NEXP])
    s_sm = sb("s_sm", [P, 2, NEXP])
    s_wk = sb("s_wk", [P, 2, NEXP])
    s_mx = sb("s_mx", [P, 2, 4])
    s_msk = sb("s_msk", [P, 2, NEXP], U32)
    s_negs = sb("s_negs", [P, NEXP])
    s_cwt = sb("s_cwt", [P, 2, NEXP], BF16)
    s_cwf = sb("s_cwf", [NEXP, 2, P], BF16)
    s_rin = sb("s_rin", [P, 1, 8, P], BF16)
    s_rsk = sb("s_rsk", [P, 1, 8, P], BF16)
    s_rot = sb("s_rot", [P, 1, 16, P], BF16)
    s_rw1 = sb("s_rw1", [P, 2, 8, P], BF16)
    s_rw2 = sb("s_rw2", [P, 2, 8, P], BF16)
    s_x1s = sb("s_x1s", [P, 8, FS], BF16)
    s_hs = sb("s_hs", [P, 2, 8, HF], BF16)
    s_racc = sb("s_racc", [P, 2, 2, HF])
    s_cwr = sb("s_cwr", [P, FS], BF16)
    s_cwo = sb("s_cwo", [16, FS], BF16)
    s_x2p = sb("s_x2p", [P, 8, HF])

    s_r2 = sb("s_r2", [P, 2, HF])
    s_ci = sb("s_ci", [P, 2, 256])
    s_co = sb("s_co", [P, 2, 256], BF16)
    _cibf = s_ci.ap().rearrange("p a b -> p (a b)").bitcast(BF16)

    s_flag = sb("s_flag", [1, 4], U32)
    s_sp1t = sb("s_sp1t", [P, 8, SC], BF16)
    s_one = sb("s_one", [1, 68], U32)

    p_dn = ctx.enter_context(nc.psum_tensor("p_dn", [P, 6, 2, HF], F32))
    p_rot = ctx.enter_context(nc.psum_tensor("p_rot", [P, 2, FS], F32))

    class Sem:
        def __init__(self, name):
            self.h = ctx.enter_context(nc.semaphore(name))
            self.c = 0

    sm_w = Sem("sm_w")
    sm_ld = Sem("sm_ld")
    sm_ex = Sem("sm_ex")
    sm_st = Sem("sm_st")
    sm_po = Sem("sm_po")
    sm_pe = Sem("sm_pe")
    sm_act = Sem("sm_act")
    sm_dve = Sem("sm_dve")
    sm_pool = Sem("sm_pool")
    sm_bar = Sem("sm_bar")

    cmds = {e: [] for e in ("sync", "tensor", "vector", "scalar", "gpsimd")}

    def emit(eng, fn):
        cmds[eng].append(fn)

    def wt(eng, sem, val):
        emit(eng, lambda e, _s=sem.h, _v=val: e.wait_ge(_s, _v))

    def op(eng, make, sem=None, n=1):
        if sem is not None:
            sem.c += n

        def f(e, _mk=make, _sem=sem, _n=n):
            ins = _mk(e)
            if _sem is not None:
                ins.then_inc(_sem.h, _n)
        emit(eng, f)
        return sem.c if sem is not None else None

    def dma(eng, out_ap, in_ap, sem):
        return op(eng, lambda e, _o=out_ap, _i=in_ap: e.dma_start(_o, _i), sem, 16)

    ENGS = ("tensor", "vector", "scalar", "gpsimd", "sync")

    def barrier():
        tgt = sm_bar.c + len(ENGS)
        st_now = sm_st.c
        for e in ENGS:
            def f(eng, _t=tgt, _st=st_now):
                eng.wait_ge(sm_st.h, _st)
                eng.drain().then_inc(sm_bar.h, 1)
                eng.wait_ge(sm_bar.h, _t)
            emit(e, f)
        sm_bar.c = tgt

    R = {}

    def sync_setup(eng):
        for nm in ("me", "pn", "pc", "fl", "fl2", "ne"):
            R[nm] = eng.alloc_register("r_" + nm)
        pid = eng.partition_id()
        eng.reg_mov(R["me"], pid)
        eng.reg_mod(R["me"], R["me"], 2)
        eng.reg_mov(R["pn"], 1)
        eng.reg_sub(R["pn"], R["pn"], R["me"])
        eng.reg_mov(R["pc"], 0)
    emit("sync", sync_setup)

    def me_sw(e, fn0, fn1):
        with e.If_eq(R["me"], 0):
            fn0(e)
        with e.Else():
            fn1(e)

    def flag_write(eng, slot):
        me_sw(eng,
              lambda e: e.dma_start(sh_flags.ap()[0:1, slot:slot + 1], s_one[0:1, 0:1]).then_inc(sm_ex.h, 16),
              lambda e: e.dma_start(sh_flags.ap()[1:2, slot:slot + 1], s_one[0:1, 0:1]).then_inc(sm_ex.h, 16))

    def flag_poll(eng, slot):
        eng.reg_mov(R["ne"], 1)
        with eng.While(R["ne"]):
            eng.reg_add(R["pc"], R["pc"], 16)
            eng.dma_start(s_flag[0:1, 0:2],
                          bass.AP(sh_flags, slot, [[1, 1], [64, 2], [1, 1]]),
                          allow_slow_non_contiguous=True).then_inc(sm_po.h, 16)
            eng.wait_ge(sm_po.h, R["pc"])
            eng.reg_load(R["fl"], s_flag[0:1, 0:1])
            eng.reg_load(R["fl2"], s_flag[0:1, 1:2])
            eng.reg_mul(R["fl"], R["fl"], R["me"])
            eng.reg_mul(R["fl2"], R["fl2"], R["pn"])
            eng.reg_add(R["fl"], R["fl"], R["fl2"])
            eng.reg_sub(R["ne"], R["fl"], MAGIC)

    def ddump(name, src_ap, sem, mark):
        if not KDBG:
            return
        wt("sync", sem, mark)
        nr = src_ap.partition_size()
        nf = src_ap.free_size()
        t = dbg[name]
        if src_ap.dtype == BF16:
            dst = t.ap().bitcast(BF16)[0:nr, 0:nf]
        else:
            dst = t.ap()[0:nr, 0:nf]
        dma("sync", dst, src_ap, sm_st)

    rot_i = [0]
    rot_guard = [None, None]

    def rot_next():
        b = rot_i[0] % 2
        rot_i[0] += 1
        if rot_guard[b] is not None:
            gsem, gval = rot_guard[b]
            wt("tensor", gsem, gval)
        return b

    def rot_release(b, sem, val):
        rot_guard[b] = (sem, val)

    def prologue():
        dma("sync", s_vecs[:], t_vecs[:, :], sm_ld)
        dma("sync", s_consts[:], t_consts[:, :], sm_ld)
        dma("sync", s_id16[:], t_id16[:, :], sm_ld)
        small_jobs = [(t_idb[:, :], s_idb[:, :], P)]
        for a in range(8):
            small_jobs.append((t_wibc.ap()[a], s_wibc[:, a, :], P))
            small_jobs.append((t_wrt.ap()[a], s_wrt[:, a, :], P))
        for c0 in range(0, EH, 256):
            small_jobs.append((t_wd2[:, c0:c0 + 256], s_wd2[:, c0:c0 + 256], 64))
            small_jobs.append((t_b2[:, c0:c0 + 256], s_b2[:, c0:c0 + 256], 16))
        for (src, dst, npart) in small_jobs:
            cw = src.shape[-1]
            m = dma("sync", s_ci[0:npart, 0, 0:cw], src, sm_ld)
            wt("scalar", sm_ld, m)
            m = op("scalar", lambda e, _d=dst, _cw=cw, _np=npart:
                   e.activation(_d, s_ci[0:_np, 0, 0:_cw], AF.Copy), sm_act)
            wt("sync", sm_act, m)
        m = op("scalar", lambda e: e.activation(
            s_A.ap().rearrange("p a b -> p (a b)"), s_vecs[:, 64:192], AF.Exp), sm_act)
        wt("vector", sm_act, m)

        def fv(e):
            e.tensor_scalar_mul(s_A.ap().rearrange("p a b -> p (a b)"),
                                s_A.ap().rearrange("p a b -> p (a b)"), -1.0)
            e.memset(s_onesb[:], 1.0)
            e.memset(s_o128[:, :], 1.0)
            e.memset(s_one[0:1, 0:1], MAGIC)
            e.memset(s_one[0:1, 1:68], 0)
            e.memset(s_negs[:], -1.0)
            e.memset(s_xtail.ap().rearrange("p a b -> p (a b)"), 0.0)
            e.memset(s_hc.ap().rearrange("p a b -> p (a b)"), 0.0)
            return e.memset(s_rows[:], 0.0)
        m = op("vector", fv, sm_dve)
        wt("sync", sm_dve, m)

        ex0 = sm_ex.c

        if not os.environ.get("NOHELLO"):
            def f(e, _b=ex0):
                me_sw(e,
                      lambda ee: ee.dma_start(sh_flags.ap()[0:1, 0:64], s_one[0:1, 4:68]).then_inc(sm_ex.h, 16),
                      lambda ee: ee.dma_start(sh_flags.ap()[1:2, 0:64], s_one[0:1, 4:68]).then_inc(sm_ex.h, 16))
                e.wait_ge(sm_ex.h, _b + 16)
                flag_write(e, FL_HELLO)
                e.wait_ge(sm_ex.h, _b + 32)
                flag_poll(e, FL_HELLO)
            emit("sync", f)
            sm_ex.c += 32

        CH = 256
        store_marks = [0, 0]
        _jobs = [
            (t_win, w_in_b, 8 * 8 * P * P),
            (t_wsk, w_sk_b, 8 * 8 * P * P),
            (t_wout, w_out_b, 8 * 16 * P * P),
            (t_wf1, w_f1_b, NE * 8 * 8 * P * P),
            (t_wf2, w_f2_b, NE * 8 * 8 * P * P),
        ]
        if os.environ.get("NOCAST"):
            _jobs = _jobs[:2]
        for (src, dst, nel) in _jobs:
            nch = nel // (P * CH)
            for c in range(nch):
                par = c % 2
                if store_marks[par]:
                    wt("sync", sm_w, store_marks[par])
                m = dma("sync", s_ci[:, par, :],
                        bass.AP(src, c * P * CH, [[CH, P], [1, CH]]), sm_w)
                wt("scalar", sm_w, m)
                cm = op("scalar", lambda e, _p=par: e.activation(s_co[:, _p, :], s_ci[:, _p, :], AF.Copy), sm_act)
                wt("sync", sm_act, cm)
                store_marks[par] = dma("sync", bass.AP(dst, c * P * CH, [[CH, P], [1, CH]]),
                                       s_co[:, par, :], sm_w)
        wt("sync", sm_w, sm_w.c)
        barrier()

    def phaseA(s, acut=99):
        slot3 = 0
        par = 0
        rm = dma("sync", s_resid[:, slot3, :, :],
                 t_res.ap().rearrange("(dc p) t -> p dc t", p=P)[:, :, s * SC:(s + 1) * SC], sm_ld)
        wt("vector", sm_ld, rm)

        bk = rot_next()
        st_marks = {}
        for dc in range(8):
            if dc >= 2:
                wt("vector", sm_pe, st_marks[dc - 2])
            mq = op("vector", lambda e, _sl=slot3, _dc=dc:
                    e.tensor_tensor(s_sq[:, _dc % 2, :], s_resid[:, _sl, _dc, :],
                                    s_resid[:, _sl, _dc, :], ALU.mult), sm_dve)
            wt("tensor", sm_dve, mq)
            st_marks[dc] = op("tensor", lambda e, _dc=dc, _bk=bk:
               e.matmul(p_rot[:, _bk, 0:SC], s_o128[:, :], s_sq[:, _dc % 2, :],
                        start=(_dc == 0), stop=(_dc == 7)), sm_pe)
        wt("scalar", sm_pe, sm_pe.c)

        def fa(e, _bk=bk):
            return e.activation(s_lnt[:, :], p_rot[:, _bk, 0:SC], AF.Ln,
                                scale=1.0 / 1024.0, bias=1e-5)
        m = op("scalar", fa, sm_act)
        rot_release(bk, sm_act, m)
        wt("scalar", sm_act, m)
        m = op("scalar", lambda e: e.activation(s_rep[:, 0, :], s_lnt[:, :], AF.Exp, scale=-0.5), sm_act)
        if acut <= 1:
            return
        wt("vector", sm_act, m)
        for dc in range(8):
            op("vector", lambda e, _sl=slot3, _dc=dc:
               e.scalar_tensor_tensor(s_xn[:, _dc, :], s_resid[:, _sl, _dc, :],
                                      s_vecs[:, 56 + _dc:57 + _dc], s_rep[:, 0, :],
                                      ALU.mult, ALU.mult), sm_dve)
        xn_done = sm_dve.c
        if acut <= 2:
            return

        evac_marks = {}
        for et in range(8):
            if et >= 1:
                wt("sync", sm_act, evac_marks[et - 1])
            dma("sync", s_rin[:, 0, :, :],
                w_in_b.ap()[et].rearrange("dc p m -> p dc m"), sm_ld)
            m = dma("sync", s_rsk[:, 0, :, :],
                    w_sk_b.ap()[et].rearrange("dc p m -> p dc m"), sm_ld)
            bki = rot_next()
            bks = rot_next()
            wt("tensor", sm_ld, m)
            if et == 0:
                wt("tensor", sm_dve, xn_done)

            _plvl = int(os.environ.get("PLVL", "9"))

            def ft(e, _et=et, _bi=bki, _bs=bks):
                last = None
                for dc in range(8):
                    last = e.matmul(p_rot[:, _bi, 0:SC], s_rin[:, 0, dc, :], s_xn[:, dc, :],
                                    start=(dc == 0), stop=(dc == 7))
                return last
            m1 = op("tensor", ft, sm_pe)

            def ft2(e, _et=et, _bs=bks):
                last = None
                for dc in range(8):
                    last = e.matmul(p_rot[:, _bs, 0:SC], s_rsk[:, 0, dc, :], s_xn[:, dc, :],
                                    start=(dc == 0), stop=(dc == 7))
                return last
            m2 = op("tensor", ft2, sm_pe)
            wt("scalar", sm_pe, m2)
            mi = op("scalar", lambda e, _et=et, _bi=bki:
                    e.activation(s_xpre[:, _et, 3:3 + SC], p_rot[:, _bi, 0:SC], AF.Copy), sm_act)
            rot_release(bki, sm_act, mi)
            em = op("scalar", lambda e, _et=et, _bs=bks, _par=par:
                    e.activation(s_sg[:, _par, _et, :], p_rot[:, _bs, 0:SC], AF.Silu), sm_act)
            rot_release(bks, sm_act, em)
            evac_marks[et] = em
        if acut <= 3:
            return
        wt("vector", sm_act, sm_act.c)
        m = op("vector", lambda e: e.tensor_copy(s_xpre[:, :, 0:3], s_xtail[:]), sm_dve)
        wt("vector", sm_dve, m)

        silu_marks = {}
        for et in range(8):
            cp = et % 2
            if et >= 1:
                wt("vector", sm_act, silu_marks[et - 1])
            base = sm_dve.c
            op("vector", lambda e, _et=et, _cp=cp:
               e.tensor_scalar(s_cacc[:, _cp, :], s_xpre[:, _et, 0:SC],
                               s_vecs[:, 24 + _et * 4:25 + _et * 4],
                               s_vecs[:, 0 + _et:1 + _et], ALU.mult, ALU.add), sm_dve)
            for k in (1, 2, 3):
                wt("vector", sm_dve, base + k)
                dstp = cp if k % 2 == 0 else 1 - cp
                srcp = 1 - dstp
                op("vector", lambda e, _et=et, _k=k, _d=dstp, _s=srcp:
                   e.scalar_tensor_tensor(s_cacc[:, _d, :], s_xpre[:, _et, _k:_k + SC],
                                          s_vecs[:, 24 + _et * 4 + _k:25 + _et * 4 + _k],
                                          s_cacc[:, _s, :], ALU.mult, ALU.add), sm_dve)
            cv = sm_dve.c
            wt("scalar", sm_dve, cv)
            silu_marks[et] = op("scalar", lambda e, _et=et, _par=par:
               e.activation(s_x[:, _par, _et, :], s_cacc[:, 1 - (_et % 2), :], AF.Silu), sm_act)
        x_done = sm_act.c
        wt("vector", sm_act, x_done)
        op("vector", lambda e: e.tensor_copy(s_xtail[:], s_xpre[:, :, SC:SC + 3]), sm_dve)

        if acut <= 4:
            return
        wt("tensor", sm_act, x_done)
        bk2 = rot_next()

        def ft(e, _bk=bk2, _par=par):
            last = None
            for et in range(8):
                last = e.matmul(p_rot[:, _bk, 0:SC], s_wibc[:, et, :], s_x[:, _par, et, :],
                                start=(et == 0), stop=(et == 7))
            return last
        m = op("tensor", ft, sm_pe)
        wt("scalar", sm_pe, m)
        for (mm_, off) in ((DR, 0), (N, 64), (N, 96)):
            mev = op("scalar", lambda e, _mm=mm_, _off=off, _bk=bk2:
               e.activation(s_ibc[_off:_off + _mm, :], p_rot[_off:_off + _mm, _bk, 0:SC], AF.Copy), sm_act)
        rot_release(bk2, sm_act, mev)
        ibc_evac = sm_act.c

        if acut <= 5:
            return
        wt("sync", sm_act, ibc_evac)
        ex0 = sm_ex.c

        def f(e, _s=s, _b=ex0):
            me_sw(e,
                  lambda ee: ee.dma_start(sh_ibc.ap()[0, _s], s_ibc[:, :]).then_inc(sm_ex.h, 16),
                  lambda ee: ee.dma_start(sh_ibc.ap()[1, _s], s_ibc[:, :]).then_inc(sm_ex.h, 16))
            e.wait_ge(sm_ex.h, _b + 16)
            flag_write(e, FL_IBC + _s)
            flag_poll(e, FL_IBC + _s)
            me_sw(e,
                  lambda ee: ee.dma_start(s_ibc2[:, :], sh_ibc.ap()[1, _s]).then_inc(sm_ex.h, 16),
                  lambda ee: ee.dma_start(s_ibc2[:, :], sh_ibc.ap()[0, _s]).then_inc(sm_ex.h, 16))
        emit("sync", f)
        sm_ex.c += 48
        wt("vector", sm_ex, sm_ex.c)
        m = op("vector", lambda e: e.tensor_add(s_ibc[:, :], s_ibc[:, :], s_ibc2[:, :]), sm_dve)
        wt("scalar", sm_dve, m)
        op("scalar", lambda e: e.activation(s_bcbB[:, :], s_ibc[64:80, :], AF.Copy), sm_act)
        op("scalar", lambda e: e.activation(s_bcbC[:, :], s_ibc[96:112, :], AF.Copy), sm_act)
        m = op("scalar", lambda e: e.activation(s_innb[:, :], s_ibc[0:64, :], AF.Copy), sm_act)
        bc_cast = m

        if acut <= 6:
            return
        wt("sync", sm_act, bc_cast)
        dma("sync", bc_d.ap()[0, :, s * SC:(s + 1) * SC], s_bcbB[:, :], sm_ld)
        m = dma("sync", bc_d.ap()[1, :, s * SC:(s + 1) * SC], s_bcbC[:, :], sm_ld)
        wt("sync", sm_ld, m)
        for h in range(2):
            dma("sync", s_Brep[:, h, :, :],
                bc_d.ap()[0, :, s * SC + h * HSC:s * SC + (h + 1) * HSC]
                .unsqueeze(0).broadcast_to([P, N, HSC]), sm_ld)
            dma("sync", s_Crep[:, h, :, :],
                bc_d.ap()[1, :, s * SC + h * HSC:s * SC + (h + 1) * HSC]
                .unsqueeze(0).broadcast_to([P, N, HSC]), sm_ld)
        bc_ready = sm_ld.c

        if acut <= 7:
            return
        wt("tensor", sm_act, bc_cast)
        for et in range(8):
            bb = rot_next()
            m = op("tensor", lambda e, _et=et, _bb=bb:
                   e.matmul(p_rot[:, _bb, 0:SC], s_wd2[:, _et * P:(_et + 1) * P],
                            s_innb[:, :], start=True, stop=True), sm_pe)
            wt("scalar", sm_pe, m)
            em = op("scalar", lambda e, _et=et, _par=par, _bb=bb:
                    e.activation(s_zd[:, _par, _et, :], p_rot[:, _bb, 0:SC], AF.Copy), sm_act)
            rot_release(bb, sm_act, em)
        zd_done = sm_act.c

        if acut <= 8:
            return
        wt("scalar", sm_act, zd_done)
        for et in range(8):
            op("scalar", lambda e, _et=et, _par=par:
               e.activation(s_sp1t[:, _et, :], s_zd[:, _par, _et, :], AF.Abs,
                            bias=s_vecs[:, 16 + _et:17 + _et], scale=1.0), sm_act)
        m = sm_act.c
        wt("scalar", sm_act, m)
        m = op("scalar", lambda e: e.activation(
            s_sp1t.ap().rearrange("p a b -> p (a b)"),
            s_sp1t.ap().rearrange("p a b -> p (a b)"), AF.Exp, scale=-1.0), sm_act)
        wt("scalar", sm_act, m)
        m = op("scalar", lambda e: e.activation(
            s_sp1t.ap().rearrange("p a b -> p (a b)"),
            s_sp1t.ap().rearrange("p a b -> p (a b)"), AF.Ln, bias=1.0), sm_act)
        sp_done = m
        wt("vector", sm_act, sp_done)
        for et in range(8):
            op("vector", lambda e, _et=et, _par=par:
               e.tensor_scalar(s_dl[:, _et, :], s_zd[:, _par, _et, :],
                               s_vecs[:, 16 + _et:17 + _et], 0.0, ALU.add, ALU.max), sm_dve)
        m = sm_dve.c
        wt("vector", sm_dve, m)
        m = op("vector", lambda e: e.tensor_add(
            s_dl.ap().rearrange("p a b -> p (a b)"),
            s_dl.ap().rearrange("p a b -> p (a b)"),
            s_sp1t.ap().rearrange("p a b -> p (a b)")), sm_dve)
        wt("vector", sm_dve, m)
        op("vector", lambda e, _par=par: e.tensor_tensor(
            s_w.ap().rearrange("p a b -> p (a b)"),
            s_dl.ap().rearrange("p a b -> p (a b)"),
            s_x.ap()[:, _par, :, :].rearrange("p a b -> p (a b)"), ALU.mult), sm_dve)
        wt("vector", sm_ld, bc_ready)
        if KDBG and s == 0:
            ddump("dbg_xn", s_xn.ap().rearrange("p a t -> p (a t)"), sm_dve, sm_dve.c)
            ddump("dbg_x", s_x.ap()[:, 0, :, :].rearrange("p a t -> p (a t)"), sm_dve, sm_dve.c)
            ddump("dbg_dl", s_dl.ap().rearrange("p a t -> p (a t)"), sm_dve, sm_dve.c)
            ddump("dbg_ibc", s_ibc.ap(), sm_dve, sm_dve.c)
            ddump("dbg_zd", s_zd.ap()[:, 0, :, :].rearrange("p a t -> p (a t)"), sm_dve, sm_dve.c)
            ddump("dbg_sg", s_sg.ap()[:, 0, :, :].rearrange("p a t -> p (a t)"), sm_act, sm_act.c)
        barrier()

    def phaseS(s):
        par = 0
        tree_marks = {}
        scan_marks = {}
        for et in range(8):
            i = s * 8 + et
            ap_ = et % 2
            u_sl = (2 * i) % 3
            h_sl = (2 * i + 1) % 3
            if et >= 2:
                wt("scalar", sm_dve, scan_marks[et - 2])

            def fa(e, _et=et, _ap=ap_):
                e.memzero(s_a[:, _ap, :, 0:1])
                last = None
                for n in range(N):
                    last = e.activation(s_a[:, _ap, n, 1:SEG], s_dl[:, _et, :], AF.Exp,
                                        scale=s_A[:, _et, n:n + 1])
                return last
            a_done = op("scalar", fa, sm_act)

            if et >= 2:
                wt("gpsimd", sm_dve, tree_marks[et - 2])

            def fp(e, _et=et, _u=u_sl, _par=par):
                e.tensor_copy(s_uh[:, _u, :, 0:1], s_hc.ap()[:, _et, :].unsqueeze(2))
                last = None
                for h in range(2):
                    last = e.tensor_tensor(
                        s_uh[:, _u, :, 1 + h * HSC:1 + (h + 1) * HSC],
                        s_w.ap()[:, _et, h * HSC:(h + 1) * HSC].unsqueeze(1).broadcast_to([P, N, HSC]),
                        s_Brep[:, h, :, :], ALU.mult)
                return last
            u_done = op("gpsimd", fp, sm_pool)

            wt("vector", sm_act, a_done)
            wt("vector", sm_pool, u_done)
            m = op("vector", lambda e, _ap=ap_, _u=u_sl, _h=h_sl:
                   e.tensor_tensor_scan(
                       s_uh.ap()[:, _h, :, :].rearrange("p n t -> p (n t)"),
                       s_a.ap()[:, _ap, :, :].rearrange("p n t -> p (n t)"),
                       s_uh.ap()[:, _u, :, :].rearrange("p n t -> p (n t)"),
                       0.0, ALU.mult, ALU.add), sm_dve)
            wt("vector", sm_dve, m)
            m = op("vector", lambda e, _et=et, _h=h_sl:
                   e.tensor_copy(s_hc.ap()[:, _et, :].unsqueeze(2),
                                 s_uh[:, _h, :, SEG - 1:SEG]), sm_dve)
            scan_marks[et] = m

            if KDBG and s == 0 and et == 0:
                ddump("dbg_a", s_a.ap()[:, ap_, :, :].rearrange("p n t -> p (n t)"), sm_act, a_done)
                ddump("dbg_u", s_uh.ap()[:, u_sl, :, :].rearrange("p n t -> p (n t)"), sm_pool, u_done)
                ddump("dbg_h", s_uh.ap()[:, h_sl, :, :].rearrange("p n t -> p (n t)"), sm_dve, m)
            wt("gpsimd", sm_dve, m)

            def fp2(e, _h=h_sl):
                last = None
                for h in range(2):
                    last = e.tensor_tensor(
                        s_uh[:, _h, :, 1 + h * HSC:1 + (h + 1) * HSC],
                        s_uh[:, _h, :, 1 + h * HSC:1 + (h + 1) * HSC],
                        s_Crep[:, h, :, :], ALU.mult)
                return last
            m_done = op("gpsimd", fp2, sm_pool)

            wt("vector", sm_pool, m_done)
            base = sm_dve.c

            def fv1(e, _h=h_sl):
                hb = s_uh[:, _h]
                return e.tensor_add(hb[:, 0:8, :], hb[:, 0:8, :], hb[:, 8:16, :])
            op("vector", fv1, sm_dve)
            wt("vector", sm_dve, base + 1)
            op("vector", lambda e, _h=h_sl: e.tensor_add(
                s_uh[:, _h][:, 0:4, :], s_uh[:, _h][:, 0:4, :], s_uh[:, _h][:, 4:8, :]), sm_dve)
            wt("vector", sm_dve, base + 2)
            op("vector", lambda e, _h=h_sl: e.tensor_add(
                s_uh[:, _h][:, 0:2, :], s_uh[:, _h][:, 0:2, :], s_uh[:, _h][:, 2:4, :]), sm_dve)
            wt("vector", sm_dve, base + 3)
            op("vector", lambda e, _h=h_sl: e.tensor_tensor(
                s_y[:, :], s_uh[:, _h][:, 0, 1:SEG], s_uh[:, _h][:, 1, 1:SEG], ALU.add), sm_dve)
            wt("vector", sm_dve, base + 4)
            op("vector", lambda e, _et=et, _par=par: e.scalar_tensor_tensor(
                s_g1[:], s_x[:, _par, _et, :], s_vecs[:, 8 + _et:9 + _et],
                s_y[:], ALU.mult, ALU.add), sm_dve)
            wt("vector", sm_dve, base + 5)
            m = op("vector", lambda e, _et=et, _par=par: e.tensor_tensor(
                s_y2[:, _par, _et, :], s_g1[:], s_sg[:, _par, _et, :], ALU.mult), sm_dve)
            tree_marks[et] = m
        barrier()

    def phaseO(s):
        par = 0
        ex0 = sm_ex.c

        def f(e, _s=s, _par=par, _b=ex0):
            src = s_y2.ap()[:, _par, :, :].rearrange("p a t -> p (a t)")
            me_sw(e,
                  lambda ee: ee.dma_start(sh_y2.ap()[0, _s], src).then_inc(sm_ex.h, 16),
                  lambda ee: ee.dma_start(sh_y2.ap()[1, _s], src).then_inc(sm_ex.h, 16))
            e.wait_ge(sm_ex.h, _b + 16)
            flag_write(e, FL_Y2 + _s)
            flag_poll(e, FL_Y2 + _s)
            dst = s_y2p.ap().rearrange("p a t -> p (a t)")
            me_sw(e,
                  lambda ee: ee.dma_start(dst, sh_y2.ap()[1, _s]).then_inc(sm_ex.h, 16),
                  lambda ee: ee.dma_start(dst, sh_y2.ap()[0, _s]).then_inc(sm_ex.h, 16))
        emit("sync", f)
        sm_ex.c += 48
        y2x = sm_ex.c
        if KDBG and s == 0:
            ddump("dbg_y2p", s_y2p.ap().rearrange("p a t -> p (a t)"), sm_ex, y2x)
            dwo = dbg["dbg_wo"].ap().bitcast(BF16)
            for ec in range(16):
                dma("sync", dwo[:, ec * P:(ec + 1) * P], w_out_b.ap()[0, ec], sm_st)

        wt("tensor", sm_ex, y2x)
        evac_marks = {}
        for dt in range(8):
            if dt >= 1:
                wt("sync", sm_dve, evac_marks[dt - 1])
            m = dma("sync", s_rot[:, 0, :, :],
                    w_out_b.ap()[dt].rearrange("ec p m -> p ec m"), sm_ld)
            bk = rot_next()
            wt("tensor", sm_ld, m)

            def ft(e, _dt=dt, _bk=bk, _par=par):
                last = None
                for ec in range(16):
                    rhs = s_y2[:, _par, ec, :] if ec < 8 else s_y2p[:, ec - 8, :]
                    last = e.matmul(p_rot[:, _bk, 0:SC], s_rot[:, 0, ec, :], rhs,
                                    start=(ec == 0), stop=(ec == 15))
                return last
            m = op("tensor", ft, sm_pe)
            wt("vector", sm_pe, m)
            em = op("vector", lambda e, _dt=dt, _bk=bk, _sl=0:
                    e.scalar_tensor_tensor(s_x1p[:, _dt, :], s_resid[:, _sl, _dt, :], 2.0,
                                           p_rot[:, _bk, 0:SC], ALU.mult, ALU.add), sm_dve)
            evac_marks[dt] = em
            rot_release(bk, sm_dve, em)

        wt("scalar", sm_dve, evac_marks[7])
        mcp = op("scalar", lambda e: e.activation(
            s_xn.ap().rearrange("p a t -> p (a t)"),
            s_x1p.ap().rearrange("p a t -> p (a t)"), AF.Copy), sm_act)
        wt("tensor", sm_act, mcp)
        if KDBG and s == 0:
            ddump("dbg_x1pre", s_xn.ap().rearrange("p a t -> p (a t)"), sm_act, mcp)
        bk = rot_next()
        bk2 = rot_next()
        st_marks = {}
        for dc in range(8):
            if dc >= 2:
                wt("vector", sm_pe, st_marks[dc - 2])
            mq = op("vector", lambda e, _dc=dc: e.tensor_tensor(
                s_sq[:, _dc % 2, :], s_x1p[:, _dc, :], s_x1p[:, _dc, :], ALU.mult), sm_dve)
            wt("tensor", sm_dve, mq)

            def ft(e, _dc=dc, _b1=bk, _b2=bk2):
                e.matmul(p_rot[:, _b1, 0:SC], s_o128[:, :], s_xn[:, _dc, :],
                         start=(_dc == 0), stop=(_dc == 7))
                return e.matmul(p_rot[:, _b2, 0:SC], s_o128[:, :], s_sq[:, _dc % 2, :],
                                start=(_dc == 0), stop=(_dc == 7))
            st_marks[dc] = op("tensor", ft, sm_pe)
        st_done = sm_pe.c
        wt("vector", sm_pe, st_done)
        base = sm_dve.c
        op("vector", lambda e, _b=bk: e.tensor_scalar_mul(s_rep[:, 0, :], p_rot[:, _b, 0:SC], 1.0 / 1024.0), sm_dve)
        op("vector", lambda e, _b=bk2: e.tensor_scalar_mul(s_lnt[:, :], p_rot[:, _b, 0:SC], 1.0 / 1024.0), sm_dve)
        rot_release(bk, sm_dve, base + 1)
        rot_release(bk2, sm_dve, base + 2)
        wt("vector", sm_dve, base + 2)
        op("vector", lambda e: e.tensor_tensor(s_lnt2[:, :], s_rep[:, 0, :], s_rep[:, 0, :], ALU.mult), sm_dve)
        wt("vector", sm_dve, base + 3)
        m = op("vector", lambda e: e.tensor_sub(s_lnt[:, :], s_lnt[:, :], s_lnt2[:, :]), sm_dve)
        wt("scalar", sm_dve, m)
        m = op("scalar", lambda e: e.activation(s_lnt[:, :], s_lnt[:, :], AF.Ln, bias=1e-5), sm_act)
        wt("scalar", sm_act, m)
        m = op("scalar", lambda e: e.activation(s_lnt[:, :], s_lnt[:, :], AF.Exp, scale=-0.5), sm_act)
        wt("vector", sm_act, m)
        base2 = sm_dve.c
        op("vector", lambda e: e.tensor_tensor(s_rep[:, 1, :], s_rep[:, 0, :], s_lnt[:, :], ALU.mult), sm_dve)
        wt("vector", sm_dve, base2 + 1)
        m = op("vector", lambda e: e.tensor_copy(s_rep[:, 0, :], s_lnt[:, :]), sm_dve)
        wt("vector", sm_dve, base2 + 2)
        base = sm_dve.c
        op("vector", lambda e: e.tensor_tensor(
            s_x1p.ap(), s_x1p.ap(),
            s_rep.ap()[:, 0, :].unsqueeze(1).broadcast_to([P, 8, SC]),
            ALU.mult), sm_dve)
        wt("vector", sm_dve, base + 1)
        m = op("vector", lambda e: e.tensor_tensor(
            s_x1p.ap(), s_x1p.ap(),
            s_rep.ap()[:, 1, :].unsqueeze(1).broadcast_to([P, 8, SC]),
            ALU.subtract), sm_dve)
        wt("scalar", sm_dve, m)
        x1b_done = op("scalar", lambda e: e.activation(
            s_x1b.ap().rearrange("p a t -> p (a t)"),
            s_x1p.ap().rearrange("p a t -> p (a t)"), AF.Copy), sm_act)

        wt("sync", sm_act, x1b_done)

        def fst(e, _s=s):
            d1 = x1b_d.ap()[:, :, _s * HSC:(_s + 1) * HSC].rearrange("a p t -> p a t")
            me_sw(e,
                  lambda ee: ee.dma_start(d1, s_x1b[:, :, 0:HSC]).then_inc(sm_st.h, 16),
                  lambda ee: ee.dma_start(d1, s_x1b[:, :, HSC:SC]).then_inc(sm_st.h, 16))
        emit("sync", fst)
        sm_st.c += 16

        bk = rot_next()
        wt("tensor", sm_act, x1b_done)

        def ft(e, _bk=bk):
            last = None
            for dc in range(8):
                last = e.matmul(p_rot[:, _bk, 0:SC], s_wrt[:, dc, :], s_x1b[:, dc, :],
                                start=(dc == 0), stop=(dc == 7))
            return last
        m = op("tensor", ft, sm_pe)
        wt("scalar", sm_pe, m)
        m = op("scalar", lambda e, _bk=bk: e.activation(
            s_y[0:NEXP, :], p_rot[0:NEXP, _bk, 0:SC], AF.Copy), sm_act)
        rot_release(bk, sm_act, m)
        wt("vector", sm_act, m)
        m = op("vector", lambda e: e.tensor_scalar_add(
            s_y[0:NEXP, :], s_y[0:NEXP, :], s_vecs[0:NEXP, 192:193]), sm_dve)
        bk = rot_next()
        wt("tensor", sm_dve, m)

        def ft2(e, _bk=bk):
            last = None
            for blk in range(2):
                last = e.transpose(p_rot[0:P, _bk, blk * NEXP:(blk + 1) * NEXP],
                                   s_y[0:NEXP, blk * P:(blk + 1) * P], s_id16[0:NEXP, 0:NEXP])
            return last
        m = op("tensor", ft2, sm_pe)
        wt("scalar", sm_pe, m)
        m = op("scalar", lambda e, _bk=bk: e.activation(
            s_lgt.ap().rearrange("p a b -> p (a b)"),
            p_rot[:, _bk, 0:2 * NEXP], AF.Copy), sm_act)
        rot_release(bk, sm_act, m)
        lgt_done = m

        for blk in range(2):
            wt("vector", sm_act, lgt_done)
            m = op("vector", lambda e, _b=blk: e.tensor_reduce(
                s_mx[:, _b, 0:1], s_lgt[:, _b, :], AX.X, ALU.max, negate=True), sm_dve)
            wt("scalar", sm_dve, m)
            m = op("scalar", lambda e, _b=blk: e.activation(
                s_sm[:, _b, :], s_lgt[:, _b, :], AF.Exp, bias=s_mx[:, _b, 0:1]), sm_act)
            wt("vector", sm_act, m)
            m = op("vector", lambda e, _b=blk: e.tensor_reduce(
                s_mx[:, _b, 1:2], s_sm[:, _b, :], AX.X, ALU.add), sm_dve)
            wt("scalar", sm_dve, m)
            m = op("scalar", lambda e, _b=blk: e.activation(
                s_mx[:, _b, 2:3], s_mx[:, _b, 1:2], AF.Ln), sm_act)
            wt("scalar", sm_act, m)
            m = op("scalar", lambda e, _b=blk: e.activation(
                s_mx[:, _b, 2:3], s_mx[:, _b, 2:3], AF.Exp, scale=-1.0), sm_act)
            wt("vector", sm_act, m)
            base = sm_dve.c
            op("vector", lambda e, _b=blk: e.tensor_scalar_mul(
                s_sm[:, _b, :], s_sm[:, _b, :], s_mx[:, _b, 2:3]), sm_dve)
            wt("vector", sm_dve, base + 1)
            op("vector", lambda e, _b=blk: e.tensor_copy(s_wk[:, _b, :], s_sm[:, _b, :]), sm_dve)
            wt("vector", sm_dve, base + 2)
            for it in range(3):
                op("vector", lambda e, _b=blk: e.tensor_reduce(
                    s_mx[:, _b, 3:4], s_wk[:, _b, :], AX.X, ALU.max), sm_dve)
                wt("vector", sm_dve, sm_dve.c)
                if it < 2:
                    op("vector", lambda e, _b=blk: e.tensor_single_scalar(
                        s_msk[:, _b, :], s_wk[:, _b, :], s_mx[:, _b, 3:4], ALU.is_ge), sm_dve)
                    wt("vector", sm_dve, sm_dve.c)
                    op("vector", lambda e, _b=blk: e.copy_predicated(
                        s_wk[:, _b, :], s_msk[:, _b, :], s_negs[:, :]), sm_dve)
                    wt("vector", sm_dve, sm_dve.c)
            op("vector", lambda e, _b=blk: e.tensor_single_scalar(
                s_msk[:, _b, :], s_sm[:, _b, :], s_mx[:, _b, 3:4], ALU.is_ge), sm_dve)
            wt("vector", sm_dve, sm_dve.c)
            op("vector", lambda e, _b=blk: e.memset(s_wk[:, _b, :], 0.0), sm_dve)
            wt("vector", sm_dve, sm_dve.c)
            op("vector", lambda e, _b=blk: e.copy_predicated(
                s_wk[:, _b, :], s_msk[:, _b, :], s_sm[:, _b, :]), sm_dve)
            wt("vector", sm_dve, sm_dve.c)
            op("vector", lambda e, _b=blk: e.tensor_copy(s_cwt[:, _b, :], s_wk[:, _b, :]), sm_dve)
        cw_done = sm_dve.c

        bk = rot_next()
        wt("tensor", sm_dve, cw_done)
        pbf = p_rot.ap().bitcast(BF16)

        def ft3(e, _bk=bk, _v=pbf):
            last = None
            for blk in range(2):
                last = e.transpose(_v[0:NEXP, _bk, blk * P:(blk + 1) * P], s_cwt[:, blk, :], s_idb[:, :])
            return last
        m = op("tensor", ft3, sm_pe)
        wt("scalar", sm_pe, m)

        def fa(e, _bk=bk, _v=pbf):
            last = None
            for blk in range(2):
                last = e.activation(s_cwf[:, blk, :], _v[0:NEXP, _bk, blk * P:(blk + 1) * P], AF.Copy)
            return last
        m = op("scalar", fa, sm_act)
        rot_release(bk, sm_act, m)
        wt("sync", sm_act, m)

        if KDBG and s == 0:
            ddump("dbg_y2", s_y2.ap()[:, 0, :, :].rearrange("p a t -> p (a t)"), sm_dve, sm_dve.c)
            ddump("dbg_x1", s_x1p.ap().rearrange("p a t -> p (a t)"), sm_dve, sm_dve.c)
            ddump("dbg_cw", s_cwf.ap().rearrange("a b c -> a (b c)"), sm_act, sm_act.c)

        def fcw2(e, _s=s):
            d = cw_d.ap()[:, _s * HSC:(_s + 1) * HSC]
            me_sw(e,
                  lambda ee: ee.dma_start(d, s_cwf[:, 0, :]).then_inc(sm_st.h, 16),
                  lambda ee: ee.dma_start(d, s_cwf[:, 1, :]).then_inc(sm_st.h, 16))
        emit("sync", fcw2)
        sm_st.c += 16
        barrier()

    def phaseF(fi):
        x1s_ld = dma("sync", s_x1s.ap(),
                     x1b_d.ap()[:, :, fi * FS:(fi + 1) * FS].rearrange("a p t -> p a t"), sm_ld)
        mz = op("vector", lambda e: e.memset(s_cwo[0:16, :], 1.0), sm_dve)
        wt("sync", sm_dve, mz)
        m = dma("sync", s_cwo[0:NEXP, :], cw_d.ap()[:, fi * FS:(fi + 1) * FS], sm_ld)
        wt("vector", sm_ld, m)
        cwo_done = op("vector", lambda e: e.memset(
            s_racc.ap().rearrange("p a b t -> p (a b t)"), 0.0), sm_dve)

        wt("tensor", sm_dve, cwo_done)
        wt("tensor", sm_ld, m)
        wt("tensor", sm_ld, x1s_ld)
        if KDBG:
            ddump(f"dbg_cwo{fi}", s_cwo.ap()[0:16, :], sm_ld, m)
            ddump(f"dbg_fx1{fi}", s_x1s.ap().rearrange("p a t -> p (a t)"), sm_ld, x1s_ld)

        def ft(e):
            last = None
            for dt in range(6):
                for h in range(2):
                    last = e.matmul(p_dn[:, dt, h, :], s_b2[:, dt * P:(dt + 1) * P],
                                    s_cwo[:, h * HF:(h + 1) * HF], start=(h == 0), stop=False)
            return last
        op("tensor", ft, sm_pe)
        for dt in (6, 7):
            bk = rot_next()
            m = op("tensor", lambda e, _dt=dt, _bk=bk:
                   [e.matmul(p_rot[:, _bk, h * HF:(h + 1) * HF], s_b2[:, _dt * P:(_dt + 1) * P],
                             s_cwo[:, h * HF:(h + 1) * HF], start=(h == 0), stop=(h == 1)) for h in range(2)][-1], sm_pe)
            wt("scalar", sm_pe, m)
            m = op("scalar", lambda e, _dt=dt, _bk=bk:
                   [e.activation(s_racc[:, _dt - 6, h, :], p_rot[:, _bk, h * HF:(h + 1) * HF], AF.Copy)
                    for h in range(2)][-1], sm_act)
            rot_release(bk, sm_act, m)

        w1_marks = {}
        w2_marks = {}
        for ex in range(NE):
            cwr_ld = None
            if ex < NEXP:
                cwr_ld = dma("sync", s_cwr[:, :],
                             bass.AP(cw_d, ex * TOK + fi * FS, [[0, P], [1, 1], [1, FS]]), sm_ld)
            for ht in range(8):
                gidx = ex * 8 + ht
                if gidx >= 2:
                    wt("sync", sm_pe, w1_marks[gidx - 2])
                m = dma("sync", s_rw1[:, gidx % 2, :, :],
                        w_f1_b.ap()[ex, ht].rearrange("dc p m -> p dc m"), sm_ld)
                bk = rot_next()
                wt("tensor", sm_ld, m)

                def ft(e, _ht=ht, _bk=bk, _sl=gidx % 2):
                    last = None
                    for dc in range(8):
                        last = e.matmul(p_rot[:, _bk, :], s_rw1[:, _sl, dc, :], s_x1s[:, dc, :],
                                        start=(dc == 0), stop=(dc == 7))
                    return last
                mmk = op("tensor", ft, sm_pe)
                w1_marks[gidx] = mmk
                wt("scalar", sm_pe, mmk)
                m = op("scalar", lambda e, _e=ex, _ht=ht, _bk=bk:
                       [e.activation(s_hs[:, h, _ht, :], p_rot[:, _bk, h * HF:(h + 1) * HF],
                                     AF.Relu, bias=s_vecs[:, 193 + _ht * 16 + _e:194 + _ht * 16 + _e])
                        for h in range(2)][-1], sm_act)
                rot_release(bk, sm_act, m)
                if ex < NEXP:
                    wt("gpsimd", sm_act, m)
                    if ht == 0:
                        wt("gpsimd", sm_ld, cwr_ld)
                    op("gpsimd", lambda e, _ht=ht:
                       [e.tensor_tensor(s_hs[:, h, _ht, :], s_hs[:, h, _ht, :],
                                        s_cwr.ap().rearrange("p (h t) -> p h t", h=2)[:, h, :], ALU.mult)
                        for h in range(2)][-1], sm_pool)
            hs_pool = sm_pool.c
            hs_act = sm_act.c
            if KDBG and fi == 0 and ex == 0:
                ddump("dbg_hs", s_hs.ap().rearrange("p a b t -> p (a b t)"), sm_pool, hs_pool)
            if ex < NEXP:
                wt("tensor", sm_pool, hs_pool)
            else:
                wt("tensor", sm_act, hs_act)
            for dt in range(8):
                gdx = ex * 8 + dt
                if gdx >= 2:
                    wt("sync", sm_pe, w2_marks[gdx - 2])
                m = dma("sync", s_rw2[:, gdx % 2, :, :],
                        w_f2_b.ap()[ex, dt].rearrange("ht p m -> p ht m"), sm_ld)
                wt("tensor", sm_ld, m)
                if dt < 6:
                    def ft(e, _e=ex, _dt=dt, _sl=gdx % 2):
                        last = None
                        for ht in range(8):
                            for h in range(2):
                                last = e.matmul(p_dn[:, _dt, h, :], s_rw2[:, _sl, ht, :],
                                                s_hs[:, h, ht, :], start=False,
                                                stop=(_e == NE - 1 and ht == 7))
                        return last
                    w2_marks[gdx] = op("tensor", ft, sm_pe)
                else:
                    bk = rot_next()

                    def ft(e, _dt=dt, _bk=bk, _sl=gdx % 2):
                        last = None
                        for ht in range(8):
                            for h in range(2):
                                last = e.matmul(p_rot[:, _bk, h * HF:(h + 1) * HF],
                                                s_rw2[:, _sl, ht, :], s_hs[:, h, ht, :],
                                                start=(ht == 0 and h == 0),
                                                stop=(ht == 7 and h == 1))
                        return last
                    m = op("tensor", ft, sm_pe)
                    w2_marks[gdx] = m
                    wt("vector", sm_pe, m)
                    base = sm_dve.c
                    for h in range(2):
                        op("vector", lambda e, _dt=dt, _bk=bk, _h=h:
                           e.scalar_tensor_tensor(s_racc[:, _dt - 6, _h, :],
                                                  p_rot[:, _bk, _h * HF:(_h + 1) * HF], 1.0,
                                                  s_racc[:, _dt - 6, _h, :], ALU.mult, ALU.add), sm_dve)
                        wt("vector", sm_dve, base + 1 + h)
                    rot_release(bk, sm_dve, base + 2)
        dn_done = sm_pe.c
        if KDBG:
            ddump(f"dbg_racc{fi}", s_racc.ap().rearrange("p a b t -> p (a b t)"), sm_dve, sm_dve.c)
            for dt in range(6):
                wt("scalar", sm_pe, dn_done)
                if dt >= 1:
                    wt("scalar", sm_st, sm_st.c)
                mdx = op("scalar", lambda e, _dt=dt: e.activation(
                    s_hs.ap()[:, 0, 0:2, :].rearrange("p a t -> p (a t)"),
                    p_dn.ap()[:, _dt, :, :].rearrange("p b t -> p (b t)"),
                    AF.Copy), sm_act)
                wt("sync", sm_act, mdx)
                dma("sync", dbg[f"dbg_pdn{fi}"].ap().bitcast(BF16)[:, dt * 2 * HF:(dt + 1) * 2 * HF],
                    s_hs.ap()[:, 0, 0:2, :].rearrange("p a t -> p (a t)"), sm_st)
            wt("scalar", sm_st, sm_st.c)

        for h in range(2):
            x2p_marks = []
            for dt in range(8):
                if dt == 0:
                    wt("vector", sm_pe, dn_done)
                if dt < 6:
                    m = op("vector", lambda e, _dt=dt, _h=h:
                           e.scalar_tensor_tensor(s_x2p[:, _dt, :], p_dn[:, _dt, _h, :], 1.0,
                                                  s_x1s[:, _dt, _h * HF:(_h + 1) * HF],
                                                  ALU.mult, ALU.add), sm_dve)
                else:
                    m = op("vector", lambda e, _dt=dt, _h=h:
                           e.tensor_add(s_x2p[:, _dt, :], s_racc[:, _dt - 6, _h, :],
                                        s_x1s[:, _dt, _h * HF:(_h + 1) * HF]), sm_dve)
                x2p_marks.append(m)
            if KDBG:
                ddump(f"dbg_x2a{fi}{h}", s_x2p.ap().rearrange("p a t -> p (a t)"), sm_dve, sm_dve.c)
            wt("scalar", sm_dve, x2p_marks[7])
            mcp = op("scalar", lambda e: e.activation(
                s_x1s.ap()[:, :, 0:HF].rearrange("p a t -> p a t"),
                s_x2p.ap(), AF.Copy), sm_act)
            wt("tensor", sm_act, mcp)
            bk = rot_next()
            bk2 = rot_next()
            st_marks = {}
            for dc in range(8):
                if dc >= 2:
                    wt("vector", sm_pe, st_marks[dc - 2])
                mq = op("vector", lambda e, _dc=dc: e.tensor_tensor(
                    s_sq[:, _dc % 2, 0:HF], s_x2p[:, _dc, :], s_x2p[:, _dc, :], ALU.mult), sm_dve)
                wt("tensor", sm_dve, mq)

                def ft(e, _dc=dc, _b1=bk, _b2=bk2):
                    e.matmul(p_rot[:, _b1, 0:HF], s_o128[:, :], s_x1s[:, _dc, 0:HF],
                             start=(_dc == 0), stop=(_dc == 7))
                    return e.matmul(p_rot[:, _b2, 0:HF], s_o128[:, :], s_sq[:, _dc % 2, 0:HF],
                                    start=(_dc == 0), stop=(_dc == 7))
                st_marks[dc] = op("tensor", ft, sm_pe)
            st2 = sm_pe.c
            wt("vector", sm_pe, st2)
            base = sm_dve.c
            op("vector", lambda e, _b=bk: e.tensor_scalar_mul(s_r2[:, 0, :], p_rot[:, _b, 0:HF], 1.0 / 1024.0), sm_dve)
            op("vector", lambda e, _b=bk2: e.tensor_scalar_mul(s_lnt[:, 0:HF], p_rot[:, _b, 0:HF], 1.0 / 1024.0), sm_dve)
            rot_release(bk, sm_dve, base + 1)
            rot_release(bk2, sm_dve, base + 2)
            wt("vector", sm_dve, base + 2)
            op("vector", lambda e: e.tensor_tensor(s_lnt2[:, 0:HF], s_r2[:, 0, :], s_r2[:, 0, :], ALU.mult), sm_dve)
            wt("vector", sm_dve, base + 3)
            m = op("vector", lambda e: e.tensor_sub(s_lnt[:, 0:HF], s_lnt[:, 0:HF], s_lnt2[:, 0:HF]), sm_dve)
            wt("scalar", sm_dve, m)
            m = op("scalar", lambda e: e.activation(s_lnt[:, 0:HF], s_lnt[:, 0:HF], AF.Ln, bias=1e-5), sm_act)
            wt("scalar", sm_act, m)
            m = op("scalar", lambda e: e.activation(s_lnt[:, 0:HF], s_lnt[:, 0:HF], AF.Exp, scale=-0.5), sm_act)
            wt("vector", sm_act, m)
            base2 = sm_dve.c
            op("vector", lambda e: e.tensor_tensor(s_r2[:, 1, :], s_r2[:, 0, :], s_lnt[:, 0:HF], ALU.mult), sm_dve)
            wt("vector", sm_dve, base2 + 1)
            m = op("vector", lambda e: e.tensor_copy(s_r2[:, 0, :], s_lnt[:, 0:HF]), sm_dve)
            wt("vector", sm_dve, base2 + 2)
            base = sm_dve.c
            op("vector", lambda e: e.tensor_tensor(
                s_x2p.ap(), s_x2p.ap(),
                s_r2.ap()[:, 0, :].unsqueeze(1).broadcast_to([P, 8, HF]),
                ALU.mult), sm_dve)
            wt("vector", sm_dve, base + 1)
            m = op("vector", lambda e: e.tensor_tensor(
                s_x2p.ap(), s_x2p.ap(),
                s_r2.ap()[:, 1, :].unsqueeze(1).broadcast_to([P, 8, HF]),
                ALU.subtract), sm_dve)
            wt("sync", sm_dve, m)
            if KDBG:
                ddump(f"dbg_x2n{fi}{h}", s_x2p.ap().rearrange("p a t -> p (a t)"), sm_dve, m)
            col = fi * FS + h * HF
            dma("sync", t_out.ap().rearrange("(a p) t -> p a t", p=P)[:, :, col:col + HF],
                s_x2p[:, :, :], sm_st)
            barrier()

    pass
    _lim = os.environ.get("KLIM", "")
    if _lim == "A0":
        prologue()
        phaseA(0)
    elif _lim == "A0NP":
        phaseA(0)
    elif _lim.startswith("N"):
        n = int(_lim[1:])
        prologue()
        for s in range(n):
            phaseA(s)
            phaseS(s)
            phaseO(s)
    elif _lim.startswith("A:"):
        prologue()
        phaseA(0, acut=int(_lim[2:]))
    elif _lim in ("A", "AS", "ASO"):
        prologue()
        phaseA(0)
        if len(_lim) >= 2:
            phaseS(0)
        if len(_lim) >= 3:
            phaseO(0)
    else:
        prologue()
        for s in range(NSC):
            phaseA(s)
            phaseS(s)
            phaseO(s)
        for fi in range(NFS):
            phaseF(fi)

    final_st = sm_st.c
    for e in ("sync", "gpsimd"):
        emit(e, lambda eng, _v=final_st: eng.wait_ge(sm_st.h, _v))

    with ctx:
        with nc.Block() as block:
            @block.sync
            def _(eng):
                for fn in cmds["sync"]:
                    fn(eng)

            @block.tensor
            def _(eng):
                for fn in cmds["tensor"]:
                    fn(eng)

            @block.vector
            def _(eng):
                for fn in cmds["vector"]:
                    fn(eng)

            @block.scalar
            def _(eng):
                for fn in cmds["scalar"]:
                    fn(eng)

            @block.gpsimd
            def _(eng):
                for fn in cmds["gpsimd"]:
                    fn(eng)

    return nc


_NC_CACHE = {}


def kernel(**inputs):
    if "nc" not in _NC_CACHE:
        _NC_CACHE["nc"] = build_nc()
    nc = _NC_CACHE["nc"]
    in_maps = [prep_core_inputs(k, inputs) for k in range(NCORES)]
    res = run_bass_kernel_spmd(nc, in_maps, core_ids=list(range(NCORES)), trace=bool(os.environ.get("KTRACE")))
    _NC_CACHE["res"] = res
    out = np.zeros((B, L, D), np.float32)
    for k in range(NCORES):
        b, m = k // 2, k % 2
        o = np.asarray(res.results[k]["o_x2"])
        for s in range(NSC):
            t0 = s * SC + m * HSC
            out[b, t0:t0 + HSC, :] = o[:, s * HSC:(s + 1) * HSC].T
    return out


# version 07
# speedup vs baseline: 1.2393x; 1.0680x over previous
import os
"""Fused single-launch Trainium2 megakernel for mk_seq_postln_d1 (mamba mixer +
dense-MoE FFN, depth 1, d_model 1024, batch 4, seq 4096).

Sharding: core k -> batch k//2, mamba-channel half k%2, per-SC token half k%2
(each SC of 256 tokens: core m owns columns [m*128, m*128+128)).
Cross-core (pair) exchange via addr_space="Shared" DRAM + flag polling.
Layout: feature-major activations ([feature-chunk on partitions, tokens free]).
Scan: segmented tensor_tensor_scan (16 state lanes per channel row, reset cols).
"""
from contextlib import ExitStack

import numpy as np
import ml_dtypes

import concourse.bass as bass
import concourse.mybir as mybir
from concourse.bass_utils import run_bass_kernel_spmd

F32 = mybir.dt.float32
BF16 = mybir.dt.bfloat16
U32 = mybir.dt.uint32
AF = mybir.ActivationFunctionType
ALU = mybir.AluOpType
AX = mybir.AxisListType

B, L, D, E, N, DR = 4, 4096, 1024, 2048, 16, 64
EH = E // 2
NEXP = 15
NE = 16
SC = 256
NSC = L // SC
HSC = SC // 2
TOK = NSC * HSC
FS = 512
HF = FS // 2
NFS = TOK // FS
SEG = SC + 1
P = 128
MAGIC = 0x5A17C0DE
NCORES = 8

CONSTS = [1e-5, 1.0 / 1024.0, -0.5, -1.0, 1.0, 0.0, 2.0, 0.5]


def prep_core_inputs(core, I):
    b = core // 2
    m = core % 2
    sl = slice(m * EH, (m + 1) * EH)
    d = {}
    d["resid_fm"] = np.ascontiguousarray(np.asarray(I["L0_attn_resid"][b]).T).astype(np.float32)
    for nm, w in (("w_in", I["L0_attn_in_w"]), ("w_skip", I["L0_attn_skip_w"])):
        wt = np.asarray(w)[sl].T.astype(np.float32)
        d[nm] = np.ascontiguousarray(wt.reshape(8, P, 8, P).transpose(2, 0, 1, 3))
    ot = np.asarray(I["L0_attn_out_w"]).T.astype(np.float32)
    oc = np.concatenate([ot[sl], ot[(1 - m) * EH:(2 - m) * EH]], axis=0)
    d["w_out"] = np.ascontiguousarray(oc.reshape(16, P, 8, P).transpose(2, 0, 1, 3))
    wibc = np.zeros((8, P, P), np.float32)
    w = np.asarray(I["L0_attn_wd1"])[:, sl].T.astype(np.float32).reshape(8, P, DR)
    wibc[:, :, 0:64] = w
    wb = np.asarray(I["L0_attn_wB"])[:, sl].T.astype(np.float32).reshape(8, P, N)
    wibc[:, :, 64:80] = wb
    wc = np.asarray(I["L0_attn_wC"])[:, sl].T.astype(np.float32).reshape(8, P, N)
    wibc[:, :, 96:112] = wc
    d["w_ibc"] = np.ascontiguousarray(wibc)
    d["w_wd2"] = np.ascontiguousarray(np.asarray(I["L0_attn_wd2"])[sl].T.astype(np.float32))
    rw = np.zeros((8, P, P), np.float32)
    rw[:, :, 0:NEXP] = np.asarray(I["L0_ffn_router_W"]).astype(np.float32).reshape(8, P, NEXP)
    d["w_rt"] = np.ascontiguousarray(rw)
    e1 = np.concatenate([np.asarray(I["L0_ffn_eW1"]), np.asarray(I["L0_ffn_sW1"])[None]], 0).astype(np.float32)
    e2 = np.concatenate([np.asarray(I["L0_ffn_eW2"]), np.asarray(I["L0_ffn_sW2"])[None]], 0).astype(np.float32)
    d["w_f1"] = np.ascontiguousarray(e1.reshape(NE, 8, P, 8, P).transpose(0, 3, 1, 2, 4))
    d["w_f2"] = np.ascontiguousarray(e2.reshape(NE, 8, P, 8, P).transpose(0, 3, 1, 2, 4))
    d["b2cat"] = np.ascontiguousarray(
        np.concatenate([np.asarray(I["L0_ffn_eb2"]), np.asarray(I["L0_ffn_sb2"])[None]], 0).astype(np.float32))
    cols = []

    def pp(v):
        return np.asarray(v).astype(np.float32)[sl].reshape(8, P).T

    cols.append(pp(I["L0_attn_conv_b"]))
    cols.append(pp(I["L0_attn_W_D"]))
    cols.append(pp(I["L0_attn_wd2_b"]))
    cw = np.asarray(I["L0_attn_conv_w"]).astype(np.float32)[sl]
    cols.append(cw.reshape(8, P, 4).transpose(1, 0, 2).reshape(P, 32))
    cols.append(np.asarray(I["L0_attn_norm_w"]).astype(np.float32).reshape(8, P).T)
    al = np.asarray(I["L0_attn_A_log"]).astype(np.float32)[sl]
    cols.append(al.reshape(8, P, N).transpose(1, 0, 2).reshape(P, 128))
    rb = np.zeros((P, 1), np.float32)
    rb[:NEXP, 0] = np.asarray(I["L0_ffn_router_b"]).astype(np.float32)
    cols.append(rb)
    eb1 = np.concatenate([np.asarray(I["L0_ffn_eb1"]), np.asarray(I["L0_ffn_sb1"])[None]], 0).astype(np.float32)
    cols.append(eb1.reshape(NE, 8, P).transpose(2, 1, 0).reshape(P, 128))
    cols.append(np.ones((P, 1), np.float32))
    d["vecs"] = np.ascontiguousarray(np.concatenate(cols, axis=1))
    d["consts"] = np.broadcast_to(np.array(CONSTS, np.float32), (P, len(CONSTS))).copy()
    d["id128b"] = np.eye(P, dtype=np.float32)
    d["id16"] = np.eye(16, dtype=np.float32)
    return d


def build_nc():
    nc = bass.Bass("TRN2", target_bir_lowering=False)

    di = lambda n, sh, dt=F32: nc.dram_tensor(n, sh, dt, kind="ExternalInput")
    t_res = di("resid_fm", [D, L])
    t_win = di("w_in", [8, 8, P, P])
    t_wsk = di("w_skip", [8, 8, P, P])
    t_wout = di("w_out", [8, 16, P, P])
    t_wibc = di("w_ibc", [8, P, P])
    t_wd2 = di("w_wd2", [DR, EH])
    t_wrt = di("w_rt", [8, P, P])
    t_wf1 = di("w_f1", [NE, 8, 8, P, P])
    t_wf2 = di("w_f2", [NE, 8, 8, P, P])
    t_b2 = di("b2cat", [16, D])
    t_vecs = di("vecs", [P, 322])
    t_consts = di("consts", [P, 8])
    t_idb = di("id128b", [P, P])
    t_id16 = di("id16", [16, 16])

    t_out = nc.dram_tensor("o_x2", [D, TOK], F32, kind="ExternalOutput")
    KDBG = bool(os.environ.get("KDEBUG"))
    if KDBG:
        dbg = {}
        for nm, sh in (("dbg_xn", [P, 8 * SC]), ("dbg_x", [P, 8 * SC]), ("dbg_dl", [P, 8 * SC]),
                       ("dbg_ibc", [112, SC]), ("dbg_a", [P, N * SEG]), ("dbg_u", [P, N * SEG]),
                       ("dbg_h", [P, N * SEG]), ("dbg_y2", [P, 8 * SC]), ("dbg_x1", [P, 8 * SC]),
                       ("dbg_cw", [NEXP, 2 * P]), ("dbg_sg", [P, 8 * SC]), ("dbg_zd", [P, 8 * SC]),
                       ("dbg_y2p", [P, 8 * SC]), ("dbg_x1pre", [P, 8 * SC]), ("dbg_wo", [P, 16 * P]),
                       ("dbg_hs", [P, 8 * HF])):
            dbg[nm] = nc.dram_tensor(nm, sh, F32, kind="ExternalOutput")
        for fi in range(4):
            for nm, sh in ((f"dbg_cwo{fi}", [16, FS]), (f"dbg_fx1{fi}", [P, 4 * FS]),
                           (f"dbg_racc{fi}", [P, 4 * HF]), (f"dbg_pdn{fi}", [P, 6 * 2 * HF])):
                dbg[nm] = nc.dram_tensor(nm, sh, F32, kind="ExternalOutput")
            for h in range(2):
                for nm, sh in ((f"dbg_x2a{fi}{h}", [P, 8 * HF]), (f"dbg_x2n{fi}{h}", [P, 8 * HF])):
                    dbg[nm] = nc.dram_tensor(nm, sh, F32, kind="ExternalOutput")
        dbg["dbg_x1fd"] = nc.dram_tensor("dbg_x1fd", [P, 8 * TOK], F32, kind="ExternalOutput")

    ds = lambda n, sh, dt: nc.dram_tensor(n, sh, dt)
    w_in_b = ds("w_in_b", [8, 8, P, P], BF16)
    w_sk_b = ds("w_sk_b", [8, 8, P, P], BF16)
    w_out_b = ds("w_out_b", [8, 16, P, P], BF16)
    w_f1_b = ds("w_f1_b", [NE, 8, 8, P, P], BF16)
    w_f2_b = ds("w_f2_b", [NE, 8, 8, P, P], BF16)
    x1b_d = ds("x1b_d", [8, P, TOK], BF16)
    rows_d = ds("rows_d", [4, 4096], F32)
    bc_d = ds("bc_d", [2, 16, L], BF16)
    cw_d = ds("cw_d", [NEXP, TOK], BF16)
    ln2_d = ds("ln2_d", [2, TOK], F32)

    sh_flags = nc.dram_tensor("sh_flags", [2, 64], U32, addr_space="Shared")
    sh_ibc = nc.dram_tensor("sh_ibc", [2, NSC, 112, SC], F32, addr_space="Shared")
    sh_y2 = nc.dram_tensor("sh_y2", [2, NSC, P, 8 * SC], BF16, addr_space="Shared")
    FL_HELLO, FL_IBC, FL_Y2 = 0, 1, 20

    ctx = ExitStack()
    sb = lambda n, sh, dt=F32: ctx.enter_context(nc.sbuf_tensor(n, sh, dt))

    s_vecs = sb("s_vecs", [P, 322])
    s_consts = sb("s_consts", [P, 8])
    for v, i in zip(CONSTS, range(len(CONSTS))):
        nc.const_aps.register(F32, v, s_consts[:, i:i + 1])
    s_idb = sb("s_idb", [P, P], BF16)
    s_id16 = sb("s_id16", [16, 16])
    s_A = sb("s_A", [P, 8, N])
    s_wibc = sb("s_wibc", [P, 8, P], BF16)
    s_wd2 = sb("s_wd2", [DR, EH], BF16)
    s_wrt = sb("s_wrt", [P, 8, P], BF16)
    s_b2 = sb("s_b2", [16, D], BF16)

    s_resid = sb("s_resid", [P, 1, 8, SC])
    s_sq = sb("s_sq", [P, 2, SC], BF16)
    s_onesb = sb("s_onesb", [P, 1], BF16)
    s_o128 = sb("s_o128", [P, P], BF16)
    s_lnt = sb("s_lnt", [P, SC])
    s_rows = sb("s_rows", [1, 64])
    s_rep = sb("s_rep", [P, 2, SC])
    s_xn = sb("s_xn", [P, 8, SC], BF16)
    s_xpre = sb("s_xpre", [P, 8, SC + 3], BF16)
    s_xtail = sb("s_xtail", [P, 8, 3], BF16)
    s_cacc = sb("s_cacc", [P, 2, SC])
    s_x = sb("s_x", [P, 1, 8, SC], BF16)
    s_sg = sb("s_sg", [P, 1, 8, SC], BF16)
    s_zd = sb("s_zd", [P, 1, 8, SC], BF16)
    _ci_bf = None
    s_dl = sb("s_dl", [P, 8, SC], BF16)
    s_w = sb("s_w", [P, 8, SC], BF16)
    s_ibc = sb("s_ibc", [112, SC])
    s_ibc2 = sb("s_ibc2", [112, SC])
    s_innb = sb("s_innb", [64, SC], BF16)
    s_bcbB = sb("s_bcbB", [16, SC], BF16)
    s_bcbC = sb("s_bcbC", [16, SC], BF16)
    s_Brep = sb("s_Brep", [P, 2, N, HSC], BF16)
    s_Crep = sb("s_Crep", [P, 2, N, HSC], BF16)
    s_a = sb("s_a", [P, 2, N, SEG])
    s_uh = sb("s_uh", [P, 3, N, SEG], BF16)
    s_hc = sb("s_hc", [P, 8, N])
    s_y = sb("s_y", [P, SC])
    s_g1 = sb("s_g1", [P, SC])
    s_lnt2 = s_g1
    s_lg = None
    s_y2 = sb("s_y2", [P, 1, 8, SC], BF16)
    s_y2p = sb("s_y2p", [P, 8, SC], BF16)
    s_x1p = sb("s_x1p", [P, 8, SC])
    s_x1b = sb("s_x1b", [P, 8, SC], BF16)
    s_lgt = sb("s_lgt", [P, 2, NEXP])
    s_sm = sb("s_sm", [P, 2, NEXP])
    s_wk = sb("s_wk", [P, 2, NEXP])
    s_mx = sb("s_mx", [P, 2, 4])
    s_msk = sb("s_msk", [P, 2, NEXP], U32)
    s_negs = sb("s_negs", [P, NEXP])
    s_cwt = sb("s_cwt", [P, 2, NEXP], BF16)
    s_cwf = sb("s_cwf", [NEXP, 2, P], BF16)
    s_rin = sb("s_rin", [P, 1, 8, P], BF16)
    s_rsk = sb("s_rsk", [P, 1, 8, P], BF16)
    s_rot = sb("s_rot", [P, 1, 16, P], BF16)
    s_rw1 = sb("s_rw1", [P, 2, 8, P], BF16)
    s_rw2 = sb("s_rw2", [P, 2, 8, P], BF16)
    s_x1s = sb("s_x1s", [P, 8, FS], BF16)
    s_hs = sb("s_hs", [P, 2, 8, HF], BF16)
    s_racc = sb("s_racc", [P, 2, 2, HF])
    s_cwr = sb("s_cwr", [P, FS], BF16)
    s_cwo = sb("s_cwo", [16, FS], BF16)
    s_x2p = sb("s_x2p", [P, 8, HF])

    s_r2 = sb("s_r2", [P, 2, HF])
    s_ci = sb("s_ci", [P, 2, 256])
    s_co = sb("s_co", [P, 2, 256], BF16)
    _cibf = s_ci.ap().rearrange("p a b -> p (a b)").bitcast(BF16)

    s_flag = sb("s_flag", [1, 4], U32)
    s_sp1t = sb("s_sp1t", [P, 8, SC], BF16)
    s_one = sb("s_one", [1, 68], U32)

    p_dn = ctx.enter_context(nc.psum_tensor("p_dn", [P, 6, 2, HF], F32))
    p_rot = ctx.enter_context(nc.psum_tensor("p_rot", [P, 2, FS], F32))

    class Sem:
        def __init__(self, name):
            self.h = ctx.enter_context(nc.semaphore(name))
            self.c = 0

    sm_w = Sem("sm_w")
    sm_ld = Sem("sm_ld")
    sm_ex = Sem("sm_ex")
    sm_st = Sem("sm_st")
    sm_po = Sem("sm_po")
    sm_pe = Sem("sm_pe")
    sm_act = Sem("sm_act")
    sm_dve = Sem("sm_dve")
    sm_pool = Sem("sm_pool")
    sm_bar = Sem("sm_bar")

    cmds = {e: [] for e in ("sync", "tensor", "vector", "scalar", "gpsimd")}

    def emit(eng, fn):
        cmds[eng].append(fn)

    def wt(eng, sem, val):
        emit(eng, lambda e, _s=sem.h, _v=val: e.wait_ge(_s, _v))

    def op(eng, make, sem=None, n=1):
        if sem is not None:
            sem.c += n

        def f(e, _mk=make, _sem=sem, _n=n):
            ins = _mk(e)
            if _sem is not None:
                ins.then_inc(_sem.h, _n)
        emit(eng, f)
        return sem.c if sem is not None else None

    def dma(eng, out_ap, in_ap, sem):
        return op(eng, lambda e, _o=out_ap, _i=in_ap: e.dma_start(_o, _i), sem, 16)

    ENGS = ("tensor", "vector", "scalar", "gpsimd", "sync")

    def barrier():
        tgt = sm_bar.c + len(ENGS)
        st_now = sm_st.c
        for e in ENGS:
            def f(eng, _t=tgt, _st=st_now):
                eng.wait_ge(sm_st.h, _st)
                eng.drain().then_inc(sm_bar.h, 1)
                eng.wait_ge(sm_bar.h, _t)
            emit(e, f)
        sm_bar.c = tgt

    R = {}

    def sync_setup(eng):
        for nm in ("me", "pn", "pc", "fl", "fl2", "ne"):
            R[nm] = eng.alloc_register("r_" + nm)
        pid = eng.partition_id()
        eng.reg_mov(R["me"], pid)
        eng.reg_mod(R["me"], R["me"], 2)
        eng.reg_mov(R["pn"], 1)
        eng.reg_sub(R["pn"], R["pn"], R["me"])
        eng.reg_mov(R["pc"], 0)
    emit("sync", sync_setup)

    def me_sw(e, fn0, fn1):
        with e.If_eq(R["me"], 0):
            fn0(e)
        with e.Else():
            fn1(e)

    def flag_write(eng, slot):
        me_sw(eng,
              lambda e: e.dma_start(sh_flags.ap()[0:1, slot:slot + 1], s_one[0:1, 0:1]).then_inc(sm_ex.h, 16),
              lambda e: e.dma_start(sh_flags.ap()[1:2, slot:slot + 1], s_one[0:1, 0:1]).then_inc(sm_ex.h, 16))

    def flag_poll(eng, slot):
        eng.reg_mov(R["ne"], 1)
        with eng.While(R["ne"]):
            eng.reg_add(R["pc"], R["pc"], 16)
            eng.dma_start(s_flag[0:1, 0:2],
                          bass.AP(sh_flags, slot, [[1, 1], [64, 2], [1, 1]]),
                          allow_slow_non_contiguous=True).then_inc(sm_po.h, 16)
            eng.wait_ge(sm_po.h, R["pc"])
            eng.reg_load(R["fl"], s_flag[0:1, 0:1])
            eng.reg_load(R["fl2"], s_flag[0:1, 1:2])
            eng.reg_mul(R["fl"], R["fl"], R["me"])
            eng.reg_mul(R["fl2"], R["fl2"], R["pn"])
            eng.reg_add(R["fl"], R["fl"], R["fl2"])
            eng.reg_sub(R["ne"], R["fl"], MAGIC)

    def ddump(name, src_ap, sem, mark):
        if not KDBG:
            return
        wt("sync", sem, mark)
        nr = src_ap.partition_size()
        nf = src_ap.free_size()
        t = dbg[name]
        if src_ap.dtype == BF16:
            dst = t.ap().bitcast(BF16)[0:nr, 0:nf]
        else:
            dst = t.ap()[0:nr, 0:nf]
        dma("sync", dst, src_ap, sm_st)

    rot_i = [0]
    rot_guard = [None, None]

    def rot_next():
        b = rot_i[0] % 2
        rot_i[0] += 1
        if rot_guard[b] is not None:
            gsem, gval = rot_guard[b]
            wt("tensor", gsem, gval)
        return b

    def rot_release(b, sem, val):
        rot_guard[b] = (sem, val)

    def prologue():
        dma("sync", s_vecs[:], t_vecs[:, :], sm_ld)
        dma("sync", s_consts[:], t_consts[:, :], sm_ld)
        dma("sync", s_id16[:], t_id16[:, :], sm_ld)
        small_jobs = [(t_idb[:, :], s_idb[:, :], P)]
        for a in range(8):
            small_jobs.append((t_wibc.ap()[a], s_wibc[:, a, :], P))
            small_jobs.append((t_wrt.ap()[a], s_wrt[:, a, :], P))
        for c0 in range(0, EH, 256):
            small_jobs.append((t_wd2[:, c0:c0 + 256], s_wd2[:, c0:c0 + 256], 64))
            small_jobs.append((t_b2[:, c0:c0 + 256], s_b2[:, c0:c0 + 256], 16))
        for (src, dst, npart) in small_jobs:
            cw = src.shape[-1]
            m = dma("sync", s_ci[0:npart, 0, 0:cw], src, sm_ld)
            wt("scalar", sm_ld, m)
            m = op("scalar", lambda e, _d=dst, _cw=cw, _np=npart:
                   e.activation(_d, s_ci[0:_np, 0, 0:_cw], AF.Copy), sm_act)
            wt("sync", sm_act, m)
        m = op("scalar", lambda e: e.activation(
            s_A.ap().rearrange("p a b -> p (a b)"), s_vecs[:, 64:192], AF.Exp), sm_act)
        wt("vector", sm_act, m)

        def fv(e):
            e.tensor_scalar_mul(s_A.ap().rearrange("p a b -> p (a b)"),
                                s_A.ap().rearrange("p a b -> p (a b)"), -1.0)
            e.memset(s_onesb[:], 1.0)
            e.memset(s_o128[:, :], 1.0)
            e.memset(s_one[0:1, 0:1], MAGIC)
            e.memset(s_one[0:1, 1:68], 0)
            e.memset(s_negs[:], -1.0)
            e.memset(s_xtail.ap().rearrange("p a b -> p (a b)"), 0.0)
            e.memset(s_hc.ap().rearrange("p a b -> p (a b)"), 0.0)
            return e.memset(s_rows[:], 0.0)
        m = op("vector", fv, sm_dve)
        wt("sync", sm_dve, m)

        ex0 = sm_ex.c

        if not os.environ.get("NOHELLO"):
            def f(e, _b=ex0):
                me_sw(e,
                      lambda ee: ee.dma_start(sh_flags.ap()[0:1, 0:64], s_one[0:1, 4:68]).then_inc(sm_ex.h, 16),
                      lambda ee: ee.dma_start(sh_flags.ap()[1:2, 0:64], s_one[0:1, 4:68]).then_inc(sm_ex.h, 16))
                e.wait_ge(sm_ex.h, _b + 16)
                flag_write(e, FL_HELLO)
                e.wait_ge(sm_ex.h, _b + 32)
                flag_poll(e, FL_HELLO)
            emit("sync", f)
            sm_ex.c += 32

        CH = 256
        store_marks = [0, 0]
        _jobs = [
            (t_win, w_in_b, 8 * 8 * P * P),
            (t_wsk, w_sk_b, 8 * 8 * P * P),
            (t_wout, w_out_b, 8 * 16 * P * P),
            (t_wf1, w_f1_b, NE * 8 * 8 * P * P),
            (t_wf2, w_f2_b, NE * 8 * 8 * P * P),
        ]
        if os.environ.get("NOCAST"):
            _jobs = _jobs[:2]
        for (src, dst, nel) in _jobs:
            nch = nel // (P * CH)
            for c in range(nch):
                par = c % 2
                if store_marks[par]:
                    wt("sync", sm_w, store_marks[par])
                m = dma("sync", s_ci[:, par, :],
                        bass.AP(src, c * P * CH, [[CH, P], [1, CH]]), sm_w)
                wt("scalar", sm_w, m)
                cm = op("scalar", lambda e, _p=par: e.activation(s_co[:, _p, :], s_ci[:, _p, :], AF.Copy), sm_act)
                wt("sync", sm_act, cm)
                store_marks[par] = dma("sync", bass.AP(dst, c * P * CH, [[CH, P], [1, CH]]),
                                       s_co[:, par, :], sm_w)
        wt("sync", sm_w, sm_w.c)
        barrier()

    def phaseA(s, acut=99):
        slot3 = 0
        par = 0
        rm = dma("sync", s_resid[:, slot3, :, :],
                 t_res.ap().rearrange("(dc p) t -> p dc t", p=P)[:, :, s * SC:(s + 1) * SC], sm_ld)
        wt("vector", sm_ld, rm)

        bk = rot_next()
        st_marks = {}
        for dc in range(8):
            if dc >= 2:
                wt("vector", sm_pe, st_marks[dc - 2])
            mq = op("vector", lambda e, _sl=slot3, _dc=dc:
                    e.tensor_tensor(s_sq[:, _dc % 2, :], s_resid[:, _sl, _dc, :],
                                    s_resid[:, _sl, _dc, :], ALU.mult), sm_dve)
            wt("tensor", sm_dve, mq)
            st_marks[dc] = op("tensor", lambda e, _dc=dc, _bk=bk:
               e.matmul(p_rot[:, _bk, 0:SC], s_o128[:, :], s_sq[:, _dc % 2, :],
                        start=(_dc == 0), stop=(_dc == 7)), sm_pe)
        wt("scalar", sm_pe, sm_pe.c)

        def fa(e, _bk=bk):
            return e.activation(s_lnt[:, :], p_rot[:, _bk, 0:SC], AF.Ln,
                                scale=1.0 / 1024.0, bias=1e-5)
        m = op("scalar", fa, sm_act)
        rot_release(bk, sm_act, m)
        wt("scalar", sm_act, m)
        m = op("scalar", lambda e: e.activation(s_rep[:, 0, :], s_lnt[:, :], AF.Exp, scale=-0.5), sm_act)
        if acut <= 1:
            return
        wt("vector", sm_act, m)
        for dc in range(8):
            op("vector", lambda e, _sl=slot3, _dc=dc:
               e.scalar_tensor_tensor(s_xn[:, _dc, :], s_resid[:, _sl, _dc, :],
                                      s_vecs[:, 56 + _dc:57 + _dc], s_rep[:, 0, :],
                                      ALU.mult, ALU.mult), sm_dve)
        xn_done = sm_dve.c
        if acut <= 2:
            return

        evac_marks = {}
        for et in range(8):
            if et >= 1:
                wt("sync", sm_act, evac_marks[et - 1])
            mia = dma("sync", s_rin[:, 0, 0:4, :],
                      w_in_b.ap()[et, 0:4].rearrange("dc p m -> p dc m"), sm_ld)
            msa = dma("sync", s_rsk[:, 0, 0:4, :],
                      w_sk_b.ap()[et, 0:4].rearrange("dc p m -> p dc m"), sm_ld)
            mib = dma("sync", s_rin[:, 0, 4:8, :],
                      w_in_b.ap()[et, 4:8].rearrange("dc p m -> p dc m"), sm_ld)
            msb = dma("sync", s_rsk[:, 0, 4:8, :],
                      w_sk_b.ap()[et, 4:8].rearrange("dc p m -> p dc m"), sm_ld)
            bki = rot_next()
            bks = rot_next()
            if et == 0:
                wt("tensor", sm_dve, xn_done)

            def fthalf(e, _b, _w, _h):
                last = None
                for dc in range(_h * 4, _h * 4 + 4):
                    last = e.matmul(p_rot[:, _b, 0:SC], _w[:, 0, dc, :], s_xn[:, dc, :],
                                    start=(dc == 0), stop=(dc == 7))
                return last
            wt("tensor", sm_ld, mia)
            op("tensor", lambda e, _b=bki: fthalf(e, _b, s_rin, 0), sm_pe)
            wt("tensor", sm_ld, msa)
            op("tensor", lambda e, _b=bks: fthalf(e, _b, s_rsk, 0), sm_pe)
            wt("tensor", sm_ld, mib)
            m1 = op("tensor", lambda e, _b=bki: fthalf(e, _b, s_rin, 1), sm_pe)
            wt("tensor", sm_ld, msb)
            m2 = op("tensor", lambda e, _b=bks: fthalf(e, _b, s_rsk, 1), sm_pe)
            wt("scalar", sm_pe, m2)
            mi = op("scalar", lambda e, _et=et, _bi=bki:
                    e.activation(s_xpre[:, _et, 3:3 + SC], p_rot[:, _bi, 0:SC], AF.Copy), sm_act)
            rot_release(bki, sm_act, mi)
            em = op("scalar", lambda e, _et=et, _bs=bks, _par=par:
                    e.activation(s_sg[:, _par, _et, :], p_rot[:, _bs, 0:SC], AF.Silu), sm_act)
            rot_release(bks, sm_act, em)
            evac_marks[et] = em
        if acut <= 3:
            return
        wt("vector", sm_act, sm_act.c)
        m = op("vector", lambda e: e.tensor_copy(s_xpre[:, :, 0:3], s_xtail[:]), sm_dve)
        wt("vector", sm_dve, m)

        silu_marks = {}
        for et in range(8):
            cp = et % 2
            if et >= 1:
                wt("vector", sm_act, silu_marks[et - 1])
            base = sm_dve.c
            op("vector", lambda e, _et=et, _cp=cp:
               e.tensor_scalar(s_cacc[:, _cp, :], s_xpre[:, _et, 0:SC],
                               s_vecs[:, 24 + _et * 4:25 + _et * 4],
                               s_vecs[:, 0 + _et:1 + _et], ALU.mult, ALU.add), sm_dve)
            for k in (1, 2, 3):
                wt("vector", sm_dve, base + k)
                dstp = cp if k % 2 == 0 else 1 - cp
                srcp = 1 - dstp
                op("vector", lambda e, _et=et, _k=k, _d=dstp, _s=srcp:
                   e.scalar_tensor_tensor(s_cacc[:, _d, :], s_xpre[:, _et, _k:_k + SC],
                                          s_vecs[:, 24 + _et * 4 + _k:25 + _et * 4 + _k],
                                          s_cacc[:, _s, :], ALU.mult, ALU.add), sm_dve)
            cv = sm_dve.c
            wt("scalar", sm_dve, cv)
            silu_marks[et] = op("scalar", lambda e, _et=et, _par=par:
               e.activation(s_x[:, _par, _et, :], s_cacc[:, 1 - (_et % 2), :], AF.Silu), sm_act)
        x_done = sm_act.c
        wt("vector", sm_act, x_done)
        op("vector", lambda e: e.tensor_copy(s_xtail[:], s_xpre[:, :, SC:SC + 3]), sm_dve)

        if acut <= 4:
            return
        wt("tensor", sm_act, x_done)
        bk2 = rot_next()

        def ft(e, _bk=bk2, _par=par):
            last = None
            for et in range(8):
                last = e.matmul(p_rot[:, _bk, 0:SC], s_wibc[:, et, :], s_x[:, _par, et, :],
                                start=(et == 0), stop=(et == 7))
            return last
        m = op("tensor", ft, sm_pe)
        wt("scalar", sm_pe, m)
        for (mm_, off) in ((DR, 0), (N, 64), (N, 96)):
            mev = op("scalar", lambda e, _mm=mm_, _off=off, _bk=bk2:
               e.activation(s_ibc[_off:_off + _mm, :], p_rot[_off:_off + _mm, _bk, 0:SC], AF.Copy), sm_act)
        rot_release(bk2, sm_act, mev)
        ibc_evac = sm_act.c

        if acut <= 5:
            return
        wt("sync", sm_act, ibc_evac)
        ex0 = sm_ex.c

        def f(e, _s=s, _b=ex0):
            me_sw(e,
                  lambda ee: ee.dma_start(sh_ibc.ap()[0, _s], s_ibc[:, :]).then_inc(sm_ex.h, 16),
                  lambda ee: ee.dma_start(sh_ibc.ap()[1, _s], s_ibc[:, :]).then_inc(sm_ex.h, 16))
            e.wait_ge(sm_ex.h, _b + 16)
            flag_write(e, FL_IBC + _s)
            flag_poll(e, FL_IBC + _s)
            me_sw(e,
                  lambda ee: ee.dma_start(s_ibc2[:, :], sh_ibc.ap()[1, _s]).then_inc(sm_ex.h, 16),
                  lambda ee: ee.dma_start(s_ibc2[:, :], sh_ibc.ap()[0, _s]).then_inc(sm_ex.h, 16))
        emit("sync", f)
        sm_ex.c += 48
        wt("vector", sm_ex, sm_ex.c)
        m = op("vector", lambda e: e.tensor_add(s_ibc[:, :], s_ibc[:, :], s_ibc2[:, :]), sm_dve)
        wt("scalar", sm_dve, m)
        op("scalar", lambda e: e.activation(s_bcbB[:, :], s_ibc[64:80, :], AF.Copy), sm_act)
        op("scalar", lambda e: e.activation(s_bcbC[:, :], s_ibc[96:112, :], AF.Copy), sm_act)
        m = op("scalar", lambda e: e.activation(s_innb[:, :], s_ibc[0:64, :], AF.Copy), sm_act)
        bc_cast = m

        if acut <= 6:
            return
        wt("sync", sm_act, bc_cast)
        dma("sync", bc_d.ap()[0, :, s * SC:(s + 1) * SC], s_bcbB[:, :], sm_ld)
        m = dma("sync", bc_d.ap()[1, :, s * SC:(s + 1) * SC], s_bcbC[:, :], sm_ld)
        wt("sync", sm_ld, m)
        for h in range(2):
            dma("sync", s_Brep[:, h, :, :],
                bc_d.ap()[0, :, s * SC + h * HSC:s * SC + (h + 1) * HSC]
                .unsqueeze(0).broadcast_to([P, N, HSC]), sm_ld)
            dma("sync", s_Crep[:, h, :, :],
                bc_d.ap()[1, :, s * SC + h * HSC:s * SC + (h + 1) * HSC]
                .unsqueeze(0).broadcast_to([P, N, HSC]), sm_ld)
        bc_ready = sm_ld.c

        if acut <= 7:
            return
        wt("tensor", sm_act, bc_cast)
        for et in range(8):
            bb = rot_next()
            m = op("tensor", lambda e, _et=et, _bb=bb:
                   e.matmul(p_rot[:, _bb, 0:SC], s_wd2[:, _et * P:(_et + 1) * P],
                            s_innb[:, :], start=True, stop=True), sm_pe)
            wt("scalar", sm_pe, m)
            em = op("scalar", lambda e, _et=et, _par=par, _bb=bb:
                    e.activation(s_zd[:, _par, _et, :], p_rot[:, _bb, 0:SC], AF.Copy), sm_act)
            rot_release(bb, sm_act, em)
        zd_done = sm_act.c

        if acut <= 8:
            return
        wt("scalar", sm_act, zd_done)
        for et in range(8):
            op("scalar", lambda e, _et=et, _par=par:
               e.activation(s_sp1t[:, _et, :], s_zd[:, _par, _et, :], AF.Abs,
                            bias=s_vecs[:, 16 + _et:17 + _et], scale=1.0), sm_act)
        m = sm_act.c
        wt("scalar", sm_act, m)
        m = op("scalar", lambda e: e.activation(
            s_sp1t.ap().rearrange("p a b -> p (a b)"),
            s_sp1t.ap().rearrange("p a b -> p (a b)"), AF.Exp, scale=-1.0), sm_act)
        wt("scalar", sm_act, m)
        m = op("scalar", lambda e: e.activation(
            s_sp1t.ap().rearrange("p a b -> p (a b)"),
            s_sp1t.ap().rearrange("p a b -> p (a b)"), AF.Ln, bias=1.0), sm_act)
        sp_done = m
        wt("vector", sm_act, sp_done)
        for et in range(8):
            op("vector", lambda e, _et=et, _par=par:
               e.tensor_scalar(s_dl[:, _et, :], s_zd[:, _par, _et, :],
                               s_vecs[:, 16 + _et:17 + _et], 0.0, ALU.add, ALU.max), sm_dve)
        m = sm_dve.c
        wt("vector", sm_dve, m)
        m = op("vector", lambda e: e.tensor_add(
            s_dl.ap().rearrange("p a b -> p (a b)"),
            s_dl.ap().rearrange("p a b -> p (a b)"),
            s_sp1t.ap().rearrange("p a b -> p (a b)")), sm_dve)
        wt("vector", sm_dve, m)
        op("vector", lambda e, _par=par: e.tensor_tensor(
            s_w.ap().rearrange("p a b -> p (a b)"),
            s_dl.ap().rearrange("p a b -> p (a b)"),
            s_x.ap()[:, _par, :, :].rearrange("p a b -> p (a b)"), ALU.mult), sm_dve)
        wt("vector", sm_ld, bc_ready)
        if KDBG and s == 0:
            ddump("dbg_xn", s_xn.ap().rearrange("p a t -> p (a t)"), sm_dve, sm_dve.c)
            ddump("dbg_x", s_x.ap()[:, 0, :, :].rearrange("p a t -> p (a t)"), sm_dve, sm_dve.c)
            ddump("dbg_dl", s_dl.ap().rearrange("p a t -> p (a t)"), sm_dve, sm_dve.c)
            ddump("dbg_ibc", s_ibc.ap(), sm_dve, sm_dve.c)
            ddump("dbg_zd", s_zd.ap()[:, 0, :, :].rearrange("p a t -> p (a t)"), sm_dve, sm_dve.c)
            ddump("dbg_sg", s_sg.ap()[:, 0, :, :].rearrange("p a t -> p (a t)"), sm_act, sm_act.c)
        barrier()

    def phaseS(s):
        par = 0
        tree_marks = {}
        scan_marks = {}
        for et in range(8):
            i = s * 8 + et
            ap_ = et % 2
            u_sl = (2 * i) % 3
            h_sl = (2 * i + 1) % 3
            if et >= 2:
                wt("scalar", sm_dve, scan_marks[et - 2])

            def fa(e, _et=et, _ap=ap_):
                e.memzero(s_a[:, _ap, :, 0:1])
                last = None
                for n in range(N):
                    last = e.activation(s_a[:, _ap, n, 1:SEG], s_dl[:, _et, :], AF.Exp,
                                        scale=s_A[:, _et, n:n + 1])
                return last
            a_done = op("scalar", fa, sm_act)

            if et >= 2:
                wt("gpsimd", sm_dve, tree_marks[et - 2])

            def fp(e, _et=et, _u=u_sl, _par=par):
                e.tensor_copy(s_uh[:, _u, :, 0:1], s_hc.ap()[:, _et, :].unsqueeze(2))
                last = None
                for h in range(2):
                    last = e.tensor_tensor(
                        s_uh[:, _u, :, 1 + h * HSC:1 + (h + 1) * HSC],
                        s_w.ap()[:, _et, h * HSC:(h + 1) * HSC].unsqueeze(1).broadcast_to([P, N, HSC]),
                        s_Brep[:, h, :, :], ALU.mult)
                return last
            u_done = op("gpsimd", fp, sm_pool)

            wt("vector", sm_act, a_done)
            wt("vector", sm_pool, u_done)
            m = op("vector", lambda e, _ap=ap_, _u=u_sl, _h=h_sl:
                   e.tensor_tensor_scan(
                       s_uh.ap()[:, _h, :, :].rearrange("p n t -> p (n t)"),
                       s_a.ap()[:, _ap, :, :].rearrange("p n t -> p (n t)"),
                       s_uh.ap()[:, _u, :, :].rearrange("p n t -> p (n t)"),
                       0.0, ALU.mult, ALU.add), sm_dve)
            wt("vector", sm_dve, m)
            m = op("vector", lambda e, _et=et, _h=h_sl:
                   e.tensor_copy(s_hc.ap()[:, _et, :].unsqueeze(2),
                                 s_uh[:, _h, :, SEG - 1:SEG]), sm_dve)
            scan_marks[et] = m

            if KDBG and s == 0 and et == 0:
                ddump("dbg_a", s_a.ap()[:, ap_, :, :].rearrange("p n t -> p (n t)"), sm_act, a_done)
                ddump("dbg_u", s_uh.ap()[:, u_sl, :, :].rearrange("p n t -> p (n t)"), sm_pool, u_done)
                ddump("dbg_h", s_uh.ap()[:, h_sl, :, :].rearrange("p n t -> p (n t)"), sm_dve, m)
            wt("gpsimd", sm_dve, m)

            def fp2(e, _h=h_sl):
                last = None
                for h in range(2):
                    last = e.tensor_tensor(
                        s_uh[:, _h, :, 1 + h * HSC:1 + (h + 1) * HSC],
                        s_uh[:, _h, :, 1 + h * HSC:1 + (h + 1) * HSC],
                        s_Crep[:, h, :, :], ALU.mult)
                return last
            m_done = op("gpsimd", fp2, sm_pool)

            wt("vector", sm_pool, m_done)
            base = sm_dve.c

            def fv1(e, _h=h_sl):
                hb = s_uh[:, _h]
                return e.tensor_add(hb[:, 0:8, :], hb[:, 0:8, :], hb[:, 8:16, :])
            op("vector", fv1, sm_dve)
            wt("vector", sm_dve, base + 1)
            op("vector", lambda e, _h=h_sl: e.tensor_add(
                s_uh[:, _h][:, 0:4, :], s_uh[:, _h][:, 0:4, :], s_uh[:, _h][:, 4:8, :]), sm_dve)
            wt("vector", sm_dve, base + 2)
            op("vector", lambda e, _h=h_sl: e.tensor_add(
                s_uh[:, _h][:, 0:2, :], s_uh[:, _h][:, 0:2, :], s_uh[:, _h][:, 2:4, :]), sm_dve)
            wt("vector", sm_dve, base + 3)
            op("vector", lambda e, _h=h_sl: e.tensor_tensor(
                s_y[:, :], s_uh[:, _h][:, 0, 1:SEG], s_uh[:, _h][:, 1, 1:SEG], ALU.add), sm_dve)
            wt("vector", sm_dve, base + 4)
            op("vector", lambda e, _et=et, _par=par: e.scalar_tensor_tensor(
                s_g1[:], s_x[:, _par, _et, :], s_vecs[:, 8 + _et:9 + _et],
                s_y[:], ALU.mult, ALU.add), sm_dve)
            wt("vector", sm_dve, base + 5)
            m = op("vector", lambda e, _et=et, _par=par: e.tensor_tensor(
                s_y2[:, _par, _et, :], s_g1[:], s_sg[:, _par, _et, :], ALU.mult), sm_dve)
            tree_marks[et] = m
        barrier()

    def phaseO(s):
        par = 0
        ex0 = sm_ex.c

        def f(e, _s=s, _par=par, _b=ex0):
            src = s_y2.ap()[:, _par, :, :].rearrange("p a t -> p (a t)")
            me_sw(e,
                  lambda ee: ee.dma_start(sh_y2.ap()[0, _s], src).then_inc(sm_ex.h, 16),
                  lambda ee: ee.dma_start(sh_y2.ap()[1, _s], src).then_inc(sm_ex.h, 16))
            e.wait_ge(sm_ex.h, _b + 16)
            flag_write(e, FL_Y2 + _s)
            flag_poll(e, FL_Y2 + _s)
            dst = s_y2p.ap().rearrange("p a t -> p (a t)")
            me_sw(e,
                  lambda ee: ee.dma_start(dst, sh_y2.ap()[1, _s]).then_inc(sm_ex.h, 16),
                  lambda ee: ee.dma_start(dst, sh_y2.ap()[0, _s]).then_inc(sm_ex.h, 16))
        emit("sync", f)
        sm_ex.c += 48
        y2x = sm_ex.c
        if KDBG and s == 0:
            ddump("dbg_y2p", s_y2p.ap().rearrange("p a t -> p (a t)"), sm_ex, y2x)
            dwo = dbg["dbg_wo"].ap().bitcast(BF16)
            for ec in range(16):
                dma("sync", dwo[:, ec * P:(ec + 1) * P], w_out_b.ap()[0, ec], sm_st)

        wt("tensor", sm_ex, y2x)
        evac_marks = {}
        for dt in range(8):
            if dt >= 1:
                wt("sync", sm_dve, evac_marks[dt - 1])
            m = dma("sync", s_rot[:, 0, :, :],
                    w_out_b.ap()[dt].rearrange("ec p m -> p ec m"), sm_ld)
            bk = rot_next()
            wt("tensor", sm_ld, m)

            def ft(e, _dt=dt, _bk=bk, _par=par):
                last = None
                for ec in range(16):
                    rhs = s_y2[:, _par, ec, :] if ec < 8 else s_y2p[:, ec - 8, :]
                    last = e.matmul(p_rot[:, _bk, 0:SC], s_rot[:, 0, ec, :], rhs,
                                    start=(ec == 0), stop=(ec == 15))
                return last
            m = op("tensor", ft, sm_pe)
            wt("vector", sm_pe, m)
            em = op("vector", lambda e, _dt=dt, _bk=bk, _sl=0:
                    e.scalar_tensor_tensor(s_x1p[:, _dt, :], s_resid[:, _sl, _dt, :], 2.0,
                                           p_rot[:, _bk, 0:SC], ALU.mult, ALU.add), sm_dve)
            evac_marks[dt] = em
            rot_release(bk, sm_dve, em)

        wt("scalar", sm_dve, evac_marks[7])
        mcp = op("scalar", lambda e: e.activation(
            s_xn.ap().rearrange("p a t -> p (a t)"),
            s_x1p.ap().rearrange("p a t -> p (a t)"), AF.Copy), sm_act)
        wt("tensor", sm_act, mcp)
        if KDBG and s == 0:
            ddump("dbg_x1pre", s_xn.ap().rearrange("p a t -> p (a t)"), sm_act, mcp)
        bk = rot_next()
        bk2 = rot_next()
        st_marks = {}
        for dc in range(8):
            if dc >= 2:
                wt("vector", sm_pe, st_marks[dc - 2])
            mq = op("vector", lambda e, _dc=dc: e.tensor_tensor(
                s_sq[:, _dc % 2, :], s_x1p[:, _dc, :], s_x1p[:, _dc, :], ALU.mult), sm_dve)
            wt("tensor", sm_dve, mq)

            def ft(e, _dc=dc, _b1=bk, _b2=bk2):
                e.matmul(p_rot[:, _b1, 0:SC], s_o128[:, :], s_xn[:, _dc, :],
                         start=(_dc == 0), stop=(_dc == 7))
                return e.matmul(p_rot[:, _b2, 0:SC], s_o128[:, :], s_sq[:, _dc % 2, :],
                                start=(_dc == 0), stop=(_dc == 7))
            st_marks[dc] = op("tensor", ft, sm_pe)
        st_done = sm_pe.c
        wt("vector", sm_pe, st_done)
        base = sm_dve.c
        op("vector", lambda e, _b=bk: e.tensor_scalar_mul(s_rep[:, 0, :], p_rot[:, _b, 0:SC], 1.0 / 1024.0), sm_dve)
        op("vector", lambda e, _b=bk2: e.tensor_scalar_mul(s_lnt[:, :], p_rot[:, _b, 0:SC], 1.0 / 1024.0), sm_dve)
        rot_release(bk, sm_dve, base + 1)
        rot_release(bk2, sm_dve, base + 2)
        wt("vector", sm_dve, base + 2)
        op("vector", lambda e: e.tensor_tensor(s_lnt2[:, :], s_rep[:, 0, :], s_rep[:, 0, :], ALU.mult), sm_dve)
        wt("vector", sm_dve, base + 3)
        m = op("vector", lambda e: e.tensor_sub(s_lnt[:, :], s_lnt[:, :], s_lnt2[:, :]), sm_dve)
        wt("scalar", sm_dve, m)
        m = op("scalar", lambda e: e.activation(s_lnt[:, :], s_lnt[:, :], AF.Ln, bias=1e-5), sm_act)
        wt("scalar", sm_act, m)
        m = op("scalar", lambda e: e.activation(s_lnt[:, :], s_lnt[:, :], AF.Exp, scale=-0.5), sm_act)
        wt("vector", sm_act, m)
        base2 = sm_dve.c
        op("vector", lambda e: e.tensor_tensor(s_rep[:, 1, :], s_rep[:, 0, :], s_lnt[:, :], ALU.mult), sm_dve)
        wt("vector", sm_dve, base2 + 1)
        m = op("vector", lambda e: e.tensor_copy(s_rep[:, 0, :], s_lnt[:, :]), sm_dve)
        wt("vector", sm_dve, base2 + 2)
        base = sm_dve.c
        op("vector", lambda e: e.tensor_tensor(
            s_x1p.ap(), s_x1p.ap(),
            s_rep.ap()[:, 0, :].unsqueeze(1).broadcast_to([P, 8, SC]),
            ALU.mult), sm_dve)
        wt("vector", sm_dve, base + 1)
        m = op("vector", lambda e: e.tensor_tensor(
            s_x1p.ap(), s_x1p.ap(),
            s_rep.ap()[:, 1, :].unsqueeze(1).broadcast_to([P, 8, SC]),
            ALU.subtract), sm_dve)
        wt("scalar", sm_dve, m)
        x1b_done = op("scalar", lambda e: e.activation(
            s_x1b.ap().rearrange("p a t -> p (a t)"),
            s_x1p.ap().rearrange("p a t -> p (a t)"), AF.Copy), sm_act)

        wt("sync", sm_act, x1b_done)

        def fst(e, _s=s):
            d1 = x1b_d.ap()[:, :, _s * HSC:(_s + 1) * HSC].rearrange("a p t -> p a t")
            me_sw(e,
                  lambda ee: ee.dma_start(d1, s_x1b[:, :, 0:HSC]).then_inc(sm_st.h, 16),
                  lambda ee: ee.dma_start(d1, s_x1b[:, :, HSC:SC]).then_inc(sm_st.h, 16))
        emit("sync", fst)
        sm_st.c += 16

        bk = rot_next()
        wt("tensor", sm_act, x1b_done)

        def ft(e, _bk=bk):
            last = None
            for dc in range(8):
                last = e.matmul(p_rot[:, _bk, 0:SC], s_wrt[:, dc, :], s_x1b[:, dc, :],
                                start=(dc == 0), stop=(dc == 7))
            return last
        m = op("tensor", ft, sm_pe)
        wt("scalar", sm_pe, m)
        m = op("scalar", lambda e, _bk=bk: e.activation(
            s_y[0:NEXP, :], p_rot[0:NEXP, _bk, 0:SC], AF.Copy), sm_act)
        rot_release(bk, sm_act, m)
        wt("vector", sm_act, m)
        m = op("vector", lambda e: e.tensor_scalar_add(
            s_y[0:NEXP, :], s_y[0:NEXP, :], s_vecs[0:NEXP, 192:193]), sm_dve)
        bk = rot_next()
        wt("tensor", sm_dve, m)

        def ft2(e, _bk=bk):
            last = None
            for blk in range(2):
                last = e.transpose(p_rot[0:P, _bk, blk * NEXP:(blk + 1) * NEXP],
                                   s_y[0:NEXP, blk * P:(blk + 1) * P], s_id16[0:NEXP, 0:NEXP])
            return last
        m = op("tensor", ft2, sm_pe)
        wt("scalar", sm_pe, m)
        m = op("scalar", lambda e, _bk=bk: e.activation(
            s_lgt.ap().rearrange("p a b -> p (a b)"),
            p_rot[:, _bk, 0:2 * NEXP], AF.Copy), sm_act)
        rot_release(bk, sm_act, m)
        lgt_done = m

        for blk in range(2):
            wt("vector", sm_act, lgt_done)
            m = op("vector", lambda e, _b=blk: e.tensor_reduce(
                s_mx[:, _b, 0:1], s_lgt[:, _b, :], AX.X, ALU.max, negate=True), sm_dve)
            wt("scalar", sm_dve, m)
            m = op("scalar", lambda e, _b=blk: e.activation(
                s_sm[:, _b, :], s_lgt[:, _b, :], AF.Exp, bias=s_mx[:, _b, 0:1]), sm_act)
            wt("vector", sm_act, m)
            m = op("vector", lambda e, _b=blk: e.tensor_reduce(
                s_mx[:, _b, 1:2], s_sm[:, _b, :], AX.X, ALU.add), sm_dve)
            wt("scalar", sm_dve, m)
            m = op("scalar", lambda e, _b=blk: e.activation(
                s_mx[:, _b, 2:3], s_mx[:, _b, 1:2], AF.Ln), sm_act)
            wt("scalar", sm_act, m)
            m = op("scalar", lambda e, _b=blk: e.activation(
                s_mx[:, _b, 2:3], s_mx[:, _b, 2:3], AF.Exp, scale=-1.0), sm_act)
            wt("vector", sm_act, m)
            base = sm_dve.c
            op("vector", lambda e, _b=blk: e.tensor_scalar_mul(
                s_sm[:, _b, :], s_sm[:, _b, :], s_mx[:, _b, 2:3]), sm_dve)
            wt("vector", sm_dve, base + 1)
            op("vector", lambda e, _b=blk: e.tensor_copy(s_wk[:, _b, :], s_sm[:, _b, :]), sm_dve)
            wt("vector", sm_dve, base + 2)
            for it in range(3):
                op("vector", lambda e, _b=blk: e.tensor_reduce(
                    s_mx[:, _b, 3:4], s_wk[:, _b, :], AX.X, ALU.max), sm_dve)
                wt("vector", sm_dve, sm_dve.c)
                if it < 2:
                    op("vector", lambda e, _b=blk: e.tensor_single_scalar(
                        s_msk[:, _b, :], s_wk[:, _b, :], s_mx[:, _b, 3:4], ALU.is_ge), sm_dve)
                    wt("vector", sm_dve, sm_dve.c)
                    op("vector", lambda e, _b=blk: e.copy_predicated(
                        s_wk[:, _b, :], s_msk[:, _b, :], s_negs[:, :]), sm_dve)
                    wt("vector", sm_dve, sm_dve.c)
            op("vector", lambda e, _b=blk: e.tensor_single_scalar(
                s_msk[:, _b, :], s_sm[:, _b, :], s_mx[:, _b, 3:4], ALU.is_ge), sm_dve)
            wt("vector", sm_dve, sm_dve.c)
            op("vector", lambda e, _b=blk: e.memset(s_wk[:, _b, :], 0.0), sm_dve)
            wt("vector", sm_dve, sm_dve.c)
            op("vector", lambda e, _b=blk: e.copy_predicated(
                s_wk[:, _b, :], s_msk[:, _b, :], s_sm[:, _b, :]), sm_dve)
            wt("vector", sm_dve, sm_dve.c)
            op("vector", lambda e, _b=blk: e.tensor_copy(s_cwt[:, _b, :], s_wk[:, _b, :]), sm_dve)
        cw_done = sm_dve.c

        bk = rot_next()
        wt("tensor", sm_dve, cw_done)
        pbf = p_rot.ap().bitcast(BF16)

        def ft3(e, _bk=bk, _v=pbf):
            last = None
            for blk in range(2):
                last = e.transpose(_v[0:NEXP, _bk, blk * P:(blk + 1) * P], s_cwt[:, blk, :], s_idb[:, :])
            return last
        m = op("tensor", ft3, sm_pe)
        wt("scalar", sm_pe, m)

        def fa(e, _bk=bk, _v=pbf):
            last = None
            for blk in range(2):
                last = e.activation(s_cwf[:, blk, :], _v[0:NEXP, _bk, blk * P:(blk + 1) * P], AF.Copy)
            return last
        m = op("scalar", fa, sm_act)
        rot_release(bk, sm_act, m)
        wt("sync", sm_act, m)

        if KDBG and s == 0:
            ddump("dbg_y2", s_y2.ap()[:, 0, :, :].rearrange("p a t -> p (a t)"), sm_dve, sm_dve.c)
            ddump("dbg_x1", s_x1p.ap().rearrange("p a t -> p (a t)"), sm_dve, sm_dve.c)
            ddump("dbg_cw", s_cwf.ap().rearrange("a b c -> a (b c)"), sm_act, sm_act.c)

        def fcw2(e, _s=s):
            d = cw_d.ap()[:, _s * HSC:(_s + 1) * HSC]
            me_sw(e,
                  lambda ee: ee.dma_start(d, s_cwf[:, 0, :]).then_inc(sm_st.h, 16),
                  lambda ee: ee.dma_start(d, s_cwf[:, 1, :]).then_inc(sm_st.h, 16))
        emit("sync", fcw2)
        sm_st.c += 16
        barrier()

    def phaseF(fi):
        x1s_ld = dma("sync", s_x1s.ap(),
                     x1b_d.ap()[:, :, fi * FS:(fi + 1) * FS].rearrange("a p t -> p a t"), sm_ld)
        mz = op("vector", lambda e: e.memset(s_cwo[0:16, :], 1.0), sm_dve)
        wt("sync", sm_dve, mz)
        m = dma("sync", s_cwo[0:NEXP, :], cw_d.ap()[:, fi * FS:(fi + 1) * FS], sm_ld)
        wt("vector", sm_ld, m)
        cwo_done = op("vector", lambda e: e.memset(
            s_racc.ap().rearrange("p a b t -> p (a b t)"), 0.0), sm_dve)

        wt("tensor", sm_dve, cwo_done)
        wt("tensor", sm_ld, m)
        wt("tensor", sm_ld, x1s_ld)
        if KDBG:
            ddump(f"dbg_cwo{fi}", s_cwo.ap()[0:16, :], sm_ld, m)
            ddump(f"dbg_fx1{fi}", s_x1s.ap().rearrange("p a t -> p (a t)"), sm_ld, x1s_ld)

        def ft(e):
            last = None
            for dt in range(6):
                for h in range(2):
                    last = e.matmul(p_dn[:, dt, h, :], s_b2[:, dt * P:(dt + 1) * P],
                                    s_cwo[:, h * HF:(h + 1) * HF], start=(h == 0), stop=False)
            return last
        op("tensor", ft, sm_pe)
        for dt in (6, 7):
            bk = rot_next()
            m = op("tensor", lambda e, _dt=dt, _bk=bk:
                   [e.matmul(p_rot[:, _bk, h * HF:(h + 1) * HF], s_b2[:, _dt * P:(_dt + 1) * P],
                             s_cwo[:, h * HF:(h + 1) * HF], start=(h == 0), stop=(h == 1)) for h in range(2)][-1], sm_pe)
            wt("scalar", sm_pe, m)
            m = op("scalar", lambda e, _dt=dt, _bk=bk:
                   [e.activation(s_racc[:, _dt - 6, h, :], p_rot[:, _bk, h * HF:(h + 1) * HF], AF.Copy)
                    for h in range(2)][-1], sm_act)
            rot_release(bk, sm_act, m)

        w1_marks = {}
        w2_marks = {}
        for ex in range(NE):
            cwr_ld = None
            if ex < NEXP:
                cwr_ld = dma("sync", s_cwr[:, :],
                             bass.AP(cw_d, ex * TOK + fi * FS, [[0, P], [1, 1], [1, FS]]), sm_ld)
            for ht in range(8):
                gidx = ex * 8 + ht
                if gidx >= 2:
                    wt("sync", sm_pe, w1_marks[gidx - 2])
                m = dma("sync", s_rw1[:, gidx % 2, :, :],
                        w_f1_b.ap()[ex, ht].rearrange("dc p m -> p dc m"), sm_ld)
                bk = rot_next()
                wt("tensor", sm_ld, m)

                def ft(e, _ht=ht, _bk=bk, _sl=gidx % 2):
                    last = None
                    for dc in range(8):
                        last = e.matmul(p_rot[:, _bk, :], s_rw1[:, _sl, dc, :], s_x1s[:, dc, :],
                                        start=(dc == 0), stop=(dc == 7))
                    return last
                mmk = op("tensor", ft, sm_pe)
                w1_marks[gidx] = mmk
                wt("scalar", sm_pe, mmk)
                m = op("scalar", lambda e, _e=ex, _ht=ht, _bk=bk:
                       [e.activation(s_hs[:, h, _ht, :], p_rot[:, _bk, h * HF:(h + 1) * HF],
                                     AF.Relu, bias=s_vecs[:, 193 + _ht * 16 + _e:194 + _ht * 16 + _e])
                        for h in range(2)][-1], sm_act)
                rot_release(bk, sm_act, m)
                if ex < NEXP:
                    wt("gpsimd", sm_act, m)
                    if ht == 0:
                        wt("gpsimd", sm_ld, cwr_ld)
                    op("gpsimd", lambda e, _ht=ht:
                       [e.tensor_tensor(s_hs[:, h, _ht, :], s_hs[:, h, _ht, :],
                                        s_cwr.ap().rearrange("p (h t) -> p h t", h=2)[:, h, :], ALU.mult)
                        for h in range(2)][-1], sm_pool)
            hs_pool = sm_pool.c
            hs_act = sm_act.c
            if KDBG and fi == 0 and ex == 0:
                ddump("dbg_hs", s_hs.ap().rearrange("p a b t -> p (a b t)"), sm_pool, hs_pool)
            if ex < NEXP:
                wt("tensor", sm_pool, hs_pool)
            else:
                wt("tensor", sm_act, hs_act)
            for dt in range(8):
                gdx = ex * 8 + dt
                if gdx >= 2:
                    wt("sync", sm_pe, w2_marks[gdx - 2])
                m = dma("sync", s_rw2[:, gdx % 2, :, :],
                        w_f2_b.ap()[ex, dt].rearrange("ht p m -> p ht m"), sm_ld)
                wt("tensor", sm_ld, m)
                if dt < 6:
                    def ft(e, _e=ex, _dt=dt, _sl=gdx % 2):
                        last = None
                        for ht in range(8):
                            for h in range(2):
                                last = e.matmul(p_dn[:, _dt, h, :], s_rw2[:, _sl, ht, :],
                                                s_hs[:, h, ht, :], start=False,
                                                stop=(_e == NE - 1 and ht == 7))
                        return last
                    w2_marks[gdx] = op("tensor", ft, sm_pe)
                else:
                    bk = rot_next()

                    def ft(e, _dt=dt, _bk=bk, _sl=gdx % 2):
                        last = None
                        for ht in range(8):
                            for h in range(2):
                                last = e.matmul(p_rot[:, _bk, h * HF:(h + 1) * HF],
                                                s_rw2[:, _sl, ht, :], s_hs[:, h, ht, :],
                                                start=(ht == 0 and h == 0),
                                                stop=(ht == 7 and h == 1))
                        return last
                    m = op("tensor", ft, sm_pe)
                    w2_marks[gdx] = m
                    wt("vector", sm_pe, m)
                    base = sm_dve.c
                    for h in range(2):
                        op("vector", lambda e, _dt=dt, _bk=bk, _h=h:
                           e.scalar_tensor_tensor(s_racc[:, _dt - 6, _h, :],
                                                  p_rot[:, _bk, _h * HF:(_h + 1) * HF], 1.0,
                                                  s_racc[:, _dt - 6, _h, :], ALU.mult, ALU.add), sm_dve)
                        wt("vector", sm_dve, base + 1 + h)
                    rot_release(bk, sm_dve, base + 2)
        dn_done = sm_pe.c
        if KDBG:
            ddump(f"dbg_racc{fi}", s_racc.ap().rearrange("p a b t -> p (a b t)"), sm_dve, sm_dve.c)
            for dt in range(6):
                wt("scalar", sm_pe, dn_done)
                if dt >= 1:
                    wt("scalar", sm_st, sm_st.c)
                mdx = op("scalar", lambda e, _dt=dt: e.activation(
                    s_hs.ap()[:, 0, 0:2, :].rearrange("p a t -> p (a t)"),
                    p_dn.ap()[:, _dt, :, :].rearrange("p b t -> p (b t)"),
                    AF.Copy), sm_act)
                wt("sync", sm_act, mdx)
                dma("sync", dbg[f"dbg_pdn{fi}"].ap().bitcast(BF16)[:, dt * 2 * HF:(dt + 1) * 2 * HF],
                    s_hs.ap()[:, 0, 0:2, :].rearrange("p a t -> p (a t)"), sm_st)
            wt("scalar", sm_st, sm_st.c)

        for h in range(2):
            x2p_marks = []
            for dt in range(8):
                if dt == 0:
                    wt("vector", sm_pe, dn_done)
                if dt < 6:
                    m = op("vector", lambda e, _dt=dt, _h=h:
                           e.scalar_tensor_tensor(s_x2p[:, _dt, :], p_dn[:, _dt, _h, :], 1.0,
                                                  s_x1s[:, _dt, _h * HF:(_h + 1) * HF],
                                                  ALU.mult, ALU.add), sm_dve)
                else:
                    m = op("vector", lambda e, _dt=dt, _h=h:
                           e.tensor_add(s_x2p[:, _dt, :], s_racc[:, _dt - 6, _h, :],
                                        s_x1s[:, _dt, _h * HF:(_h + 1) * HF]), sm_dve)
                x2p_marks.append(m)
            if KDBG:
                ddump(f"dbg_x2a{fi}{h}", s_x2p.ap().rearrange("p a t -> p (a t)"), sm_dve, sm_dve.c)
            wt("scalar", sm_dve, x2p_marks[7])
            mcp = op("scalar", lambda e: e.activation(
                s_x1s.ap()[:, :, 0:HF].rearrange("p a t -> p a t"),
                s_x2p.ap(), AF.Copy), sm_act)
            wt("tensor", sm_act, mcp)
            bk = rot_next()
            bk2 = rot_next()
            st_marks = {}
            for dc in range(8):
                if dc >= 2:
                    wt("vector", sm_pe, st_marks[dc - 2])
                mq = op("vector", lambda e, _dc=dc: e.tensor_tensor(
                    s_sq[:, _dc % 2, 0:HF], s_x2p[:, _dc, :], s_x2p[:, _dc, :], ALU.mult), sm_dve)
                wt("tensor", sm_dve, mq)

                def ft(e, _dc=dc, _b1=bk, _b2=bk2):
                    e.matmul(p_rot[:, _b1, 0:HF], s_o128[:, :], s_x1s[:, _dc, 0:HF],
                             start=(_dc == 0), stop=(_dc == 7))
                    return e.matmul(p_rot[:, _b2, 0:HF], s_o128[:, :], s_sq[:, _dc % 2, 0:HF],
                                    start=(_dc == 0), stop=(_dc == 7))
                st_marks[dc] = op("tensor", ft, sm_pe)
            st2 = sm_pe.c
            wt("vector", sm_pe, st2)
            base = sm_dve.c
            op("vector", lambda e, _b=bk: e.tensor_scalar_mul(s_r2[:, 0, :], p_rot[:, _b, 0:HF], 1.0 / 1024.0), sm_dve)
            op("vector", lambda e, _b=bk2: e.tensor_scalar_mul(s_lnt[:, 0:HF], p_rot[:, _b, 0:HF], 1.0 / 1024.0), sm_dve)
            rot_release(bk, sm_dve, base + 1)
            rot_release(bk2, sm_dve, base + 2)
            wt("vector", sm_dve, base + 2)
            op("vector", lambda e: e.tensor_tensor(s_lnt2[:, 0:HF], s_r2[:, 0, :], s_r2[:, 0, :], ALU.mult), sm_dve)
            wt("vector", sm_dve, base + 3)
            m = op("vector", lambda e: e.tensor_sub(s_lnt[:, 0:HF], s_lnt[:, 0:HF], s_lnt2[:, 0:HF]), sm_dve)
            wt("scalar", sm_dve, m)
            m = op("scalar", lambda e: e.activation(s_lnt[:, 0:HF], s_lnt[:, 0:HF], AF.Ln, bias=1e-5), sm_act)
            wt("scalar", sm_act, m)
            m = op("scalar", lambda e: e.activation(s_lnt[:, 0:HF], s_lnt[:, 0:HF], AF.Exp, scale=-0.5), sm_act)
            wt("vector", sm_act, m)
            base2 = sm_dve.c
            op("vector", lambda e: e.tensor_tensor(s_r2[:, 1, :], s_r2[:, 0, :], s_lnt[:, 0:HF], ALU.mult), sm_dve)
            wt("vector", sm_dve, base2 + 1)
            m = op("vector", lambda e: e.tensor_copy(s_r2[:, 0, :], s_lnt[:, 0:HF]), sm_dve)
            wt("vector", sm_dve, base2 + 2)
            base = sm_dve.c
            op("vector", lambda e: e.tensor_tensor(
                s_x2p.ap(), s_x2p.ap(),
                s_r2.ap()[:, 0, :].unsqueeze(1).broadcast_to([P, 8, HF]),
                ALU.mult), sm_dve)
            wt("vector", sm_dve, base + 1)
            m = op("vector", lambda e: e.tensor_tensor(
                s_x2p.ap(), s_x2p.ap(),
                s_r2.ap()[:, 1, :].unsqueeze(1).broadcast_to([P, 8, HF]),
                ALU.subtract), sm_dve)
            wt("sync", sm_dve, m)
            if KDBG:
                ddump(f"dbg_x2n{fi}{h}", s_x2p.ap().rearrange("p a t -> p (a t)"), sm_dve, m)
            col = fi * FS + h * HF
            dma("sync", t_out.ap().rearrange("(a p) t -> p a t", p=P)[:, :, col:col + HF],
                s_x2p[:, :, :], sm_st)
            barrier()

    pass
    _lim = os.environ.get("KLIM", "")
    if _lim == "A0":
        prologue()
        phaseA(0)
    elif _lim == "A0NP":
        phaseA(0)
    elif _lim.startswith("N"):
        n = int(_lim[1:])
        prologue()
        for s in range(n):
            phaseA(s)
            phaseS(s)
            phaseO(s)
    elif _lim.startswith("A:"):
        prologue()
        phaseA(0, acut=int(_lim[2:]))
    elif _lim in ("A", "AS", "ASO"):
        prologue()
        phaseA(0)
        if len(_lim) >= 2:
            phaseS(0)
        if len(_lim) >= 3:
            phaseO(0)
    else:
        prologue()
        for s in range(NSC):
            phaseA(s)
            phaseS(s)
            phaseO(s)
        for fi in range(NFS):
            phaseF(fi)

    final_st = sm_st.c
    for e in ("sync", "gpsimd"):
        emit(e, lambda eng, _v=final_st: eng.wait_ge(sm_st.h, _v))

    with ctx:
        with nc.Block() as block:
            @block.sync
            def _(eng):
                for fn in cmds["sync"]:
                    fn(eng)

            @block.tensor
            def _(eng):
                for fn in cmds["tensor"]:
                    fn(eng)

            @block.vector
            def _(eng):
                for fn in cmds["vector"]:
                    fn(eng)

            @block.scalar
            def _(eng):
                for fn in cmds["scalar"]:
                    fn(eng)

            @block.gpsimd
            def _(eng):
                for fn in cmds["gpsimd"]:
                    fn(eng)

    return nc


_NC_CACHE = {}


def kernel(**inputs):
    if "nc" not in _NC_CACHE:
        _NC_CACHE["nc"] = build_nc()
    nc = _NC_CACHE["nc"]
    in_maps = [prep_core_inputs(k, inputs) for k in range(NCORES)]
    res = run_bass_kernel_spmd(nc, in_maps, core_ids=list(range(NCORES)), trace=bool(os.environ.get("KTRACE")))
    _NC_CACHE["res"] = res
    out = np.zeros((B, L, D), np.float32)
    for k in range(NCORES):
        b, m = k // 2, k % 2
        o = np.asarray(res.results[k]["o_x2"])
        for s in range(NSC):
            t0 = s * SC + m * HSC
            out[b, t0:t0 + HSC, :] = o[:, s * HSC:(s + 1) * HSC].T
    return out
